# version 10
# baseline (speedup 1.0000x reference)
import math
import numpy as np
import concourse.bass as bass
import concourse.mybir as mybir
from concourse.bass import IndirectOffsetOnAxis
from concourse.bass_utils import run_bass_kernel_spmd

F32 = mybir.dt.float32
BF16 = mybir.dt.bfloat16
I32 = mybir.dt.int32
AF = mybir.ActivationFunctionType
ALU = mybir.AluOpType
AX = mybir.AxisListType

BIG = 1 << 20


class CFG:
    def __init__(self, ntok, d, dff, e, seq, cap):
        self.NT, self.D, self.DFF, self.E, self.SEQ, self.CAP = \
            ntok, d, dff, e, seq, cap
        self.NSEQ = ntok // seq
        self.DC = d // 128
        self.FC = dff // 128
        self.RT = ntok // 128
        self.OC = 3 * d // 128
        self.GC = d // 128
        assert ntok % 128 == 0 and d % 128 == 0 and dff % 128 == 0
        assert seq % 128 == 0 and ntok % seq == 0 and cap % 128 == 0


REAL = CFG(ntok=32768, d=1024, dff=4096, e=8, seq=2048, cap=5120)


def chunks_of(total, size):
    out, o = [], 0
    while o < total:
        out.append((o, min(size, total - o)))
        o += size
    return out


class Ctx:
    ENGINES = ("sync", "tensor", "vector", "scalar", "gpsimd")

    def __init__(self, nc):
        self.nc = nc
        self.streams = {e: [] for e in self.ENGINES}
        self.cnt = {}
        self.sems = {}
        for name in ("ld", "gd", "pe", "dv", "ac"):
            self.sems[name] = nc.alloc_semaphore(name)
            self.cnt[name] = 0

    AUTO = {"vector": "dv", "scalar": "ac"}
    SELFWAIT = {"vector"}

    def op(self, engine, build, waits=(), inc=None, inc_k=1):
        waits = list(waits)
        auto = self.AUTO.get(engine)
        if auto is not None:
            if engine in self.SELFWAIT:
                waits.append((auto, self.cnt[auto]))
            if inc is None:
                inc, inc_k = auto, 1
            assert inc == auto
        if inc is not None:
            self.cnt[inc] += inc_k
        sems = self.sems

        def fn(eng, build=build, waits=tuple(waits), inc=inc, inc_k=inc_k):
            best = {}
            for s, v in waits:
                if v > 0:
                    best[s] = max(best.get(s, 0), v)
            for s, v in best.items():
                eng.wait_ge(sems[s], v)
            ins = build(eng)
            if inc is not None and ins is not None:
                ins.then_inc(sems[inc], inc_k)
        self.streams[engine].append(fn)
        return self.cnt.get(inc) if inc is not None else None

    def wait(self, engine, name, val):
        sems = self.sems
        if val > 0:
            self.streams[engine].append(
                lambda eng, s=name, v=val: eng.wait_ge(sems[s], v))

    def run(self):
        with self.nc.Block() as block:
            @block.sync
            def _(eng):
                for fn in self.streams["sync"]:
                    fn(eng)

            @block.tensor
            def _(eng):
                for fn in self.streams["tensor"]:
                    fn(eng)

            @block.vector
            def _(eng):
                for fn in self.streams["vector"]:
                    fn(eng)

            @block.scalar
            def _(eng):
                for fn in self.streams["scalar"]:
                    fn(eng)

            @block.gpsimd
            def _(eng):
                for fn in self.streams["gpsimd"]:
                    fn(eng)


def build_program(nc, C, phases=("r1", "e1", "m", "r2", "e2", "c2")):
    NT, D, DFF, E, CAP, SEQ = C.NT, C.D, C.DFF, C.E, C.CAP, C.SEQ
    DC, FC, RT = C.DC, C.FC, C.RT

    xin = nc.dram_tensor("xin", [NT, D], F32, kind="ExternalInput").ap()
    wqkva = nc.dram_tensor("wqkva", [3 * D, D], F32, kind="ExternalInput").ap()
    gammab = nc.dram_tensor("gammab", [128, D], F32, kind="ExternalInput").ap()
    wr_in = nc.dram_tensor("wr_in", [D, E], F32, kind="ExternalInput").ap()
    w1_in = nc.dram_tensor("w1_in", [E, D, DFF], F32, kind="ExternalInput").ap()
    w2_in = nc.dram_tensor("w2_in", [E, DFF, D], F32, kind="ExternalInput").ap()
    yout = nc.dram_tensor("yout", [NT, D], F32, kind="ExternalOutput").ap()

    xcur = nc.dram_tensor("xcur", [NT, D], F32, kind="Internal").ap()
    t_d = nc.dram_tensor("t_d", [NT, D], BF16, kind="Internal").ap()
    disp = nc.dram_tensor("disp", [E * CAP + 1, D], BF16, kind="Internal").ap()
    eout_d = nc.dram_tensor("eout_d", [E * CAP, D], BF16, kind="Internal").ap()
    bases_d = nc.dram_tensor("bases_d", [1, RT * E], F32, kind="Internal").ap()
    wT_d = nc.dram_tensor("wT_d", [128, 3 * D * D // 128], BF16, kind="Internal").ap()

    ident_np = np.eye(128, dtype=np.float32)
    identb_d = nc.inline_tensor(ident_np, "identb_d").ap()
    identf_d = nc.inline_tensor(ident_np, "identf_d").ap()
    U_d = nc.inline_tensor(np.triu(np.ones((128, 128), np.float32)), "U_d").ap()
    iota8_d = nc.inline_tensor(
        np.tile(np.arange(E, dtype=np.float32), (128, 1)), "iota8_d").ap()
    ones_d = nc.inline_tensor(np.ones((128, 1), np.float32), "ones_d").ap()

    ctx = Ctx(nc)
    from contextlib import ExitStack
    es = ExitStack()

    def sb(name, shape, dtype):
        return es.enter_context(nc.sbuf_tensor(name, shape, dtype))

    def ps(name, shape, dtype):
        return es.enter_context(nc.psum_tensor(name, shape, dtype))

    identb = sb("identb", [128, 128], BF16)
    identf = sb("identf", [128, 128], F32)
    Ub = sb("Ub", [128, 128], BF16)
    iota8 = sb("iota8", [128, E], F32)
    ones8f = sb("ones8f", [128, E], F32)
    onesRT = sb("onesRT", [1, RT], F32)
    onesb = sb("onesb", [128, 1], BF16)
    gam = sb("gam", [128, D], F32)
    wrh = sb("wrh", [128, DC * E], BF16)
    wrl = sb("wrl", [128, DC * E], BF16)
    wrf = sb("wrf", [128, DC * E], F32)

    idxS = sb("idxS", [128, RT], F32)
    posS = sb("posS", [128, RT], F32)
    keptS = sb("keptS", [128, RT], F32)
    pmaxS = sb("pmaxS", [128, RT], F32)
    sfacS = pmaxS
    flatI = sb("flatI", [128, RT], F32)
    flatIi = sb("flatIi", [128, RT], I32)
    scatI = sb("scatI", [128, 2], I32)
    meanS = sb("meanS", [128, RT], F32)
    rstdS = sb("rstdS", [128, RT], F32)
    nmr = sb("nmr", [128, RT], F32)
    statsA = meanS
    statsB = rstdS
    wrhl = sb("wrhl", [128, DC * 2 * E], BF16)

    SEQT = SEQ // 128
    A1sz = max(DC * DFF, C.OC * DC * 128 + SEQ, 3 * E * RT * 2 + 2 * RT)
    big1 = sb("big1", [128, A1sz], BF16)
    A2sz = max(FC * D, DC * SEQ + 4 * 2 * SEQ)
    big2 = sb("big2", [128, A2sz], BF16)
    bb_off = ((3 * D + 4 * SEQT + 1) // 2) * 2
    A3sz = max(FC * 256, 2 * SEQ, bb_off + 2 * E * RT)
    bigC = sb("bigC", [128, A3sz], BF16)

    w1s = big1[:, 0:DC * DFF]
    w2s = big2[:, 0:FC * D]
    h_sb = bigC[:, 0:FC * 256]
    wT = big1[:, 0:C.OC * DC * 128]
    qb = big1[:, C.OC * DC * 128:C.OC * DC * 128 + SEQ]
    xn_fm = big2[:, 0:DC * SEQ]
    kvf = big2[:, DC * SEQ:DC * SEQ + 2 * SEQ].bitcast(F32)
    af = big2[:, DC * SEQ + 2 * SEQ:DC * SEQ + 4 * SEQ].bitcast(F32)
    yf = big2[:, DC * SEQ + 4 * SEQ:DC * SEQ + 6 * SEQ].bitcast(F32)
    of = big2[:, DC * SEQ + 6 * SEQ:DC * SEQ + 8 * SEQ].bitcast(F32)
    stg = bigC[:, 0:2 * SEQ].bitcast(F32)
    fmh = bigC[:, 0:D]
    fml = bigC[:, D:2 * D]
    tlo = bigC[:, 2 * D:3 * D]
    sscol = sb("sscol", [128, SEQT], F32)[:]
    smcol = sb("smcol", [128, SEQT], F32)[:]
    cnts = big1[0:1, 0:2 * E * RT].bitcast(F32)
    basesEM = big1[0:1, 2 * E * RT:4 * E * RT].bitcast(F32)
    basesTM = big1[0:1, 4 * E * RT:6 * E * RT].bitcast(F32)
    onesRT = big1[0:1, 6 * E * RT:6 * E * RT + 2 * RT].bitcast(F32)
    basesB = bigC[:, bb_off:bb_off + 2 * E * RT].bitcast(F32)

    xt = sb("xt", [128, D], F32)
    xt2 = sb("xt2", [128, D], F32)
    tf = sb("tf", [128, D], F32)
    thi = sb("thi", [128, D], BF16)
    rsc = sb("rsc", [128, 16 * E], F32)
    ohbf = sb("ohbf", [128, 2 * E], BF16)
    zz = sb("zz", [128, D], BF16)
    mark = sb("mark", [1, 1], F32)
    eo_g = sb("eo_g", [128, D], BF16)
    tt_g = sb("tt_g", [128, D], BF16)
    mucol = sb("mucol", [128, SEQT], F32)
    scal = sb("scal", [128, SEQT], F32)
    wtmp = sb("wtmp", [128, D], F32)
    wtmpb = sb("wtmpb", [128, D], BF16)
    disp_in_b = (xt[:, 0:D].bitcast(BF16), xt2[:, 0:D].bitcast(BF16))
    disp_fm_b = (tf[:, 0:D].bitcast(BF16), wtmp[:, 0:D].bitcast(BF16))
    eo_bf8 = sb("eo_bf8", [128, DC * 256], BF16)
    eo_stage = sb("eo_stage2", [128, 2 * D], BF16)
    sq_scr = tf
    dcomb = tf


    p_q0 = ps("p_q0", [128, 512], F32)
    p_q1 = ps("p_q1", [128, 512], F32)
    p_h0t = ps("p_h0t", [128, 256], F32)
    p_h1t = ps("p_h1t", [128, 256], F32)
    p_e0t = ps("p_e0t", [128, 256], F32)
    p_e1t = ps("p_e1t", [128, 256], F32)
    p_sm = ps("p_sm", [128, 512], F32)
    p_lgt = ps("p_lgt", [128, 512], F32)
    p_h0 = p_h0t[:, 0:256]
    p_h1 = p_h1t[:, 0:256]
    p_e0 = p_e0t[:, 0:256]
    p_e1 = p_e1t[:, 0:256]
    p_smb = p_sm[:, 0:512].bitcast(BF16)
    p_lgb = p_lgt[:, 0:512].bitcast(BF16)
    p_lgA = [p_lgt[:, 0:2 * E], p_lgt[:, 2 * E:4 * E]]
    p_lgB = [p_lgt[:, 4 * E:5 * E], p_lgt[:, 5 * E:6 * E]]
    p_cm = [p_lgt[:, 48:48 + E], p_lgt[:, 56:56 + E]]
    p_cs = [p_lgt[0:1, 64:64 + E], p_lgt[0:1, 72:72 + E]]

    op, wait, cnt = ctx.op, ctx.wait, ctx.cnt



    op("sync", lambda e: e.dma_start(out=identf[:], in_=identf_d), inc="ld", inc_k=16)
    op("sync", lambda e: e.dma_start(out=iota8[:], in_=iota8_d), inc="ld", inc_k=16)
    op("sync", lambda e: e.dma_start(out=gam[:], in_=gammab), inc="ld", inc_k=16)
    op("sync", lambda e: e.dma_start(
        out=wrf[:].rearrange("p (c e) -> p c e", e=E),
        in_=wr_in.rearrange("(c p) e -> p c e", p=128)),
        inc="ld", inc_k=16)
    op("gpsimd", lambda e: e.dma_start(out=identb[:], in_=identb_d), inc="gd", inc_k=16)
    op("gpsimd", lambda e: e.dma_start(out=Ub[:], in_=U_d), inc="gd", inc_k=16)
    op("gpsimd", lambda e: e.dma_start(out=onesb[:], in_=ones_d), inc="gd", inc_k=16)
    v_ld0, v_gd0 = cnt["ld"], cnt["gd"]
    op("vector", lambda e: e.tensor_copy(out=wrh[:], in_=wrf[:]),
       waits=[("ld", v_ld0), ("gd", v_gd0)])
    op("vector", lambda e: e.tensor_tensor(
        out=wrl[:], in0=wrf[:], in1=wrh[:], op=ALU.subtract))
    op("vector", lambda e: e.tensor_copy(
        out=wrhl[:].rearrange("p (c t e) -> p c t e", t=2, e=E)[:, :, 0, :],
        in_=wrh[:].rearrange("p (c e) -> p c e", e=E)))
    op("vector", lambda e: e.tensor_copy(
        out=wrhl[:].rearrange("p (c t e) -> p c t e", t=2, e=E)[:, :, 1, :],
        in_=wrl[:].rearrange("p (c e) -> p c e", e=E)))
    op("vector", lambda e: e.memset(zz[:], 0.0))
    op("vector", lambda e: e.memset(ones8f[:], 1.0))
    v_init_dv = op("vector", lambda e: e.memset(mark[:], 0.0), inc="dv")
    for r in range((E * CAP) // 128):
        op("gpsimd", lambda e, r=r: e.dma_start(
            out=disp[r * 128:(r + 1) * 128, :], in_=zz[:]),
            waits=[("dv", v_init_dv)] if r == 0 else [],
            inc="gd", inc_k=16)
    v_zero = cnt["gd"]

    wt_ev = 0
    for oc in range(C.OC):
        v = op("sync", lambda e, oc=oc: e.dma_start(
            out=wtmp[:], in_=wqkva[oc * 128:(oc + 1) * 128, :]),
            waits=[("pe", cnt["pe"])], inc="ld", inc_k=16)
        v_c = op("vector", lambda e: e.tensor_copy(out=wtmpb[:], in_=wtmp[:]),
                 waits=[("ld", v)], inc="dv")
        for dc in range(DC):
            op("tensor", lambda e, dc=dc: e.transpose(
                out=p_smb[:, dc * 128:(dc + 1) * 128],
                in_=wtmpb[:, dc * 128:(dc + 1) * 128], identity=identb[:]),
                waits=[("dv", v_c), ("dv", wt_ev)] if dc == 0 else [],
                inc="pe" if dc == DC - 1 else None)
        vpet = cnt["pe"]
        wt_ev = op("vector", lambda e, oc=oc: e.tensor_copy(
            out=wT[:, oc * DC * 128:(oc + 1) * DC * 128],
            in_=p_smb[:, 0:DC * 128]), waits=[("pe", vpet)], inc="dv")
    v_wT_dv = cnt["dv"]
    v_wT_st = op("sync", lambda e: e.dma_start(out=wT_d, in_=wT),
                 waits=[("dv", v_wT_dv)], inc="ld", inc_k=16)

    def routing_phase(xsrc, prefetch=None):
        xmark = {}
        for k in range(RT):
            x_t = xt if k % 2 == 0 else xt2
            w = [("ac", xmark[k - 2][0])] if k - 2 in xmark else []
            v = op("sync", lambda e, k=k, x_t=x_t: e.dma_start(
                out=x_t[:], in_=xsrc[k * 128:(k + 1) * 128, :]),
                waits=w, inc="ld", inc_k=16)
            va = op("scalar", lambda e, k=k, x_t=x_t: e.activation(
                out=sq_scr[:], in_=x_t[:], func=AF.Square,
                accum_out=statsB[:, k:k + 1]),
                waits=[("ld", v)], inc="ac")
            vd = op("scalar", lambda e, k=k, x_t=x_t: e.activation(
                out=sq_scr[:], in_=x_t[:], func=AF.Copy,
                accum_out=statsA[:, k:k + 1]),
                waits=[("ld", v)], inc="ac")
            xmark[k] = (vd, vd)
        v_stats_ac, v_stats_dv = cnt["ac"], cnt["dv"]
        op("vector", lambda e: e.tensor_scalar(
            out=meanS[:, 0:RT], in0=statsA[:, 0:RT], scalar1=1.0 / D,
            scalar2=None, op0=ALU.mult), waits=[("ac", v_stats_ac)])
        op("vector", lambda e: e.tensor_scalar(
            out=rstdS[:, 0:RT], in0=statsB[:, 0:RT], scalar1=1.0 / D,
            scalar2=None, op0=ALU.mult))
        op("vector", lambda e: e.tensor_tensor(
            out=posS[:, 0:RT], in0=meanS[:, 0:RT], in1=meanS[:, 0:RT],
            op=ALU.mult))
        op("vector", lambda e: e.tensor_tensor(
            out=rstdS[:, 0:RT], in0=rstdS[:, 0:RT], in1=posS[:, 0:RT],
            op=ALU.subtract))
        v1 = op("vector", lambda e: e.tensor_scalar(
            out=rstdS[:, 0:RT], in0=rstdS[:, 0:RT], scalar1=1e-5, scalar2=None,
            op0=ALU.add), inc="dv")
        va = op("scalar", lambda e: e.activation(
            out=rstdS[:, 0:RT], in_=rstdS[:, 0:RT], func=AF.Sqrt),
            waits=[("dv", v1)], inc="ac")
        op("vector", lambda e: e.reciprocal(
            out=rstdS[:, 0:RT], in_=rstdS[:, 0:RT]), waits=[("ac", va)])
        v_rstd = op("vector", lambda e: e.scalar_tensor_tensor(
            out=nmr[:, 0:RT], in0=meanS[:, 0:RT], scalar=-1.0,
            in1=rstdS[:, 0:RT], op0=ALU.mult, op1=ALU.mult), inc="dv")

        marks = {}
        for k in range(RT):
            p = k % 2
            x_t = xt if p == 0 else xt2
            pw = marks.get(k - 2, {})
            pm = marks.get(k - 1, {})
            v = op("sync", lambda e, k=k, x_t=x_t: e.dma_start(
                out=x_t[:], in_=xsrc[k * 128:(k + 1) * 128, :]),
                waits=[("gd", pw["tmp"])] if pw
                else [("ac", v_stats_ac)], inc="ld", inc_k=16)
            m = {}
            m["tmp"] = op("gpsimd", lambda e, k=k, x_t=x_t: e.tensor_scalar(
                out=tf[:], in0=x_t[:], scalar1=rstdS[:, k:k + 1],
                scalar2=nmr[:, k:k + 1], op0=ALU.mult, op1=ALU.add),
                waits=[("ld", v), ("ac", pm.get("thi", 0))] +
                      ([("dv", v_rstd)] if k == 0 else []),
                inc="gd", inc_k=1)
            m["thi"] = v_thi = op("scalar", lambda e: e.activation(
                out=thi[:], in_=tf[:], func=AF.Copy),
                waits=[("gd", m["tmp"]), ("pe", pm.get("vpeh", 0)),
                       ("gd", max(pm.get("tstore", 0), pm.get("tlo", 0)))],
                inc="ac")
            m["tstore"] = op("gpsimd", lambda e, k=k: e.dma_start(
                out=t_d[k * 128:(k + 1) * 128, :], in_=thi[:]),
                waits=[("ac", v_thi)], inc="gd", inc_k=16)
            m["tlo"] = v_tlo = op("gpsimd", lambda e: e.tensor_tensor(
                out=tlo[:], in0=tf[:], in1=thi[:], op=ALU.subtract),
                waits=[("ac", v_thi), ("pe", pm.get("vpel", 0))],
                inc="gd", inc_k=1)
            for dc in range(DC):
                op("tensor", lambda e, dc=dc: e.transpose(
                    out=p_smb[:, dc * 128:(dc + 1) * 128],
                    in_=thi[:, dc * 128:(dc + 1) * 128], identity=identb[:]),
                    waits=[("ac", v_thi),
                           ("dv", pm.get("fml", v_wT_dv if k == 0 else 0))]
                    if dc == 0 else [],
                    inc="pe" if dc == DC - 1 else None)
            m["vpeh"] = vpeh = cnt["pe"]
            v_fmh = op("scalar", lambda e: e.activation(
                out=fmh[:, 0:D], in_=p_smb[:, 0:D], func=AF.Copy),
                waits=[("pe", vpeh), ("pe", pm.get("lgmm", 0))], inc="ac")
            for dc in range(DC):
                op("tensor", lambda e, dc=dc: e.transpose(
                    out=p_smb[:, dc * 128:(dc + 1) * 128],
                    in_=tlo[:, dc * 128:(dc + 1) * 128], identity=identb[:]),
                    waits=[("gd", v_tlo), ("ac", v_fmh)] if dc == 0 else [],
                    inc="pe" if dc == DC - 1 else None)
            m["vpel"] = vpel = cnt["pe"]
            m["fml"] = v_fm = op("vector", lambda e: e.tensor_copy(
                out=fml[:, 0:D], in_=p_smb[:, 0:D]),
                waits=[("pe", vpel), ("pe", pm.get("lgmm", 0))], inc="dv")
            for dc in range(DC):
                op("tensor", lambda e, dc=dc, p=p: e.matmul(
                    out=p_lgA[p], lhsT=fmh[:, dc * 128:(dc + 1) * 128],
                    rhs=wrhl[:, dc * 2 * E:(dc + 1) * 2 * E],
                    start=(dc == 0), stop=(dc == DC - 1)),
                    waits=([("ac", v_fmh), ("dv", pw.get("lgread", 0))]
                           if dc == 0 else []),
                    inc="pe" if dc == DC - 1 else None)
            for dc in range(DC):
                op("tensor", lambda e, dc=dc, p=p: e.matmul(
                    out=p_lgB[p], lhsT=fml[:, dc * 128:(dc + 1) * 128],
                    rhs=wrh[:, dc * E:(dc + 1) * E],
                    start=(dc == 0), stop=(dc == DC - 1)),
                    waits=[("dv", v_fm)] if dc == 0 else [],
                    inc="pe" if dc == DC - 1 else None)
            m["lgmm"] = v_lgmm = cnt["pe"]
            rb = rsc[:, p * 8 * E:(p + 1) * 8 * E]
            lg = rb[:, 0:E]
            mx = rb[:, E:E + 1]
            s2 = rb[:, E + 1:E + 2]
            eq = rb[:, 2 * E:3 * E]
            pr = rb[:, 3 * E:4 * E]
            oh = rb[:, 4 * E:5 * E]
            ex = rb[:, 5 * E:6 * E]
            ohb = ohbf[:, p * E:(p + 1) * E]
            op("vector", lambda e, lg=lg, p=p: e.tensor_copy(
                out=lg, in_=p_lgA[p][:, 0:E]), waits=[("pe", v_lgmm)])
            op("vector", lambda e, lg=lg, p=p: e.tensor_tensor(
                out=lg, in0=lg, in1=p_lgA[p][:, E:2 * E], op=ALU.add))
            m["lgread"] = op("vector", lambda e, lg=lg, p=p: e.tensor_tensor(
                out=lg, in0=lg, in1=p_lgB[p], op=ALU.add), inc="dv")
            op("vector", lambda e, lg=lg, mx=mx: e.reduce_max(
                out=mx, in_=lg, axis=AX.X))
            op("vector", lambda e, lg=lg, mx=mx, eq=eq: e.tensor_scalar(
                out=eq, in0=lg, scalar1=mx, scalar2=None, op0=ALU.is_ge))
            op("vector", lambda e, eq=eq, pr=pr: e.tensor_tensor_scan(
                out=pr, data0=ones8f[:], data1=eq, initial=0.0,
                op0=ALU.mult, op1=ALU.add))
            op("vector", lambda e, pr=pr: e.tensor_scalar(
                out=pr, in0=pr, scalar1=1.0, scalar2=None, op0=ALU.is_le))
            op("vector", lambda e, eq=eq, pr=pr, oh=oh: e.tensor_tensor(
                out=oh, in0=eq, in1=pr, op=ALU.mult))
            op("vector", lambda e, oh=oh, ex=ex: e.tensor_tensor(
                out=ex, in0=oh, in1=iota8[:], op=ALU.mult))
            op("vector", lambda e, k=k, ex=ex: e.reduce_sum(
                out=idxS[:, k:k + 1], in_=ex, axis=AX.X))
            v2 = op("vector", lambda e, lg=lg, mx=mx, ex=ex: e.tensor_scalar(
                out=ex, in0=lg, scalar1=mx, scalar2=None, op0=ALU.subtract),
                inc="dv")
            va = op("scalar", lambda e, ex=ex: e.activation(
                out=ex, in_=ex, func=AF.Exp), waits=[("dv", v2)], inc="ac")
            op("vector", lambda e, ex=ex, s2=s2: e.reduce_sum(
                out=s2, in_=ex, axis=AX.X), waits=[("ac", va)])
            op("vector", lambda e, k=k, s2=s2: e.reciprocal(
                out=pmaxS[:, k:k + 1], in_=s2))
            v3 = op("vector", lambda e, oh=oh, ohb=ohb: e.tensor_copy(
                out=ohb, in_=oh), inc="dv")
            vcm = op("tensor", lambda e, p=p, ohb=ohb: e.matmul(
                out=p_cm[p], lhsT=Ub[:], rhs=ohb, start=True, stop=True),
                waits=[("dv", v3), ("dv", pw.get("cmread", 0)),
                       ("dv", pw.get("csread", 0))], inc="pe")
            vcs = op("tensor", lambda e, p=p, ohb=ohb: e.matmul(
                out=p_cs[p][:], lhsT=onesb[:], rhs=ohb, start=True, stop=True),
                inc="pe")
            m["csread"] = op("vector", lambda e, k=k, p=p: e.tensor_copy(
                out=cnts[0:1, k:k + (E - 1) * RT + 1:RT], in_=p_cs[p]),
                waits=[("pe", vcs)], inc="dv")
            m["cmread"] = op("vector", lambda e, ex=ex, p=p: e.tensor_copy(
                out=ex, in_=p_cm[p]), waits=[("pe", vcm)], inc="dv")
            op("vector", lambda e, ex=ex, oh=oh: e.tensor_tensor(
                out=ex, in0=ex, in1=oh, op=ALU.mult))
            m["tdone"] = op("vector", lambda e, k=k, ex=ex: e.reduce_sum(
                out=posS[:, k:k + 1], in_=ex, axis=AX.X), inc="dv")
            marks[k] = m
        v_passA = cnt["dv"]

        op("vector", lambda e: e.memset(onesRT[0:1, 0:RT], 1.0))
        for ee in range(E):
            op("vector", lambda e, ee=ee: e.tensor_tensor_scan(
                out=basesEM[0:1, ee * RT:(ee + 1) * RT],
                data0=onesRT[0:1, 0:RT],
                data1=cnts[0:1, ee * RT:(ee + 1) * RT],
                initial=0.0, op0=ALU.mult, op1=ALU.add),
                waits=[("dv", v_passA)] if ee == 0 else [])
        op("vector", lambda e: e.tensor_tensor(
            out=basesEM[0:1, :], in0=basesEM[0:1, :], in1=cnts[0:1, :],
            op=ALU.subtract))
        v_b = op("vector", lambda e: e.tensor_copy(
            out=basesTM[0:1, :].rearrange("p (k e) -> p k e", e=E),
            in_=basesEM[0:1, :].rearrange("p (e k) -> p k e", e=E)),
            inc="dv")
        vg1 = op("gpsimd", lambda e: e.dma_start(
            out=bases_d, in_=basesTM[0:1, :]),
            waits=[("dv", v_b)], inc="gd", inc_k=16)
        vg2 = op("gpsimd", lambda e: e.dma_start(
            out=basesB[:], in_=bases_d.to_broadcast([128, RT * E])),
            inc="gd", inc_k=16)
        if prefetch is not None:
            prefetch(vg2)

        oh2 = rsc[:, 0:E]
        bsl = rsc[:, E:2 * E]
        sc1 = rsc[:, 2 * E:2 * E + 1]
        sc2 = rsc[:, 2 * E + 1:2 * E + 2]
        op("vector", lambda e: e.tensor_scalar(
            out=flatI[:, 0:RT], in0=idxS[:, 0:RT], scalar1=float(CAP),
            scalar2=None, op0=ALU.mult),
            waits=[("dv", v_passA), ("gd", vg2)])
        tt_bufs = (tt_g, wtmpb)
        scat_gate = cnt["gd"]
        smark = {}
        for k in range(RT):
            p = k % 2
            op("vector", lambda e, k=k: e.tensor_scalar(
                out=oh2, in0=iota8[:], scalar1=idxS[:, k:k + 1],
                scalar2=None, op0=ALU.is_equal))
            op("vector", lambda e, k=k: e.tensor_tensor(
                out=bsl, in0=oh2, in1=basesB[:, k * E:(k + 1) * E],
                op=ALU.mult))
            op("vector", lambda e: e.reduce_sum(out=sc1, in_=bsl, axis=AX.X))
            op("vector", lambda e, k=k: e.scalar_tensor_tensor(
                out=posS[:, k:k + 1], in0=sc1, scalar=1.0,
                in1=posS[:, k:k + 1], op0=ALU.subtract, op1=ALU.add))
            op("vector", lambda e, k=k: e.tensor_scalar(
                out=keptS[:, k:k + 1], in0=posS[:, k:k + 1],
                scalar1=float(CAP), scalar2=None, op0=ALU.is_lt))
            op("vector", lambda e, k=k: e.scalar_tensor_tensor(
                out=flatI[:, k:k + 1], in0=posS[:, k:k + 1],
                scalar=float(CAP - 1), in1=flatI[:, k:k + 1],
                op0=ALU.min, op1=ALU.add))
            op("vector", lambda e, k=k: e.tensor_copy(
                out=flatIi[:, k:k + 1], in_=flatI[:, k:k + 1]))
            op("vector", lambda e, k=k: e.scalar_tensor_tensor(
                out=sc2, in0=flatI[:, k:k + 1], scalar=float(E * CAP),
                in1=keptS[:, k:k + 1], op0=ALU.subtract, op1=ALU.mult))
            op("vector", lambda e: e.tensor_scalar(
                out=sc1, in0=sc2, scalar1=float(E * CAP), scalar2=None,
                op0=ALU.add))
            v5 = op("vector", lambda e, k=k, p=p: e.tensor_copy(
                out=scatI[:, p:p + 1], in_=sc1),
                waits=[("gd", smark.get(k - 2, scat_gate))], inc="dv")
            vt = op("sync", lambda e, k=k, p=p: e.dma_start(
                out=tt_bufs[p][:], in_=t_d[k * 128:(k + 1) * 128, :]),
                waits=[("gd", smark.get(k - 2, scat_gate))] +
                      ([("dv", v_wT_dv)] if k < 2 else []),
                inc="ld", inc_k=16)
            smark[k] = op("gpsimd", lambda e, k=k, p=p: e.indirect_dma_start(
                out=disp,
                out_offset=IndirectOffsetOnAxis(scatI[:, p:p + 1], 0),
                in_=tt_bufs[p][:], in_offset=None),
                waits=[("dv", v5), ("ld", vt)], inc="gd", inc_k=16)
        op("vector", lambda e: e.tensor_scalar(
            out=sfacS[:, 0:RT], in0=pmaxS[:, 0:RT], scalar1=1.0, scalar2=None,
            op0=ALU.subtract))
        op("vector", lambda e: e.tensor_tensor(
            out=sfacS[:, 0:RT], in0=sfacS[:, 0:RT], in1=keptS[:, 0:RT],
            op=ALU.mult))
        op("vector", lambda e: e.tensor_scalar(
            out=sfacS[:, 0:RT], in0=sfacS[:, 0:RT], scalar1=1.0, scalar2=0.5,
            op0=ALU.add, op1=ALU.mult), inc="dv")

    def emit_w0_prefetch(w1_waits, w2_waits):
        FQ = (DFF // 128) // 4
        w1m, w2m = [], []
        for q in range(4):
            w1m.append(op("gpsimd", lambda e, q=q: e.dma_start(
                out=w1s[:, 0:DC * DFF].rearrange(
                    "p (c f) -> p c f", f=DFF)[:, :, q * FQ * 128:(q + 1) * FQ * 128],
                in_=w1_in[0].rearrange("(c p) f -> p c f", p=128)[
                    :, :, q * FQ * 128:(q + 1) * FQ * 128]),
                waits=w1_waits if q == 0 else [], inc="gd", inc_k=16))
        for q in range(4):
            w2m.append(op("gpsimd", lambda e, q=q: e.dma_start(
                out=w2s[:, q * (FC // 4) * D:(q + 1) * (FC // 4) * D],
                in_=w2_in[0].rearrange("(c p) f -> p c f", p=128)[
                    :, q * (FC // 4):(q + 1) * (FC // 4), :]),
                waits=w2_waits if q == 0 else [], inc="gd", inc_k=16))
        return (w1m, w2m)

    def expert_phase(pre=None):
        v_scat = cnt["gd"]
        tiles = [(ee, r0, rn) for ee in range(E) for (r0, rn) in chunks_of(CAP, 256)]
        import os
        _lim = int(os.environ.get("EXPERT_TILES", "0"))
        if _lim:
            tiles = tiles[:_lim]
        NTL = len(tiles)
        per_e = max(1, len(chunks_of(CAP, 256)))

        w1q = [[None] * 4 for _ in range(E)]
        w2q = [[None] * 4 for _ in range(E)]
        m_hlast = {}
        m_elast = {}
        FQ = FC // 4
        DQ = DC * FQ * 128

        def emit_w_loads(ee, pe_gate_h, pe_gate_e):
            for q in range(4):
                w1q[ee][q] = op("gpsimd", lambda e, ee=ee, q=q: e.dma_start(
                    out=w1s[:, 0:DC * DFF].rearrange(
                        "p (c f) -> p c f", f=DFF)[:, :, q * FQ * 128:(q + 1) * FQ * 128],
                    in_=w1_in[ee].rearrange("(c p) f -> p c f", p=128)[
                        :, :, q * FQ * 128:(q + 1) * FQ * 128]),
                    waits=[("pe", pe_gate_h)] if q == 0 else [],
                    inc="gd", inc_k=16)
            for q in range(4):
                w2q[ee][q] = op("gpsimd", lambda e, ee=ee, q=q: e.dma_start(
                    out=w2s[:, q * FQ * 8 * D // 8:0] if False else
                        w2s[:, q * FQ * D * 1:0] if False else
                        w2s[:, q * (FC // 4) * D:(q + 1) * (FC // 4) * D],
                    in_=w2_in[ee].rearrange("(c p) f -> p c f", p=128)[
                        :, q * (FC // 4):(q + 1) * (FC // 4), :]),
                    waits=[("pe", pe_gate_e)] if q == 0 else [],
                    inc="gd", inc_k=16)

        if pre is not None:
            w1q[0], w2q[0] = pre
        else:
            emit_w_loads(0, cnt["pe"], cnt["pe"])

        m_ld = {}
        m_dtrans = {}
        m_fmevac = {}
        m_eotrans = {}
        m_stgevA = {}
        m_stgev = {}
        m_store = {}
        m_relu = {}
        m_ev8 = {}

        def tinfo(T):
            ee, r0, rn = tiles[T]
            return ee, ee * CAP + r0, rn, chunks_of(rn, 128)

        def emit_disp_load(T):
            ee, base, rn, tcs = tinfo(T)
            w0 = []
            if T == 0:
                w0 = [("gd", v_scat), ("pe", cnt["pe"]),
                      ("dv", cnt["dv"]), ("ac", cnt["ac"])]
            elif T - 2 in m_dtrans:
                w0 = [("pe", m_dtrans[T - 2])]
            v = None
            for ti, (t0, tn) in enumerate(tcs):
                v = op("sync", lambda e, base=base, t0=t0, tn=tn, ti=ti, T=T:
                       e.dma_start(
                    out=disp_in_b[T % 2][0:tn, ti * D:(ti + 1) * D],
                    in_=disp[base + t0:base + t0 + tn, :]),
                    waits=w0 if ti == 0 else [], inc="ld", inc_k=16)
            m_ld[T] = v

        def emit_disp_trans(T):
            ee, base, rn, tcs = tinfo(T)
            p = T % 2
            for ti, (t0, tn) in enumerate(tcs):
                if ti == 0:
                    w0 = [("ld", m_ld[T]), ("dv", m_stgevA.get(T - 3, 0))]
                else:
                    w0 = [("dv", m_fmevac[T])]
                for dc in range(DC):
                    op("tensor", lambda e, ti=ti, dc=dc, tn=tn, p=p: e.transpose(
                        out=p_smb[0:128, dc * 128:dc * 128 + tn],
                        in_=disp_in_b[p][0:tn, ti * D + dc * 128:ti * D + (dc + 1) * 128],
                        identity=identb[:]),
                        waits=w0 if dc == 0 else [],
                        inc="pe" if dc == DC - 1 else None)
                vpet = cnt["pe"]
                m_fmevac[T] = op("vector", lambda e, ti=ti, tn=tn, rn=rn, p=p:
                                 e.tensor_copy(
                    out=disp_fm_b[p][:, 0:DC * rn].rearrange(
                        "p (dc f) -> p dc f", f=rn)[:, :, ti * 128:ti * 128 + tn],
                    in_=p_smb[:, 0:DC * 128].rearrange(
                        "p (dc f) -> p dc f", f=128)[:, :, 0:tn]),
                    waits=[("pe", vpet), ("pe", m_hlast.get(("t", T - 2), 0))],
                    inc="dv")
            m_dtrans[T] = cnt["pe"]

        def emit_h(T):
            ee, base, rn, tcs = tinfo(T)
            p = T % 2
            first_of_e = (T % per_e == 0)
            marks = []
            for ff in range(FC):
                pb = p_h0 if ff % 2 == 0 else p_h1
                w0 = []
                if ff == 0:
                    w0 = [("dv", m_fmevac[T]), ("ac", m_relu.get(T - 1, [0])[-1])]
                elif ff >= 2:
                    w0 = [("ac", marks[ff - 2])]
                if first_of_e and ff % FQ == 0:
                    w0 = w0 + [("gd", w1q[ee][ff // FQ])]
                for dc in range(DC):
                    op("tensor", lambda e, ff=ff, dc=dc, rn=rn, pb=pb, p=p: e.matmul(
                        out=pb[:, 0:rn],
                        lhsT=w1s[:, dc * DFF + ff * 128:dc * DFF + (ff + 1) * 128],
                        rhs=disp_fm_b[p][:, dc * rn:(dc + 1) * rn],
                        start=(dc == 0), stop=(dc == DC - 1)),
                        waits=w0 if dc == 0 else [],
                        inc="pe" if dc == DC - 1 else None)
                v_mm = cnt["pe"]
                marks.append(op("scalar", lambda e, ff=ff, rn=rn, pb=pb:
                                e.activation(
                    out=h_sb[:, ff * rn:(ff + 1) * rn], in_=pb[:, 0:rn],
                    func=AF.Relu), waits=[("pe", v_mm)], inc="ac"))
            m_relu[T] = marks
            m_hlast[("t", T)] = cnt["pe"]
            if (T + 1) % per_e == 0:
                m_hlast[ee] = cnt["pe"]
                if ee + 1 < E:
                    emit_w1_next(ee + 1)

        def emit_w1_next(ee):
            for q in range(4):
                w1q[ee][q] = op("gpsimd", lambda e, ee=ee, q=q: e.dma_start(
                    out=w1s[:, 0:DC * DFF].rearrange(
                        "p (c f) -> p c f", f=DFF)[:, :, q * FQ * 128:(q + 1) * FQ * 128],
                    in_=w1_in[ee].rearrange("(c p) f -> p c f", p=128)[
                        :, :, q * FQ * 128:(q + 1) * FQ * 128]),
                    waits=[("pe", m_hlast[ee - 1])] if q == 0 else [],
                    inc="gd", inc_k=16)

        def emit_w2_next(ee):
            for q in range(4):
                w2q[ee][q] = op("gpsimd", lambda e, ee=ee, q=q: e.dma_start(
                    out=w2s[:, q * (FC // 4) * D:(q + 1) * (FC // 4) * D],
                    in_=w2_in[ee].rearrange("(c p) f -> p c f", p=128)[
                        :, q * (FC // 4):(q + 1) * (FC // 4), :]),
                    waits=[("pe", m_elast[ee - 1])] if q == 0 else [],
                    inc="gd", inc_k=16)

        def emit_eout(T):
            ee, base, rn, tcs = tinfo(T)
            ev = []
            for dco in range(DC):
                pb = p_e0 if dco % 2 == 0 else p_e1
                w0 = []
                if dco == 0:
                    w0 = [("ac", m_relu[T][-1]), ("gd", w2q[ee][3]),
                          ("dv", m_ev8.get(T - 1, [0])[-1])]
                elif dco >= 2:
                    w0 = [("dv", ev[dco - 2])]
                for ff in range(FC):
                    op("tensor", lambda e, dco=dco, ff=ff, rn=rn, pb=pb: e.matmul(
                        out=pb[:, 0:rn],
                        lhsT=w2s[:, ff * D + dco * 128:ff * D + (dco + 1) * 128],
                        rhs=h_sb[:, ff * rn:(ff + 1) * rn],
                        start=(ff == 0), stop=(ff == FC - 1)),
                        waits=w0 if ff == 0 else [],
                        inc="pe" if ff == FC - 1 else None)
                v_mm = cnt["pe"]
                ev.append(op("vector", lambda e, rn=rn, pb=pb, dco=dco: e.tensor_copy(
                    out=eo_bf8[:, dco * 256:dco * 256 + rn], in_=pb[:, 0:rn]),
                    waits=[("pe", v_mm), ("pe", m_eotrans.get(T - 1, 0))],
                    inc="dv"))
            m_ev8[T] = ev
            m_elast[("t", T)] = cnt["pe"]
            if (T + 1) % per_e == 0:
                m_elast[ee] = cnt["pe"]
                if ee + 1 < E:
                    emit_w2_next(ee + 1)

        def emit_eout_trans(T):
            ee, base, rn, tcs = tinfo(T)
            ntc = len(tcs)
            half = DC // 2
            for b in range(2):
                bankv = p_smb if b == 0 else p_lgb
                dcos = list(range(b * half, (b + 1) * half))
                n_tr = len(dcos) * ntc
                i_tr = 0
                for dco in dcos:
                    for ti, (t0, tn) in enumerate(tcs):
                        w0 = []
                        if i_tr == 0:
                            w0 = [("dv", m_ev8[T][-1])]
                            if b == 0:
                                w0.append(("dv", m_fmevac.get(T + 2, 0)))
                        op("tensor", lambda e, dco=dco, ti=ti, tn=tn, bankv=bankv,
                           b=b, half=half: e.transpose(
                            out=bankv[0:tn, ((dco - b * half) * 2 + ti) * 128:
                                      ((dco - b * half) * 2 + ti) * 128 + 128],
                            in_=eo_bf8[:, dco * 256 + ti * 128:dco * 256 + ti * 128 + tn],
                            identity=identb[:]),
                            waits=w0,
                            inc="pe" if i_tr == n_tr - 1 else None)
                        i_tr += 1
                vpet = cnt["pe"]
                rm = min(rn, 128)
                bankv_cur = bankv
                m_stgev[T] = op("vector", lambda e, b=b, half=half, ntc=ntc,
                                rm=rm, bankv=bankv_cur: e.tensor_copy(
                    out=eo_stage[0:rm, 0:ntc * D].rearrange(
                        "p (ti dco j) -> p ti dco j", dco=DC, j=128)[
                        :, :, b * half:(b + 1) * half, :],
                    in_=bankv[0:rm, 0:half * ntc * 128].rearrange(
                        "p (dco ti j) -> p ti dco j", ti=ntc, j=128)),
                    waits=[("pe", vpet), ("ld", m_store.get(T - 1, 0))],
                    inc="dv")
                if b == 0:
                    m_stgevA[T] = m_stgev[T]
            m_eotrans[T] = cnt["pe"]
            v = None
            for ti, (t0, tn) in enumerate(tcs):
                v = op("sync", lambda e, base=base, t0=t0, tn=tn, ti=ti:
                       e.dma_start(
                    out=eout_d[base + t0:base + t0 + tn, :],
                    in_=eo_stage[0:tn, ti * D:(ti + 1) * D]),
                    waits=[("dv", m_stgev[T])] if ti == 0 else [],
                    inc="ld", inc_k=16)
            m_store[T] = v

        emit_disp_load(0)
        emit_disp_trans(0)
        if NTL > 1:
            emit_disp_load(1)
        for T in range(NTL):
            emit_h(T)
            if T + 1 < NTL:
                emit_disp_trans(T + 1)
            if T + 2 < NTL:
                emit_disp_load(T + 2)
            if T - 1 >= 0:
                emit_eout_trans(T - 1)
            emit_eout(T)
        emit_eout_trans(NTL - 1)

    cb_eo = (eo_g, thi,
             bigC[:, 4096:4096 + D], bigC[:, 4096 + D:4096 + 2 * D])
    cb_tw = (tt_g, wtmpb,
             bigC[:, 4096 + 2 * D:4096 + 3 * D],
             bigC[:, 4096 + 3 * D:4096 + 4 * D])
    cb_x = (xt, xt2,
            big1[:, 26624:26624 + 2048].bitcast(F32),
            big1[:, 26624 + 2048:26624 + 4096].bitcast(F32))
    cb_o = (tf, wtmp,
            eo_bf8[:, 0:2048].bitcast(F32),
            eo_stage[:, 0:2048].bitcast(F32))

    def combine_tiles(xsrc, dst, tiles, cm, v_eo, cycle=(0, 1, 2, 3)):
        for k in tiles:
            p = cycle[cm.get("_i", 0) % len(cycle)]
            cm["_i"] = cm.get("_i", 0) + 1
            pw = cm.get(("p", p))
            vg = op("gpsimd", lambda e, k=k, p=p: e.indirect_dma_start(
                out=cb_eo[p][:, 0:D], out_offset=None,
                in_=eout_d,
                in_offset=IndirectOffsetOnAxis(flatIi[:, k:k + 1], 0)),
                waits=[("ld", v_eo), ("dv", pw[0] if pw else 0)],
                inc="gd", inc_k=16)
            vx = op("sync", lambda e, k=k, p=p: e.dma_start(
                out=cb_x[p][:], in_=xsrc[k * 128:(k + 1) * 128, :]),
                waits=([("dv", pw[1])] if pw else [("ld", v_eo)]),
                inc="ld", inc_k=16)
            vtw = op("gpsimd", lambda e, k=k, p=p: e.tensor_scalar(
                out=cb_tw[p][:], in0=cb_x[p][:], scalar1=rstdS[:, k:k + 1],
                scalar2=nmr[:, k:k + 1], op0=ALU.mult, op1=ALU.add),
                waits=[("ld", vx)] + ([("dv", pw[1])] if pw else []),
                inc="gd", inc_k=1)
            op("vector", lambda e, p=p: e.tensor_tensor(
                out=cb_o[p][:], in0=cb_eo[p][:, 0:D], in1=cb_tw[p][:],
                op=ALU.subtract),
                waits=[("gd", vg), ("gd", vtw), ("ld", pw[2] if pw else 0)])
            op("vector", lambda e, k=k, p=p: e.scalar_tensor_tensor(
                out=cb_o[p][:], in0=cb_o[p][:], scalar=keptS[:, k:k + 1],
                in1=cb_tw[p][:], op0=ALU.mult, op1=ALU.add))
            v6 = op("vector", lambda e, k=k, p=p: e.scalar_tensor_tensor(
                out=cb_o[p][:], in0=cb_o[p][:], scalar=sfacS[:, k:k + 1],
                in1=cb_x[p][:], op0=ALU.mult, op1=ALU.add), inc="dv")
            vs = op("sync", lambda e, k=k, p=p, dst=dst: e.dma_start(
                out=dst[k * 128:(k + 1) * 128, :], in_=cb_o[p][:]),
                waits=[("dv", v6)], inc="ld", inc_k=16)
            cm[k] = cm[("p", p)] = (v6, v6, vs)

    def stats_tile(s, k, st, cm):
        srow = s * SEQ
        amark = st.setdefault(("am", s), {})
        x_t = xt if k % 2 == 0 else xt2
        w0 = [("ld", cm[s * SEQT + k][2])]
        w0 += ([("dv", st["v8"])] if k < 2 else
               [("ac", amark[k - 2][0]), ("dv", amark[k - 2][1])])
        v = op("sync", lambda e, k=k, srow=srow, x_t=x_t: e.dma_start(
            out=x_t[:], in_=xcur[srow + k * 128:srow + (k + 1) * 128, :]),
            waits=w0, inc="ld", inc_k=16)
        va = op("scalar", lambda e, k=k, x_t=x_t: e.activation(
            out=sq_scr[:], in_=x_t[:], func=AF.Square,
            accum_out=sscol[:, k:k + 1]),
            waits=[("ld", v)] + ([("dv", st["v7"])] if k == 0 else []),
            inc="ac")
        vd = op("vector", lambda e, k=k, x_t=x_t: e.reduce_sum(
            out=smcol[:, k:k + 1], in_=x_t[:], axis=AX.X),
            waits=[("ld", v)], inc="dv")
        st["v_sm"] = 0
        amark[k] = (va, vd)

    def attention_seq(s, st, cm):
        frs = chunks_of(SEQ, 512)
        srow = s * SEQ
        if True:
            v_ss = cnt["ac"]
            op("vector", lambda e: e.tensor_scalar(
                out=mucol[:, 0:SEQT], in0=smcol[:, 0:SEQT], scalar1=1.0 / D,
                scalar2=None, op0=ALU.mult),
                waits=[("ac", v_ss), ("gd", st["v_sm"])])
            op("vector", lambda e: e.tensor_scalar(
                out=scal[:, 0:SEQT], in0=sscol[:, 0:SEQT], scalar1=1.0 / D,
                scalar2=None, op0=ALU.mult))
            op("vector", lambda e: e.tensor_tensor(
                out=smcol[:, 0:SEQT], in0=mucol[:, 0:SEQT],
                in1=mucol[:, 0:SEQT], op=ALU.mult))
            v1a = op("vector", lambda e: e.tensor_tensor(
                out=scal[:, 0:SEQT], in0=scal[:, 0:SEQT], in1=smcol[:, 0:SEQT],
                op=ALU.subtract), inc="dv")
            v1b = op("vector", lambda e: e.tensor_scalar(
                out=smcol[:, 0:SEQT], in0=scal[:, 0:SEQT], scalar1=1e-5,
                scalar2=None, op0=ALU.add), inc="dv")
            va = op("scalar", lambda e: e.activation(
                out=smcol[:, 0:SEQT], in_=smcol[:, 0:SEQT], func=AF.Sqrt),
                waits=[("dv", v1b)], inc="ac")
            op("vector", lambda e: e.reciprocal(
                out=smcol[:, 0:SEQT], in_=smcol[:, 0:SEQT]), waits=[("ac", va)])
            op("vector", lambda e: e.tensor_tensor(
                out=scal[:, 0:SEQT], in0=scal[:, 0:SEQT], in1=smcol[:, 0:SEQT],
                op=ALU.mult))
            op("vector", lambda e: e.tensor_tensor(
                out=scal[:, 0:SEQT], in0=scal[:, 0:SEQT], in1=smcol[:, 0:SEQT],
                op=ALU.mult))
            v1c = op("vector", lambda e: e.tensor_scalar(
                out=scal[:, 0:SEQT], in0=scal[:, 0:SEQT], scalar1=float(D),
                scalar2=None, op0=ALU.mult), inc="dv")
            va = op("scalar", lambda e: e.activation(
                out=scal[:, 0:SEQT], in_=scal[:, 0:SEQT], func=AF.Sqrt),
                waits=[("dv", v1c)], inc="ac")
            op("vector", lambda e: e.tensor_scalar(
                out=scal[:, 0:SEQT], in0=scal[:, 0:SEQT], scalar1=1e-12,
                scalar2=None, op0=ALU.max), waits=[("ac", va)])
            op("vector", lambda e: e.reciprocal(
                out=scal[:, 0:SEQT], in_=scal[:, 0:SEQT]))
            op("vector", lambda e: e.tensor_scalar(
                out=scal[:, 0:SEQT], in0=scal[:, 0:SEQT],
                scalar1=float(math.sqrt(D)), scalar2=None, op0=ALU.mult))
            v7 = op("vector", lambda e: e.tensor_tensor(
                out=scal[:, 0:SEQT], in0=scal[:, 0:SEQT], in1=smcol[:, 0:SEQT],
                op=ALU.mult), inc="dv")
            st["v7"] = v7
            xn_ev = [st["stg"]]
            xmark2 = {}
            for k in range(SEQT):
                x_t = xt if k % 2 == 0 else xt2
                w0 = ([("dv", v7)] if k == 0 else
                      ([("dv", xmark2[k - 2])] if k - 2 in xmark2 else []))
                v = op("sync", lambda e, k=k, srow=srow, x_t=x_t: e.dma_start(
                    out=x_t[:], in_=xcur[srow + k * 128:srow + (k + 1) * 128, :]),
                    waits=w0, inc="ld", inc_k=16)
                op("vector", lambda e, k=k, x_t=x_t: e.tensor_scalar(
                    out=tf[:], in0=x_t[:], scalar1=mucol[:, k:k + 1],
                    scalar2=scal[:, k:k + 1], op0=ALU.subtract, op1=ALU.mult),
                    waits=[("ld", v), ("dv", v7)])
                op("vector", lambda e: e.tensor_tensor(
                    out=tf[:], in0=tf[:], in1=gam[:], op=ALU.mult))
                v8 = op("vector", lambda e: e.tensor_copy(out=thi[:], in_=tf[:]),
                        inc="dv")
                xmark2[k] = v8
                for dc in range(DC):
                    op("tensor", lambda e, dc=dc: e.transpose(
                        out=p_smb[:, dc * 128:(dc + 1) * 128],
                        in_=thi[:, dc * 128:(dc + 1) * 128], identity=identb[:]),
                        waits=[("dv", v8), xn_ev[0]] if dc == 0 else [],
                        inc="pe" if dc == DC - 1 else None)
                vpet = cnt["pe"]
                xn_ev[0] = ("dv", op("vector", lambda e, k=k: e.tensor_copy(
                    out=xn_fm[:, 0:DC * SEQ].rearrange(
                        "p (dc s) -> p dc s", s=SEQ)[:, :, k * 128:(k + 1) * 128],
                    in_=p_smb[:, 0:DC * 128].rearrange(
                        "p (dc f) -> p dc f", f=128)),
                    waits=[("pe", vpet)], inc="dv"))
            st["v8"] = xmark2[SEQT - 1]
            st["xnpe"] = cnt["pe"]
            v_xn = xn_ev[0]
            for g in range(C.GC):
                def qkva(oc, dest, first_w):
                    evs = []
                    for fi, (f0, fn_) in enumerate(frs):
                        tick = st["tick"]
                        st["tick"] = tick + 1
                        pb = p_q0 if tick % 2 == 0 else p_q1
                        w0 = [st["bank"][tick - 2]] if tick - 2 in st["bank"] \
                            else []
                        if fi == 0 and g == 0:
                            w0 = w0 + [v_xn]
                        for dc in range(DC):
                            op("tensor", lambda e, oc=oc, dc=dc, f0=f0, fn_=fn_,
                               pb=pb: e.matmul(
                                out=pb[:, 0:fn_],
                                lhsT=wT[:, (oc * DC + dc) * 128:(oc * DC + dc + 1) * 128],
                                rhs=xn_fm[:, dc * SEQ + f0:dc * SEQ + f0 + fn_],
                                start=(dc == 0), stop=(dc == DC - 1)),
                                waits=w0 if dc == 0 else [],
                                inc="pe" if dc == DC - 1 else None)
                        v_mm = cnt["pe"]
                        ev = dest(fi, f0, fn_, pb, v_mm,
                                  first_w if fi == 0 else [])
                        st["bank"][tick] = ev
                        evs.append(ev)
                    return evs

                def kv_ev(fi, f0, fn_, pb, v_mm, fw):
                    return ("ac", op("scalar", lambda e, f0=f0, fn_=fn_, pb=pb:
                                     e.activation(
                        out=kvf[:, f0:f0 + fn_], in_=pb[:, 0:fn_],
                        func=AF.Copy), waits=[("pe", v_mm)] + fw, inc="ac"))

                def a_ev(fi, f0, fn_, pb, v_mm, fw):
                    return ("ac", op("scalar", lambda e, f0=f0, fn_=fn_, pb=pb:
                                     e.activation(
                        out=af[:, f0:f0 + fn_], in_=pb[:, 0:fn_],
                        func=AF.Sigmoid), waits=[("pe", v_mm)] + fw, inc="ac"))

                def q_ev(fi, f0, fn_, pb, v_mm, fw):
                    return ("ac", op("scalar", lambda e, f0=f0, fn_=fn_, pb=pb:
                                     e.activation(
                        out=qb[:, f0:f0 + fn_], in_=pb[:, 0:fn_],
                        func=AF.Copy), waits=[("pe", v_mm)] + fw, inc="ac"))

                qkva(C.GC + g, kv_ev, [("dv", st["v9g"])])
                qkva(2 * C.GC + g, a_ev, [])
                qkva(g, q_ev, [])
                v_a = cnt["ac"]
                op("vector", lambda e: e.tensor_tensor_scan(
                    out=yf[:, 0:SEQ], data0=af[:, 0:SEQ], data1=kvf[:, 0:SEQ],
                    initial=0.0, op0=ALU.mult, op1=ALU.add),
                    waits=[("ac", v_a)])
                v9 = op("vector", lambda e: e.tensor_tensor(
                    out=of[:, 0:SEQ], in0=qb[:, 0:SEQ], in1=yf[:, 0:SEQ],
                    op=ALU.mult),
                    waits=[("pe", st["oevg"])], inc="dv")
                st["v9g"] = v9
                o_ev = []
                for k in range(SEQT):
                    bk = p_sm[:, 0:128] if k % 2 == 0 else p_lgt[:, 0:128]
                    w0 = []
                    if k == 0:
                        w0 = [("dv", v9)]
                    elif k >= 2:
                        w0 = [("dv", o_ev[k - 2])]
                    vpe = op("tensor", lambda e, k=k, bk=bk: e.transpose(
                        out=bk, in_=of[:, k * 128:(k + 1) * 128],
                        identity=identf[:]), waits=w0, inc="pe")
                    o_ev.append(op("vector", lambda e, k=k, bk=bk: e.tensor_copy(
                        out=stg[:, k * 128:(k + 1) * 128], in_=bk),
                        waits=[("pe", vpe)] +
                              ([("gd", st["accum"])] if k == 0 else []),
                        inc="dv"))
                st["oevg"] = cnt["pe"]
                st["stg"] = ("dv", o_ev[-1])
                v10 = o_ev[-1]
                st["accum"] = op("gpsimd", lambda e, g=g, srow=srow: e.dma_start(
                    out=xcur[srow:srow + SEQ, g * 128:(g + 1) * 128].rearrange(
                        "(c p) f -> p c f", p=128),
                    in_=stg[:, 0:SEQ].rearrange("p (c f) -> p c f", f=128),
                    accum_op=ALU.add),
                    waits=[("dv", v10)], inc="gd", inc_k=16)
                st["inter"](s, g)

    def middle_phase(combine_only=False, no_inter=False, no_inter_stats=False):
        v_eo = cnt["ld"]
        wait("sync", "gd", cnt["gd"])
        v_wt = op("sync", lambda e: e.dma_start(out=wT, in_=wT_d),
                  waits=[("pe", cnt["pe"]), ("dv", cnt["dv"]),
                         ("gd", cnt["gd"])], inc="ld", inc_k=16)
        wait("tensor", "ld", v_wt)
        cm = {}
        st = {"v9g": 0, "oevg": 0, "accum": 0, "stg": ("dv", 0),
              "v8": 0, "v7": 0, "sc9": 0, "v_sm": 0, "bank": {}, "tick": 0}

        def inter(s, g):
            if no_inter:
                return
            s1 = s + 1
            if s1 >= C.NSEQ:
                return
            combine_tiles(xin, xcur,
                          range(s1 * SEQT + 2 * g, s1 * SEQT + 2 * g + 2),
                          cm, v_eo, cycle=(2, 3))
            if g >= 1 and not no_inter_stats:
                stats_tile(s1, 2 * g - 2, st, cm)
                stats_tile(s1, 2 * g - 1, st, cm)

        st["inter"] = inter
        if combine_only:
            combine_tiles(xin, xcur, range(RT), cm, v_eo, cycle=(2, 3))
            return
        combine_tiles(xin, xcur, range(SEQT), cm, v_eo, cycle=(2, 3))
        for k in range(SEQT):
            stats_tile(0, k, st, cm)
        for s in range(C.NSEQ):
            attention_seq(s, st, cm)
            if s + 1 < C.NSEQ:
                if no_inter:
                    combine_tiles(xin, xcur,
                                  range((s + 1) * SEQT, (s + 2) * SEQT),
                                  cm, v_eo, cycle=(2, 3))
                if no_inter or no_inter_stats:
                    for k in range(SEQT - 2):
                        stats_tile(s + 1, k, st, cm)
                stats_tile(s + 1, SEQT - 2, st, cm)
                stats_tile(s + 1, SEQT - 1, st, cm)

    pre_box = {}
    if "r1" in phases:
        routing_phase(xin, prefetch=lambda vg2: pre_box.update(
            m=emit_w0_prefetch([("gd", vg2), ("ld", v_wT_st)], [("gd", vg2)])))
    if "e1" in phases:
        expert_phase(pre=pre_box.pop("m", None))
    if "m" in phases:
        middle_phase()
    if "mq" in phases:
        middle_phase(combine_only=True)
    if "mn" in phases:
        middle_phase(no_inter=True)
    if "ms" in phases:
        middle_phase(no_inter_stats=True)
    if "r2" in phases:
        wait("sync", "gd", cnt["gd"])
        wait("sync", "ld", cnt["ld"])
        wait("gpsimd", "gd", cnt["gd"])
        wait("gpsimd", "ld", cnt["ld"])
        pe_r2 = cnt["pe"]
        routing_phase(xcur, prefetch=lambda vg2: pre_box.update(
            m=emit_w0_prefetch([("gd", vg2)], [("gd", vg2), ("pe", pe_r2)])))
    if "e2" in phases:
        expert_phase(pre=pre_box.pop("m", None))
    if "c2" in phases:
        wait("sync", "gd", cnt["gd"])
        combine_tiles(xcur, yout, range(RT), {}, cnt["ld"])
    if "dx" in phases:
        wait("gpsimd", "ld", cnt["ld"])
        wait("gpsimd", "gd", cnt["gd"])
        op("gpsimd", lambda e: e.dma_start(out=yout, in_=xcur),
           inc="gd", inc_k=16)
    if "dt" in phases:
        wait("gpsimd", "ld", cnt["ld"])
        wait("gpsimd", "gd", cnt["gd"])
        op("gpsimd", lambda e: e.dma_start(out=yout, in_=t_d),
           inc="gd", inc_k=16)
    if "de" in phases:
        wait("gpsimd", "ld", cnt["ld"])
        wait("gpsimd", "gd", cnt["gd"])
        op("gpsimd", lambda e: e.dma_start(
            out=yout[0:min(NT, E * CAP), :], in_=eout_d[0:min(NT, E * CAP), :]),
           inc="gd", inc_k=16)

    wait("gpsimd", "gd", cnt["gd"])
    wait("gpsimd", "ld", cnt["ld"])
    wait("sync", "ld", cnt["ld"])
    wait("sync", "gd", cnt["gd"])

    ctx.run()
    es.close()
    return nc


def _layout_inputs(C, x, wq, gamma, wr, w1, w2):
    gam_b = np.ascontiguousarray(
        np.tile(np.asarray(gamma, np.float32).reshape(1, C.D), (128, 1)))
    return dict(
        xin=np.ascontiguousarray(np.asarray(x, np.float32).reshape(C.NT, C.D)),
        wqkva=np.ascontiguousarray(np.asarray(wq, np.float32)),
        gammab=gam_b,
        wr_in=np.ascontiguousarray(np.asarray(wr, np.float32)),
        w1_in=np.ascontiguousarray(np.asarray(w1, np.float32)),
        w2_in=np.ascontiguousarray(np.asarray(w2, np.float32)),
    )


def kernel(L0_attn_x, L0_attn_W_qkva, L0_attn_gamma, L0_ffn_Wr, L0_ffn_W1,
           L0_ffn_W2):
    C = REAL
    nc = bass.Bass("TRN2", target_bir_lowering=False)
    build_program(nc, C)
    in_map = _layout_inputs(C, L0_attn_x, L0_attn_W_qkva, L0_attn_gamma,
                            L0_ffn_Wr, L0_ffn_W1, L0_ffn_W2)
    res = run_bass_kernel_spmd(nc, [in_map], core_ids=[0])
    return res.results[0]["yout"].reshape(16, 2048, 1024)


# version 11
# speedup vs baseline: 1.0109x; 1.0109x over previous
import math
import numpy as np
import concourse.bass as bass
import concourse.mybir as mybir
from concourse.bass import IndirectOffsetOnAxis
from concourse.bass_utils import run_bass_kernel_spmd

F32 = mybir.dt.float32
BF16 = mybir.dt.bfloat16
I32 = mybir.dt.int32
AF = mybir.ActivationFunctionType
ALU = mybir.AluOpType
AX = mybir.AxisListType

BIG = 1 << 20


class CFG:
    def __init__(self, ntok, d, dff, e, seq, cap):
        self.NT, self.D, self.DFF, self.E, self.SEQ, self.CAP = \
            ntok, d, dff, e, seq, cap
        self.NSEQ = ntok // seq
        self.DC = d // 128
        self.FC = dff // 128
        self.RT = ntok // 128
        self.OC = 3 * d // 128
        self.GC = d // 128
        assert ntok % 128 == 0 and d % 128 == 0 and dff % 128 == 0
        assert seq % 128 == 0 and ntok % seq == 0 and cap % 128 == 0


REAL = CFG(ntok=32768, d=1024, dff=4096, e=8, seq=2048, cap=5120)


def chunks_of(total, size):
    out, o = [], 0
    while o < total:
        out.append((o, min(size, total - o)))
        o += size
    return out


class Ctx:
    ENGINES = ("sync", "tensor", "vector", "scalar", "gpsimd")

    def __init__(self, nc):
        self.nc = nc
        self.streams = {e: [] for e in self.ENGINES}
        self.cnt = {}
        self.sems = {}
        for name in ("ld", "gd", "pe", "dv", "ac"):
            self.sems[name] = nc.alloc_semaphore(name)
            self.cnt[name] = 0

    AUTO = {"vector": "dv", "scalar": "ac"}
    SELFWAIT = {"vector"}

    def op(self, engine, build, waits=(), inc=None, inc_k=1):
        waits = list(waits)
        auto = self.AUTO.get(engine)
        if auto is not None:
            if engine in self.SELFWAIT:
                waits.append((auto, self.cnt[auto]))
            if inc is None:
                inc, inc_k = auto, 1
            assert inc == auto
        if inc is not None:
            self.cnt[inc] += inc_k
        sems = self.sems

        def fn(eng, build=build, waits=tuple(waits), inc=inc, inc_k=inc_k):
            best = {}
            for s, v in waits:
                if v > 0:
                    best[s] = max(best.get(s, 0), v)
            for s, v in best.items():
                eng.wait_ge(sems[s], v)
            ins = build(eng)
            if inc is not None and ins is not None:
                ins.then_inc(sems[inc], inc_k)
        self.streams[engine].append(fn)
        return self.cnt.get(inc) if inc is not None else None

    def wait(self, engine, name, val):
        sems = self.sems
        if val > 0:
            self.streams[engine].append(
                lambda eng, s=name, v=val: eng.wait_ge(sems[s], v))

    def run(self):
        with self.nc.Block() as block:
            @block.sync
            def _(eng):
                for fn in self.streams["sync"]:
                    fn(eng)

            @block.tensor
            def _(eng):
                for fn in self.streams["tensor"]:
                    fn(eng)

            @block.vector
            def _(eng):
                for fn in self.streams["vector"]:
                    fn(eng)

            @block.scalar
            def _(eng):
                for fn in self.streams["scalar"]:
                    fn(eng)

            @block.gpsimd
            def _(eng):
                for fn in self.streams["gpsimd"]:
                    fn(eng)


def build_program(nc, C, phases=("r1", "e1", "m", "r2", "e2", "c2")):
    NT, D, DFF, E, CAP, SEQ = C.NT, C.D, C.DFF, C.E, C.CAP, C.SEQ
    DC, FC, RT = C.DC, C.FC, C.RT

    xin = nc.dram_tensor("xin", [NT, D], F32, kind="ExternalInput").ap()
    wqkva = nc.dram_tensor("wqkva", [3 * D, D], F32, kind="ExternalInput").ap()
    gammab = nc.dram_tensor("gammab", [128, D], F32, kind="ExternalInput").ap()
    wr_in = nc.dram_tensor("wr_in", [D, E], F32, kind="ExternalInput").ap()
    w1_in = nc.dram_tensor("w1_in", [E, D, DFF], F32, kind="ExternalInput").ap()
    w2_in = nc.dram_tensor("w2_in", [E, DFF, D], F32, kind="ExternalInput").ap()
    yout = nc.dram_tensor("yout", [NT, D], F32, kind="ExternalOutput").ap()

    xcur = nc.dram_tensor("xcur", [NT, D], F32, kind="Internal").ap()
    t_d = nc.dram_tensor("t_d", [NT, D], BF16, kind="Internal").ap()
    disp = nc.dram_tensor("disp", [E * CAP + 1, D], BF16, kind="Internal").ap()
    eout_d = nc.dram_tensor("eout_d", [E * CAP, D], BF16, kind="Internal").ap()
    bases_d = nc.dram_tensor("bases_d", [1, RT * E], F32, kind="Internal").ap()
    wT_d = nc.dram_tensor("wT_d", [128, 3 * D * D // 128], BF16, kind="Internal").ap()

    ident_np = np.eye(128, dtype=np.float32)
    identb_d = nc.inline_tensor(ident_np, "identb_d").ap()
    identf_d = nc.inline_tensor(ident_np, "identf_d").ap()
    U_d = nc.inline_tensor(np.triu(np.ones((128, 128), np.float32)), "U_d").ap()
    iota8_d = nc.inline_tensor(
        np.tile(np.arange(E, dtype=np.float32), (128, 1)), "iota8_d").ap()
    ones_d = nc.inline_tensor(np.ones((128, 1), np.float32), "ones_d").ap()

    ctx = Ctx(nc)
    from contextlib import ExitStack
    es = ExitStack()

    def sb(name, shape, dtype):
        return es.enter_context(nc.sbuf_tensor(name, shape, dtype))

    def ps(name, shape, dtype):
        return es.enter_context(nc.psum_tensor(name, shape, dtype))

    identb = sb("identb", [128, 128], BF16)
    identf = sb("identf", [128, 128], F32)
    Ub = sb("Ub", [128, 128], BF16)
    iota8 = sb("iota8", [128, E], F32)
    ones8f = sb("ones8f", [128, E], F32)
    onesRT = sb("onesRT", [1, RT], F32)
    onesb = sb("onesb", [128, 1], BF16)
    gam = sb("gam", [128, D], F32)
    wrh = sb("wrh", [128, DC * E], BF16)
    wrl = sb("wrl", [128, DC * E], BF16)
    wrf = sb("wrf", [128, DC * E], F32)

    idxS = sb("idxS", [128, RT], F32)
    posS = sb("posS", [128, RT], F32)
    keptS = sb("keptS", [128, RT], F32)
    pmaxS = sb("pmaxS", [128, RT], F32)
    sfacS = pmaxS
    flatI = sb("flatI", [128, RT], F32)
    flatIi = sb("flatIi", [128, RT], I32)
    scatI = sb("scatI", [128, 2], I32)
    meanS = sb("meanS", [128, RT], F32)
    rstdS = sb("rstdS", [128, RT], F32)
    nmr = sb("nmr", [128, RT], F32)
    statsA = meanS
    statsB = rstdS
    wrhl = sb("wrhl", [128, DC * 2 * E], BF16)

    SEQT = SEQ // 128
    A1sz = max(DC * DFF, C.OC * DC * 128 + SEQ, 3 * E * RT * 2 + 2 * RT)
    big1 = sb("big1", [128, A1sz], BF16)
    A2sz = max(FC * D, DC * SEQ + 4 * 2 * SEQ)
    big2 = sb("big2", [128, A2sz], BF16)
    bb_off = ((3 * D + 4 * SEQT + 1) // 2) * 2
    A3sz = max(FC * 256, 2 * SEQ, bb_off + 2 * E * RT)
    bigC = sb("bigC", [128, A3sz], BF16)

    w1s = big1[:, 0:DC * DFF]
    w2s = big2[:, 0:FC * D]
    h_sb = bigC[:, 0:FC * 256]
    wT = big1[:, 0:C.OC * DC * 128]
    qb = big1[:, C.OC * DC * 128:C.OC * DC * 128 + SEQ]
    xn_fm = big2[:, 0:DC * SEQ]
    kvf = big2[:, DC * SEQ:DC * SEQ + 2 * SEQ].bitcast(F32)
    af = big2[:, DC * SEQ + 2 * SEQ:DC * SEQ + 4 * SEQ].bitcast(F32)
    yf = big2[:, DC * SEQ + 4 * SEQ:DC * SEQ + 6 * SEQ].bitcast(F32)
    of = big2[:, DC * SEQ + 6 * SEQ:DC * SEQ + 8 * SEQ].bitcast(F32)
    stg = bigC[:, 0:2 * SEQ].bitcast(F32)
    fmh = bigC[:, 0:D]
    fml = bigC[:, D:2 * D]
    tlo = bigC[:, 2 * D:3 * D]
    sscol = sb("sscol", [128, SEQT], F32)[:]
    smcol = sb("smcol", [128, SEQT], F32)[:]
    cnts = big1[0:1, 0:2 * E * RT].bitcast(F32)
    basesEM = big1[0:1, 2 * E * RT:4 * E * RT].bitcast(F32)
    basesTM = big1[0:1, 4 * E * RT:6 * E * RT].bitcast(F32)
    onesRT = big1[0:1, 6 * E * RT:6 * E * RT + 2 * RT].bitcast(F32)
    basesB = bigC[:, bb_off:bb_off + 2 * E * RT].bitcast(F32)

    xt = sb("xt", [128, D], F32)
    xt2 = sb("xt2", [128, D], F32)
    tf = sb("tf", [128, D], F32)
    thi = sb("thi", [128, D], BF16)
    rsc = sb("rsc", [128, 16 * E], F32)
    ohbf = sb("ohbf", [128, 2 * E], BF16)
    zz = sb("zz", [128, D], BF16)
    mark = sb("mark", [1, 1], F32)
    eo_g = sb("eo_g", [128, D], BF16)
    tt_g = sb("tt_g", [128, D], BF16)
    mucol = sb("mucol", [128, SEQT], F32)
    scal = sb("scal", [128, SEQT], F32)
    wtmp = sb("wtmp", [128, D], F32)
    wtmpb = sb("wtmpb", [128, D], BF16)
    disp_in_b = (xt[:, 0:D].bitcast(BF16), xt2[:, 0:D].bitcast(BF16))
    disp_fm_b = (tf[:, 0:D].bitcast(BF16), wtmp[:, 0:D].bitcast(BF16))
    eo_bf8 = sb("eo_bf8", [128, DC * 256], BF16)
    eo_stage = sb("eo_stage2", [128, 2 * D], BF16)
    sq_scr = tf
    dcomb = tf


    p_q0 = ps("p_q0", [128, 512], F32)
    p_q1 = ps("p_q1", [128, 512], F32)
    p_h0t = ps("p_h0t", [128, 256], F32)
    p_h1t = ps("p_h1t", [128, 256], F32)
    p_e0t = ps("p_e0t", [128, 256], F32)
    p_e1t = ps("p_e1t", [128, 256], F32)
    p_sm = ps("p_sm", [128, 512], F32)
    p_lgt = ps("p_lgt", [128, 512], F32)
    p_h0 = p_h0t[:, 0:256]
    p_h1 = p_h1t[:, 0:256]
    p_e0 = p_e0t[:, 0:256]
    p_e1 = p_e1t[:, 0:256]
    p_smb = p_sm[:, 0:512].bitcast(BF16)
    p_lgb = p_lgt[:, 0:512].bitcast(BF16)
    p_lgA = [p_lgt[:, 0:2 * E], p_lgt[:, 2 * E:4 * E]]
    p_lgB = [p_lgt[:, 4 * E:5 * E], p_lgt[:, 5 * E:6 * E]]
    p_cm = [p_lgt[:, 48:48 + E], p_lgt[:, 56:56 + E]]
    p_cs = [p_lgt[0:1, 64:64 + E], p_lgt[0:1, 72:72 + E]]

    op, wait, cnt = ctx.op, ctx.wait, ctx.cnt



    op("sync", lambda e: e.dma_start(out=identf[:], in_=identf_d), inc="ld", inc_k=16)
    op("sync", lambda e: e.dma_start(out=iota8[:], in_=iota8_d), inc="ld", inc_k=16)
    op("sync", lambda e: e.dma_start(out=gam[:], in_=gammab), inc="ld", inc_k=16)
    op("sync", lambda e: e.dma_start(
        out=wrf[:].rearrange("p (c e) -> p c e", e=E),
        in_=wr_in.rearrange("(c p) e -> p c e", p=128)),
        inc="ld", inc_k=16)
    op("gpsimd", lambda e: e.dma_start(out=identb[:], in_=identb_d), inc="gd", inc_k=16)
    op("gpsimd", lambda e: e.dma_start(out=Ub[:], in_=U_d), inc="gd", inc_k=16)
    op("gpsimd", lambda e: e.dma_start(out=onesb[:], in_=ones_d), inc="gd", inc_k=16)
    v_ld0, v_gd0 = cnt["ld"], cnt["gd"]
    op("vector", lambda e: e.tensor_copy(out=wrh[:], in_=wrf[:]),
       waits=[("ld", v_ld0), ("gd", v_gd0)])
    op("vector", lambda e: e.tensor_tensor(
        out=wrl[:], in0=wrf[:], in1=wrh[:], op=ALU.subtract))
    op("vector", lambda e: e.tensor_copy(
        out=wrhl[:].rearrange("p (c t e) -> p c t e", t=2, e=E)[:, :, 0, :],
        in_=wrh[:].rearrange("p (c e) -> p c e", e=E)))
    op("vector", lambda e: e.tensor_copy(
        out=wrhl[:].rearrange("p (c t e) -> p c t e", t=2, e=E)[:, :, 1, :],
        in_=wrl[:].rearrange("p (c e) -> p c e", e=E)))
    op("vector", lambda e: e.memset(zz[:], 0.0))
    op("vector", lambda e: e.memset(ones8f[:], 1.0))
    v_init_dv = op("vector", lambda e: e.memset(mark[:], 0.0), inc="dv")
    for r in range((E * CAP) // 128):
        op("gpsimd", lambda e, r=r: e.dma_start(
            out=disp[r * 128:(r + 1) * 128, :], in_=zz[:]),
            waits=[("dv", v_init_dv)] if r == 0 else [],
            inc="gd", inc_k=16)
    v_zero = cnt["gd"]

    wt_ev = 0
    for oc in range(C.OC):
        v = op("sync", lambda e, oc=oc: e.dma_start(
            out=wtmp[:], in_=wqkva[oc * 128:(oc + 1) * 128, :]),
            waits=[("pe", cnt["pe"])], inc="ld", inc_k=16)
        v_c = op("vector", lambda e: e.tensor_copy(out=wtmpb[:], in_=wtmp[:]),
                 waits=[("ld", v)], inc="dv")
        for dc in range(DC):
            op("tensor", lambda e, dc=dc: e.transpose(
                out=p_smb[:, dc * 128:(dc + 1) * 128],
                in_=wtmpb[:, dc * 128:(dc + 1) * 128], identity=identb[:]),
                waits=[("dv", v_c), ("dv", wt_ev)] if dc == 0 else [],
                inc="pe" if dc == DC - 1 else None)
        vpet = cnt["pe"]
        wt_ev = op("vector", lambda e, oc=oc: e.tensor_copy(
            out=wT[:, oc * DC * 128:(oc + 1) * DC * 128],
            in_=p_smb[:, 0:DC * 128]), waits=[("pe", vpet)], inc="dv")
    v_wT_dv = cnt["dv"]
    v_wT_st = op("sync", lambda e: e.dma_start(out=wT_d, in_=wT),
                 waits=[("dv", v_wT_dv)], inc="ld", inc_k=16)

    def routing_phase(xsrc, prefetch=None):
        xmark = {}
        for k in range(RT):
            x_t = xt if k % 2 == 0 else xt2
            w = [("ac", xmark[k - 2][0])] if k - 2 in xmark else []
            v = op("sync", lambda e, k=k, x_t=x_t: e.dma_start(
                out=x_t[:], in_=xsrc[k * 128:(k + 1) * 128, :]),
                waits=w, inc="ld", inc_k=16)
            va = op("scalar", lambda e, k=k, x_t=x_t: e.activation(
                out=sq_scr[:], in_=x_t[:], func=AF.Square,
                accum_out=statsB[:, k:k + 1]),
                waits=[("ld", v)], inc="ac")
            vd = op("scalar", lambda e, k=k, x_t=x_t: e.activation(
                out=sq_scr[:], in_=x_t[:], func=AF.Copy,
                accum_out=statsA[:, k:k + 1]),
                waits=[("ld", v)], inc="ac")
            xmark[k] = (vd, vd)
        v_stats_ac, v_stats_dv = cnt["ac"], cnt["dv"]
        op("vector", lambda e: e.tensor_scalar(
            out=meanS[:, 0:RT], in0=statsA[:, 0:RT], scalar1=1.0 / D,
            scalar2=None, op0=ALU.mult), waits=[("ac", v_stats_ac)])
        op("vector", lambda e: e.tensor_scalar(
            out=rstdS[:, 0:RT], in0=statsB[:, 0:RT], scalar1=1.0 / D,
            scalar2=None, op0=ALU.mult))
        op("vector", lambda e: e.tensor_tensor(
            out=posS[:, 0:RT], in0=meanS[:, 0:RT], in1=meanS[:, 0:RT],
            op=ALU.mult))
        op("vector", lambda e: e.tensor_tensor(
            out=rstdS[:, 0:RT], in0=rstdS[:, 0:RT], in1=posS[:, 0:RT],
            op=ALU.subtract))
        v1 = op("vector", lambda e: e.tensor_scalar(
            out=rstdS[:, 0:RT], in0=rstdS[:, 0:RT], scalar1=1e-5, scalar2=None,
            op0=ALU.add), inc="dv")
        va = op("scalar", lambda e: e.activation(
            out=rstdS[:, 0:RT], in_=rstdS[:, 0:RT], func=AF.Sqrt),
            waits=[("dv", v1)], inc="ac")
        op("vector", lambda e: e.reciprocal(
            out=rstdS[:, 0:RT], in_=rstdS[:, 0:RT]), waits=[("ac", va)])
        v_rstd = op("vector", lambda e: e.scalar_tensor_tensor(
            out=nmr[:, 0:RT], in0=meanS[:, 0:RT], scalar=-1.0,
            in1=rstdS[:, 0:RT], op0=ALU.mult, op1=ALU.mult), inc="dv")

        marks = {}
        for k in range(RT):
            p = k % 2
            x_t = xt if p == 0 else xt2
            pw = marks.get(k - 2, {})
            pm = marks.get(k - 1, {})
            v = op("sync", lambda e, k=k, x_t=x_t: e.dma_start(
                out=x_t[:], in_=xsrc[k * 128:(k + 1) * 128, :]),
                waits=[("gd", pw["tmp"])] if pw
                else [("ac", v_stats_ac)], inc="ld", inc_k=16)
            m = {}
            tfp = tf if p == 0 else wtmp
            m["tmp"] = op("gpsimd", lambda e, k=k, x_t=x_t, tfp=tfp:
                          e.tensor_scalar(
                out=tfp[:], in0=x_t[:], scalar1=rstdS[:, k:k + 1],
                scalar2=nmr[:, k:k + 1], op0=ALU.mult, op1=ALU.add),
                waits=[("ld", v), ("ac", pw.get("thi", 0)),
                       ("gd", pw.get("tlo", 0))] +
                      ([("dv", v_rstd)] if k < 2 else []),
                inc="gd", inc_k=1)
            m["thi"] = v_thi = op("scalar", lambda e, tfp=tfp: e.activation(
                out=thi[:], in_=tfp[:], func=AF.Copy),
                waits=[("gd", m["tmp"]), ("pe", pm.get("vpeh", 0)),
                       ("gd", max(pm.get("tstore", 0), pm.get("tlo", 0)))],
                inc="ac")
            m["tstore"] = op("gpsimd", lambda e, k=k: e.dma_start(
                out=t_d[k * 128:(k + 1) * 128, :], in_=thi[:]),
                waits=[("ac", v_thi)], inc="gd", inc_k=16)
            m["tlo"] = v_tlo = op("gpsimd", lambda e, tfp=tfp: e.tensor_tensor(
                out=tlo[:], in0=tfp[:], in1=thi[:], op=ALU.subtract),
                waits=[("ac", v_thi), ("pe", pm.get("vpel", 0))],
                inc="gd", inc_k=1)
            for dc in range(DC):
                op("tensor", lambda e, dc=dc: e.transpose(
                    out=p_smb[:, dc * 128:(dc + 1) * 128],
                    in_=thi[:, dc * 128:(dc + 1) * 128], identity=identb[:]),
                    waits=[("ac", v_thi),
                           ("dv", pm.get("fml", v_wT_dv if k == 0 else 0))]
                    if dc == 0 else [],
                    inc="pe" if dc == DC - 1 else None)
            m["vpeh"] = vpeh = cnt["pe"]
            v_fmh = op("scalar", lambda e: e.activation(
                out=fmh[:, 0:D], in_=p_smb[:, 0:D], func=AF.Copy),
                waits=[("pe", vpeh), ("pe", pm.get("lgmm", 0))], inc="ac")
            for dc in range(DC):
                op("tensor", lambda e, dc=dc: e.transpose(
                    out=p_smb[:, dc * 128:(dc + 1) * 128],
                    in_=tlo[:, dc * 128:(dc + 1) * 128], identity=identb[:]),
                    waits=[("gd", v_tlo), ("ac", v_fmh)] if dc == 0 else [],
                    inc="pe" if dc == DC - 1 else None)
            m["vpel"] = vpel = cnt["pe"]
            m["fml"] = v_fm = op("vector", lambda e: e.tensor_copy(
                out=fml[:, 0:D], in_=p_smb[:, 0:D]),
                waits=[("pe", vpel), ("pe", pm.get("lgmm", 0))], inc="dv")
            for dc in range(DC):
                op("tensor", lambda e, dc=dc, p=p: e.matmul(
                    out=p_lgA[p], lhsT=fmh[:, dc * 128:(dc + 1) * 128],
                    rhs=wrhl[:, dc * 2 * E:(dc + 1) * 2 * E],
                    start=(dc == 0), stop=(dc == DC - 1)),
                    waits=([("ac", v_fmh), ("dv", pw.get("lgread", 0))]
                           if dc == 0 else []),
                    inc="pe" if dc == DC - 1 else None)
            for dc in range(DC):
                op("tensor", lambda e, dc=dc, p=p: e.matmul(
                    out=p_lgB[p], lhsT=fml[:, dc * 128:(dc + 1) * 128],
                    rhs=wrh[:, dc * E:(dc + 1) * E],
                    start=(dc == 0), stop=(dc == DC - 1)),
                    waits=[("dv", v_fm)] if dc == 0 else [],
                    inc="pe" if dc == DC - 1 else None)
            m["lgmm"] = v_lgmm = cnt["pe"]
            rb = rsc[:, p * 8 * E:(p + 1) * 8 * E]
            lg = rb[:, 0:E]
            mx = rb[:, E:E + 1]
            s2 = rb[:, E + 1:E + 2]
            eq = rb[:, 2 * E:3 * E]
            pr = rb[:, 3 * E:4 * E]
            oh = rb[:, 4 * E:5 * E]
            ex = rb[:, 5 * E:6 * E]
            ohb = ohbf[:, p * E:(p + 1) * E]
            op("vector", lambda e, lg=lg, p=p: e.tensor_copy(
                out=lg, in_=p_lgA[p][:, 0:E]), waits=[("pe", v_lgmm)])
            op("vector", lambda e, lg=lg, p=p: e.tensor_tensor(
                out=lg, in0=lg, in1=p_lgA[p][:, E:2 * E], op=ALU.add))
            m["lgread"] = op("vector", lambda e, lg=lg, p=p: e.tensor_tensor(
                out=lg, in0=lg, in1=p_lgB[p], op=ALU.add), inc="dv")
            op("vector", lambda e, lg=lg, mx=mx: e.reduce_max(
                out=mx, in_=lg, axis=AX.X))
            op("vector", lambda e, lg=lg, mx=mx, eq=eq: e.tensor_scalar(
                out=eq, in0=lg, scalar1=mx, scalar2=None, op0=ALU.is_ge))
            op("vector", lambda e, eq=eq, pr=pr: e.tensor_tensor_scan(
                out=pr, data0=ones8f[:], data1=eq, initial=0.0,
                op0=ALU.mult, op1=ALU.add))
            op("vector", lambda e, pr=pr: e.tensor_scalar(
                out=pr, in0=pr, scalar1=1.0, scalar2=None, op0=ALU.is_le))
            op("vector", lambda e, eq=eq, pr=pr, oh=oh: e.tensor_tensor(
                out=oh, in0=eq, in1=pr, op=ALU.mult))
            op("vector", lambda e, oh=oh, ex=ex: e.tensor_tensor(
                out=ex, in0=oh, in1=iota8[:], op=ALU.mult))
            op("vector", lambda e, k=k, ex=ex: e.reduce_sum(
                out=idxS[:, k:k + 1], in_=ex, axis=AX.X))
            v2 = op("vector", lambda e, lg=lg, mx=mx, ex=ex: e.tensor_scalar(
                out=ex, in0=lg, scalar1=mx, scalar2=None, op0=ALU.subtract),
                inc="dv")
            va = op("scalar", lambda e, ex=ex: e.activation(
                out=ex, in_=ex, func=AF.Exp), waits=[("dv", v2)], inc="ac")
            op("vector", lambda e, ex=ex, s2=s2: e.reduce_sum(
                out=s2, in_=ex, axis=AX.X), waits=[("ac", va)])
            op("vector", lambda e, k=k, s2=s2: e.reciprocal(
                out=pmaxS[:, k:k + 1], in_=s2))
            v3 = op("vector", lambda e, oh=oh, ohb=ohb: e.tensor_copy(
                out=ohb, in_=oh), inc="dv")
            vcm = op("tensor", lambda e, p=p, ohb=ohb: e.matmul(
                out=p_cm[p], lhsT=Ub[:], rhs=ohb, start=True, stop=True),
                waits=[("dv", v3), ("dv", pw.get("cmread", 0)),
                       ("dv", pw.get("csread", 0))], inc="pe")
            vcs = op("tensor", lambda e, p=p, ohb=ohb: e.matmul(
                out=p_cs[p][:], lhsT=onesb[:], rhs=ohb, start=True, stop=True),
                inc="pe")
            m["csread"] = op("vector", lambda e, k=k, p=p: e.tensor_copy(
                out=cnts[0:1, k:k + (E - 1) * RT + 1:RT], in_=p_cs[p]),
                waits=[("pe", vcs)], inc="dv")
            m["cmread"] = op("vector", lambda e, ex=ex, p=p: e.tensor_copy(
                out=ex, in_=p_cm[p]), waits=[("pe", vcm)], inc="dv")
            op("vector", lambda e, ex=ex, oh=oh: e.tensor_tensor(
                out=ex, in0=ex, in1=oh, op=ALU.mult))
            m["tdone"] = op("vector", lambda e, k=k, ex=ex: e.reduce_sum(
                out=posS[:, k:k + 1], in_=ex, axis=AX.X), inc="dv")
            marks[k] = m
        v_passA = cnt["dv"]

        op("vector", lambda e: e.memset(onesRT[0:1, 0:RT], 1.0))
        for ee in range(E):
            op("vector", lambda e, ee=ee: e.tensor_tensor_scan(
                out=basesEM[0:1, ee * RT:(ee + 1) * RT],
                data0=onesRT[0:1, 0:RT],
                data1=cnts[0:1, ee * RT:(ee + 1) * RT],
                initial=0.0, op0=ALU.mult, op1=ALU.add),
                waits=[("dv", v_passA)] if ee == 0 else [])
        op("vector", lambda e: e.tensor_tensor(
            out=basesEM[0:1, :], in0=basesEM[0:1, :], in1=cnts[0:1, :],
            op=ALU.subtract))
        v_b = op("vector", lambda e: e.tensor_copy(
            out=basesTM[0:1, :].rearrange("p (k e) -> p k e", e=E),
            in_=basesEM[0:1, :].rearrange("p (e k) -> p k e", e=E)),
            inc="dv")
        vg1 = op("gpsimd", lambda e: e.dma_start(
            out=bases_d, in_=basesTM[0:1, :]),
            waits=[("dv", v_b)], inc="gd", inc_k=16)
        vg2 = op("gpsimd", lambda e: e.dma_start(
            out=basesB[:], in_=bases_d.to_broadcast([128, RT * E])),
            inc="gd", inc_k=16)
        if prefetch is not None:
            prefetch(vg2)

        oh2 = rsc[:, 0:E]
        bsl = rsc[:, E:2 * E]
        sc1 = rsc[:, 2 * E:2 * E + 1]
        sc2 = rsc[:, 2 * E + 1:2 * E + 2]
        op("vector", lambda e: e.tensor_scalar(
            out=flatI[:, 0:RT], in0=idxS[:, 0:RT], scalar1=float(CAP),
            scalar2=None, op0=ALU.mult),
            waits=[("dv", v_passA), ("gd", vg2)])
        tt_bufs = (tt_g, wtmpb)
        scat_gate = cnt["gd"]
        smark = {}
        for k in range(RT):
            p = k % 2
            op("vector", lambda e, k=k: e.tensor_scalar(
                out=oh2, in0=iota8[:], scalar1=idxS[:, k:k + 1],
                scalar2=None, op0=ALU.is_equal))
            op("vector", lambda e, k=k: e.tensor_tensor(
                out=bsl, in0=oh2, in1=basesB[:, k * E:(k + 1) * E],
                op=ALU.mult))
            op("vector", lambda e: e.reduce_sum(out=sc1, in_=bsl, axis=AX.X))
            op("vector", lambda e, k=k: e.scalar_tensor_tensor(
                out=posS[:, k:k + 1], in0=sc1, scalar=1.0,
                in1=posS[:, k:k + 1], op0=ALU.subtract, op1=ALU.add))
            op("vector", lambda e, k=k: e.tensor_scalar(
                out=keptS[:, k:k + 1], in0=posS[:, k:k + 1],
                scalar1=float(CAP), scalar2=None, op0=ALU.is_lt))
            op("vector", lambda e, k=k: e.scalar_tensor_tensor(
                out=flatI[:, k:k + 1], in0=posS[:, k:k + 1],
                scalar=float(CAP - 1), in1=flatI[:, k:k + 1],
                op0=ALU.min, op1=ALU.add))
            op("vector", lambda e, k=k: e.tensor_copy(
                out=flatIi[:, k:k + 1], in_=flatI[:, k:k + 1]))
            op("vector", lambda e, k=k: e.scalar_tensor_tensor(
                out=sc2, in0=flatI[:, k:k + 1], scalar=float(E * CAP),
                in1=keptS[:, k:k + 1], op0=ALU.subtract, op1=ALU.mult))
            op("vector", lambda e: e.tensor_scalar(
                out=sc1, in0=sc2, scalar1=float(E * CAP), scalar2=None,
                op0=ALU.add))
            v5 = op("vector", lambda e, k=k, p=p: e.tensor_copy(
                out=scatI[:, p:p + 1], in_=sc1),
                waits=[("gd", smark.get(k - 2, scat_gate))], inc="dv")
            vt = op("sync", lambda e, k=k, p=p: e.dma_start(
                out=tt_bufs[p][:], in_=t_d[k * 128:(k + 1) * 128, :]),
                waits=[("gd", smark.get(k - 2, scat_gate))] +
                      ([("dv", v_wT_dv)] if k < 2 else []),
                inc="ld", inc_k=16)
            smark[k] = op("gpsimd", lambda e, k=k, p=p: e.indirect_dma_start(
                out=disp,
                out_offset=IndirectOffsetOnAxis(scatI[:, p:p + 1], 0),
                in_=tt_bufs[p][:], in_offset=None),
                waits=[("dv", v5), ("ld", vt)], inc="gd", inc_k=16)
        op("vector", lambda e: e.tensor_scalar(
            out=sfacS[:, 0:RT], in0=pmaxS[:, 0:RT], scalar1=1.0, scalar2=None,
            op0=ALU.subtract))
        op("vector", lambda e: e.tensor_tensor(
            out=sfacS[:, 0:RT], in0=sfacS[:, 0:RT], in1=keptS[:, 0:RT],
            op=ALU.mult))
        op("vector", lambda e: e.tensor_scalar(
            out=sfacS[:, 0:RT], in0=sfacS[:, 0:RT], scalar1=1.0, scalar2=0.5,
            op0=ALU.add, op1=ALU.mult), inc="dv")

    def emit_w0_prefetch(w1_waits, w2_waits):
        FQ = (DFF // 128) // 4
        w1m, w2m = [], []
        for q in range(4):
            w1m.append(op("gpsimd", lambda e, q=q: e.dma_start(
                out=w1s[:, 0:DC * DFF].rearrange(
                    "p (c f) -> p c f", f=DFF)[:, :, q * FQ * 128:(q + 1) * FQ * 128],
                in_=w1_in[0].rearrange("(c p) f -> p c f", p=128)[
                    :, :, q * FQ * 128:(q + 1) * FQ * 128]),
                waits=w1_waits if q == 0 else [], inc="gd", inc_k=16))
        for q in range(4):
            w2m.append(op("gpsimd", lambda e, q=q: e.dma_start(
                out=w2s[:, q * (FC // 4) * D:(q + 1) * (FC // 4) * D],
                in_=w2_in[0].rearrange("(c p) f -> p c f", p=128)[
                    :, q * (FC // 4):(q + 1) * (FC // 4), :]),
                waits=w2_waits if q == 0 else [], inc="gd", inc_k=16))
        return (w1m, w2m)

    def expert_phase(pre=None):
        v_scat = cnt["gd"]
        tiles = [(ee, r0, rn) for ee in range(E) for (r0, rn) in chunks_of(CAP, 256)]
        import os
        _lim = int(os.environ.get("EXPERT_TILES", "0"))
        if _lim:
            tiles = tiles[:_lim]
        NTL = len(tiles)
        per_e = max(1, len(chunks_of(CAP, 256)))

        w1q = [[None] * 4 for _ in range(E)]
        w2q = [[None] * 4 for _ in range(E)]
        m_hlast = {}
        m_elast = {}
        FQ = FC // 4
        DQ = DC * FQ * 128

        def emit_w_loads(ee, pe_gate_h, pe_gate_e):
            for q in range(4):
                w1q[ee][q] = op("gpsimd", lambda e, ee=ee, q=q: e.dma_start(
                    out=w1s[:, 0:DC * DFF].rearrange(
                        "p (c f) -> p c f", f=DFF)[:, :, q * FQ * 128:(q + 1) * FQ * 128],
                    in_=w1_in[ee].rearrange("(c p) f -> p c f", p=128)[
                        :, :, q * FQ * 128:(q + 1) * FQ * 128]),
                    waits=[("pe", pe_gate_h)] if q == 0 else [],
                    inc="gd", inc_k=16)
            for q in range(4):
                w2q[ee][q] = op("gpsimd", lambda e, ee=ee, q=q: e.dma_start(
                    out=w2s[:, q * FQ * 8 * D // 8:0] if False else
                        w2s[:, q * FQ * D * 1:0] if False else
                        w2s[:, q * (FC // 4) * D:(q + 1) * (FC // 4) * D],
                    in_=w2_in[ee].rearrange("(c p) f -> p c f", p=128)[
                        :, q * (FC // 4):(q + 1) * (FC // 4), :]),
                    waits=[("pe", pe_gate_e)] if q == 0 else [],
                    inc="gd", inc_k=16)

        if pre is not None:
            w1q[0], w2q[0] = pre
        else:
            emit_w_loads(0, cnt["pe"], cnt["pe"])

        m_ld = {}
        m_dtrans = {}
        m_fmevac = {}
        m_eotrans = {}
        m_stgevA = {}
        m_stgev = {}
        m_store = {}
        m_relu = {}
        m_ev8 = {}

        def tinfo(T):
            ee, r0, rn = tiles[T]
            return ee, ee * CAP + r0, rn, chunks_of(rn, 128)

        def emit_disp_load(T):
            ee, base, rn, tcs = tinfo(T)
            w0 = []
            if T == 0:
                w0 = [("gd", v_scat), ("pe", cnt["pe"]),
                      ("dv", cnt["dv"]), ("ac", cnt["ac"])]
            elif T - 2 in m_dtrans:
                w0 = [("pe", m_dtrans[T - 2])]
            v = None
            for ti, (t0, tn) in enumerate(tcs):
                v = op("sync", lambda e, base=base, t0=t0, tn=tn, ti=ti, T=T:
                       e.dma_start(
                    out=disp_in_b[T % 2][0:tn, ti * D:(ti + 1) * D],
                    in_=disp[base + t0:base + t0 + tn, :]),
                    waits=w0 if ti == 0 else [], inc="ld", inc_k=16)
            m_ld[T] = v

        def emit_disp_trans(T):
            ee, base, rn, tcs = tinfo(T)
            p = T % 2
            for ti, (t0, tn) in enumerate(tcs):
                if ti == 0:
                    w0 = [("ld", m_ld[T]), ("dv", m_stgevA.get(T - 3, 0))]
                else:
                    w0 = [("dv", m_fmevac[T])]
                for dc in range(DC):
                    op("tensor", lambda e, ti=ti, dc=dc, tn=tn, p=p: e.transpose(
                        out=p_smb[0:128, dc * 128:dc * 128 + tn],
                        in_=disp_in_b[p][0:tn, ti * D + dc * 128:ti * D + (dc + 1) * 128],
                        identity=identb[:]),
                        waits=w0 if dc == 0 else [],
                        inc="pe" if dc == DC - 1 else None)
                vpet = cnt["pe"]
                m_fmevac[T] = op("vector", lambda e, ti=ti, tn=tn, rn=rn, p=p:
                                 e.tensor_copy(
                    out=disp_fm_b[p][:, 0:DC * rn].rearrange(
                        "p (dc f) -> p dc f", f=rn)[:, :, ti * 128:ti * 128 + tn],
                    in_=p_smb[:, 0:DC * 128].rearrange(
                        "p (dc f) -> p dc f", f=128)[:, :, 0:tn]),
                    waits=[("pe", vpet), ("pe", m_hlast.get(("t", T - 2), 0))],
                    inc="dv")
            m_dtrans[T] = cnt["pe"]

        def emit_h(T):
            ee, base, rn, tcs = tinfo(T)
            p = T % 2
            first_of_e = (T % per_e == 0)
            marks = []
            for ff in range(FC):
                pb = p_h0 if ff % 2 == 0 else p_h1
                w0 = []
                if ff == 0:
                    w0 = [("dv", m_fmevac[T]), ("ac", m_relu.get(T - 1, [0])[-1])]
                elif ff >= 2:
                    w0 = [("ac", marks[ff - 2])]
                if first_of_e and ff % FQ == 0:
                    w0 = w0 + [("gd", w1q[ee][ff // FQ])]
                for dc in range(DC):
                    op("tensor", lambda e, ff=ff, dc=dc, rn=rn, pb=pb, p=p: e.matmul(
                        out=pb[:, 0:rn],
                        lhsT=w1s[:, dc * DFF + ff * 128:dc * DFF + (ff + 1) * 128],
                        rhs=disp_fm_b[p][:, dc * rn:(dc + 1) * rn],
                        start=(dc == 0), stop=(dc == DC - 1)),
                        waits=w0 if dc == 0 else [],
                        inc="pe" if dc == DC - 1 else None)
                v_mm = cnt["pe"]
                marks.append(op("scalar", lambda e, ff=ff, rn=rn, pb=pb:
                                e.activation(
                    out=h_sb[:, ff * rn:(ff + 1) * rn], in_=pb[:, 0:rn],
                    func=AF.Relu), waits=[("pe", v_mm)], inc="ac"))
            m_relu[T] = marks
            m_hlast[("t", T)] = cnt["pe"]
            if (T + 1) % per_e == 0:
                m_hlast[ee] = cnt["pe"]
                if ee + 1 < E:
                    emit_w1_next(ee + 1)

        def emit_w1_next(ee):
            for q in range(4):
                w1q[ee][q] = op("gpsimd", lambda e, ee=ee, q=q: e.dma_start(
                    out=w1s[:, 0:DC * DFF].rearrange(
                        "p (c f) -> p c f", f=DFF)[:, :, q * FQ * 128:(q + 1) * FQ * 128],
                    in_=w1_in[ee].rearrange("(c p) f -> p c f", p=128)[
                        :, :, q * FQ * 128:(q + 1) * FQ * 128]),
                    waits=[("pe", m_hlast[ee - 1])] if q == 0 else [],
                    inc="gd", inc_k=16)

        def emit_w2_next(ee):
            for q in range(4):
                w2q[ee][q] = op("gpsimd", lambda e, ee=ee, q=q: e.dma_start(
                    out=w2s[:, q * (FC // 4) * D:(q + 1) * (FC // 4) * D],
                    in_=w2_in[ee].rearrange("(c p) f -> p c f", p=128)[
                        :, q * (FC // 4):(q + 1) * (FC // 4), :]),
                    waits=[("pe", m_elast[ee - 1])] if q == 0 else [],
                    inc="gd", inc_k=16)

        def emit_eout(T):
            ee, base, rn, tcs = tinfo(T)
            ev = []
            for dco in range(DC):
                pb = p_e0 if dco % 2 == 0 else p_e1
                w0 = []
                if dco == 0:
                    w0 = [("ac", m_relu[T][-1]), ("gd", w2q[ee][3]),
                          ("dv", m_ev8.get(T - 1, [0])[-1])]
                elif dco >= 2:
                    w0 = [("dv", ev[dco - 2])]
                for ff in range(FC):
                    op("tensor", lambda e, dco=dco, ff=ff, rn=rn, pb=pb: e.matmul(
                        out=pb[:, 0:rn],
                        lhsT=w2s[:, ff * D + dco * 128:ff * D + (dco + 1) * 128],
                        rhs=h_sb[:, ff * rn:(ff + 1) * rn],
                        start=(ff == 0), stop=(ff == FC - 1)),
                        waits=w0 if ff == 0 else [],
                        inc="pe" if ff == FC - 1 else None)
                v_mm = cnt["pe"]
                ev.append(op("vector", lambda e, rn=rn, pb=pb, dco=dco: e.tensor_copy(
                    out=eo_bf8[:, dco * 256:dco * 256 + rn], in_=pb[:, 0:rn]),
                    waits=[("pe", v_mm), ("pe", m_eotrans.get(T - 1, 0))],
                    inc="dv"))
            m_ev8[T] = ev
            m_elast[("t", T)] = cnt["pe"]
            if (T + 1) % per_e == 0:
                m_elast[ee] = cnt["pe"]
                if ee + 1 < E:
                    emit_w2_next(ee + 1)

        def emit_eout_trans(T):
            ee, base, rn, tcs = tinfo(T)
            ntc = len(tcs)
            half = DC // 2
            for b in range(2):
                bankv = p_smb if b == 0 else p_lgb
                dcos = list(range(b * half, (b + 1) * half))
                n_tr = len(dcos) * ntc
                i_tr = 0
                for dco in dcos:
                    for ti, (t0, tn) in enumerate(tcs):
                        w0 = []
                        if i_tr == 0:
                            w0 = [("dv", m_ev8[T][-1])]
                            if b == 0:
                                w0.append(("dv", m_fmevac.get(T + 2, 0)))
                        op("tensor", lambda e, dco=dco, ti=ti, tn=tn, bankv=bankv,
                           b=b, half=half: e.transpose(
                            out=bankv[0:tn, ((dco - b * half) * 2 + ti) * 128:
                                      ((dco - b * half) * 2 + ti) * 128 + 128],
                            in_=eo_bf8[:, dco * 256 + ti * 128:dco * 256 + ti * 128 + tn],
                            identity=identb[:]),
                            waits=w0,
                            inc="pe" if i_tr == n_tr - 1 else None)
                        i_tr += 1
                vpet = cnt["pe"]
                rm = min(rn, 128)
                bankv_cur = bankv
                m_stgev[T] = op("vector", lambda e, b=b, half=half, ntc=ntc,
                                rm=rm, bankv=bankv_cur: e.tensor_copy(
                    out=eo_stage[0:rm, 0:ntc * D].rearrange(
                        "p (ti dco j) -> p ti dco j", dco=DC, j=128)[
                        :, :, b * half:(b + 1) * half, :],
                    in_=bankv[0:rm, 0:half * ntc * 128].rearrange(
                        "p (dco ti j) -> p ti dco j", ti=ntc, j=128)),
                    waits=[("pe", vpet), ("ld", m_store.get(T - 1, 0))],
                    inc="dv")
                if b == 0:
                    m_stgevA[T] = m_stgev[T]
            m_eotrans[T] = cnt["pe"]
            v = None
            for ti, (t0, tn) in enumerate(tcs):
                v = op("sync", lambda e, base=base, t0=t0, tn=tn, ti=ti:
                       e.dma_start(
                    out=eout_d[base + t0:base + t0 + tn, :],
                    in_=eo_stage[0:tn, ti * D:(ti + 1) * D]),
                    waits=[("dv", m_stgev[T])] if ti == 0 else [],
                    inc="ld", inc_k=16)
            m_store[T] = v

        emit_disp_load(0)
        emit_disp_trans(0)
        if NTL > 1:
            emit_disp_load(1)
        for T in range(NTL):
            emit_h(T)
            if T + 1 < NTL:
                emit_disp_trans(T + 1)
            if T + 2 < NTL:
                emit_disp_load(T + 2)
            if T - 1 >= 0:
                emit_eout_trans(T - 1)
            emit_eout(T)
        emit_eout_trans(NTL - 1)

    cb_eo = (eo_g, thi,
             bigC[:, 4096:4096 + D], bigC[:, 4096 + D:4096 + 2 * D])
    cb_tw = (tt_g, wtmpb,
             bigC[:, 4096 + 2 * D:4096 + 3 * D],
             bigC[:, 4096 + 3 * D:4096 + 4 * D])
    cb_x = (xt, xt2,
            big1[:, 26624:26624 + 2048].bitcast(F32),
            big1[:, 26624 + 2048:26624 + 4096].bitcast(F32))
    cb_o = (tf, wtmp,
            eo_bf8[:, 0:2048].bitcast(F32),
            eo_stage[:, 0:2048].bitcast(F32))

    def combine_tiles(xsrc, dst, tiles, cm, v_eo, cycle=(0, 1, 2, 3)):
        for k in tiles:
            p = cycle[cm.get("_i", 0) % len(cycle)]
            cm["_i"] = cm.get("_i", 0) + 1
            pw = cm.get(("p", p))
            vg = op("gpsimd", lambda e, k=k, p=p: e.indirect_dma_start(
                out=cb_eo[p][:, 0:D], out_offset=None,
                in_=eout_d,
                in_offset=IndirectOffsetOnAxis(flatIi[:, k:k + 1], 0)),
                waits=[("ld", v_eo), ("dv", pw[0] if pw else 0)],
                inc="gd", inc_k=16)
            vx = op("sync", lambda e, k=k, p=p: e.dma_start(
                out=cb_x[p][:], in_=xsrc[k * 128:(k + 1) * 128, :]),
                waits=([("dv", pw[1])] if pw else [("ld", v_eo)]),
                inc="ld", inc_k=16)
            vtw = op("gpsimd", lambda e, k=k, p=p: e.tensor_scalar(
                out=cb_tw[p][:], in0=cb_x[p][:], scalar1=rstdS[:, k:k + 1],
                scalar2=nmr[:, k:k + 1], op0=ALU.mult, op1=ALU.add),
                waits=[("ld", vx)] + ([("dv", pw[1])] if pw else []),
                inc="gd", inc_k=1)
            op("vector", lambda e, p=p: e.tensor_tensor(
                out=cb_o[p][:], in0=cb_eo[p][:, 0:D], in1=cb_tw[p][:],
                op=ALU.subtract),
                waits=[("gd", vg), ("gd", vtw), ("ld", pw[2] if pw else 0)])
            op("vector", lambda e, k=k, p=p: e.scalar_tensor_tensor(
                out=cb_o[p][:], in0=cb_o[p][:], scalar=keptS[:, k:k + 1],
                in1=cb_tw[p][:], op0=ALU.mult, op1=ALU.add))
            v6 = op("vector", lambda e, k=k, p=p: e.scalar_tensor_tensor(
                out=cb_o[p][:], in0=cb_o[p][:], scalar=sfacS[:, k:k + 1],
                in1=cb_x[p][:], op0=ALU.mult, op1=ALU.add), inc="dv")
            vs = op("sync", lambda e, k=k, p=p, dst=dst: e.dma_start(
                out=dst[k * 128:(k + 1) * 128, :], in_=cb_o[p][:]),
                waits=[("dv", v6)], inc="ld", inc_k=16)
            cm[k] = cm[("p", p)] = (v6, v6, vs)

    def stats_tile(s, k, st, cm):
        srow = s * SEQ
        amark = st.setdefault(("am", s), {})
        x_t = xt if k % 2 == 0 else xt2
        w0 = [("ld", cm[s * SEQT + k][2])]
        w0 += ([("dv", st["v8"])] if k < 2 else
               [("ac", amark[k - 2][0]), ("dv", amark[k - 2][1])])
        v = op("sync", lambda e, k=k, srow=srow, x_t=x_t: e.dma_start(
            out=x_t[:], in_=xcur[srow + k * 128:srow + (k + 1) * 128, :]),
            waits=w0, inc="ld", inc_k=16)
        va = op("scalar", lambda e, k=k, x_t=x_t: e.activation(
            out=sq_scr[:], in_=x_t[:], func=AF.Square,
            accum_out=sscol[:, k:k + 1]),
            waits=[("ld", v)] + ([("dv", st["v7"])] if k == 0 else []),
            inc="ac")
        vd = op("vector", lambda e, k=k, x_t=x_t: e.reduce_sum(
            out=smcol[:, k:k + 1], in_=x_t[:], axis=AX.X),
            waits=[("ld", v)], inc="dv")
        st["v_sm"] = 0
        amark[k] = (va, vd)

    def attention_seq(s, st, cm):
        frs = chunks_of(SEQ, 512)
        srow = s * SEQ
        if True:
            v_ss = cnt["ac"]
            op("vector", lambda e: e.tensor_scalar(
                out=mucol[:, 0:SEQT], in0=smcol[:, 0:SEQT], scalar1=1.0 / D,
                scalar2=None, op0=ALU.mult),
                waits=[("ac", v_ss), ("gd", st["v_sm"])])
            op("vector", lambda e: e.tensor_scalar(
                out=scal[:, 0:SEQT], in0=sscol[:, 0:SEQT], scalar1=1.0 / D,
                scalar2=None, op0=ALU.mult))
            op("vector", lambda e: e.tensor_tensor(
                out=smcol[:, 0:SEQT], in0=mucol[:, 0:SEQT],
                in1=mucol[:, 0:SEQT], op=ALU.mult))
            v1a = op("vector", lambda e: e.tensor_tensor(
                out=scal[:, 0:SEQT], in0=scal[:, 0:SEQT], in1=smcol[:, 0:SEQT],
                op=ALU.subtract), inc="dv")
            v1b = op("vector", lambda e: e.tensor_scalar(
                out=smcol[:, 0:SEQT], in0=scal[:, 0:SEQT], scalar1=1e-5,
                scalar2=None, op0=ALU.add), inc="dv")
            va = op("scalar", lambda e: e.activation(
                out=smcol[:, 0:SEQT], in_=smcol[:, 0:SEQT], func=AF.Sqrt),
                waits=[("dv", v1b)], inc="ac")
            op("vector", lambda e: e.reciprocal(
                out=smcol[:, 0:SEQT], in_=smcol[:, 0:SEQT]), waits=[("ac", va)])
            op("vector", lambda e: e.tensor_tensor(
                out=scal[:, 0:SEQT], in0=scal[:, 0:SEQT], in1=smcol[:, 0:SEQT],
                op=ALU.mult))
            op("vector", lambda e: e.tensor_tensor(
                out=scal[:, 0:SEQT], in0=scal[:, 0:SEQT], in1=smcol[:, 0:SEQT],
                op=ALU.mult))
            v1c = op("vector", lambda e: e.tensor_scalar(
                out=scal[:, 0:SEQT], in0=scal[:, 0:SEQT], scalar1=float(D),
                scalar2=None, op0=ALU.mult), inc="dv")
            va = op("scalar", lambda e: e.activation(
                out=scal[:, 0:SEQT], in_=scal[:, 0:SEQT], func=AF.Sqrt),
                waits=[("dv", v1c)], inc="ac")
            op("vector", lambda e: e.tensor_scalar(
                out=scal[:, 0:SEQT], in0=scal[:, 0:SEQT], scalar1=1e-12,
                scalar2=None, op0=ALU.max), waits=[("ac", va)])
            op("vector", lambda e: e.reciprocal(
                out=scal[:, 0:SEQT], in_=scal[:, 0:SEQT]))
            op("vector", lambda e: e.tensor_scalar(
                out=scal[:, 0:SEQT], in0=scal[:, 0:SEQT],
                scalar1=float(math.sqrt(D)), scalar2=None, op0=ALU.mult))
            v7 = op("vector", lambda e: e.tensor_tensor(
                out=scal[:, 0:SEQT], in0=scal[:, 0:SEQT], in1=smcol[:, 0:SEQT],
                op=ALU.mult), inc="dv")
            st["v7"] = v7
            xn_ev = [st["stg"]]
            xmark2 = {}
            for k in range(SEQT):
                x_t = xt if k % 2 == 0 else xt2
                w0 = ([("dv", v7)] if k == 0 else
                      ([("dv", xmark2[k - 2])] if k - 2 in xmark2 else []))
                v = op("sync", lambda e, k=k, srow=srow, x_t=x_t: e.dma_start(
                    out=x_t[:], in_=xcur[srow + k * 128:srow + (k + 1) * 128, :]),
                    waits=w0, inc="ld", inc_k=16)
                op("vector", lambda e, k=k, x_t=x_t: e.tensor_scalar(
                    out=tf[:], in0=x_t[:], scalar1=mucol[:, k:k + 1],
                    scalar2=scal[:, k:k + 1], op0=ALU.subtract, op1=ALU.mult),
                    waits=[("ld", v), ("dv", v7)])
                op("vector", lambda e: e.tensor_tensor(
                    out=tf[:], in0=tf[:], in1=gam[:], op=ALU.mult))
                v8 = op("vector", lambda e: e.tensor_copy(out=thi[:], in_=tf[:]),
                        inc="dv")
                xmark2[k] = v8
                for dc in range(DC):
                    op("tensor", lambda e, dc=dc: e.transpose(
                        out=p_smb[:, dc * 128:(dc + 1) * 128],
                        in_=thi[:, dc * 128:(dc + 1) * 128], identity=identb[:]),
                        waits=[("dv", v8), xn_ev[0]] if dc == 0 else [],
                        inc="pe" if dc == DC - 1 else None)
                vpet = cnt["pe"]
                xn_ev[0] = ("dv", op("vector", lambda e, k=k: e.tensor_copy(
                    out=xn_fm[:, 0:DC * SEQ].rearrange(
                        "p (dc s) -> p dc s", s=SEQ)[:, :, k * 128:(k + 1) * 128],
                    in_=p_smb[:, 0:DC * 128].rearrange(
                        "p (dc f) -> p dc f", f=128)),
                    waits=[("pe", vpet)], inc="dv"))
            st["v8"] = xmark2[SEQT - 1]
            st["xnpe"] = cnt["pe"]
            v_xn = xn_ev[0]
            for g in range(C.GC):
                def qkva(oc, dest, first_w):
                    evs = []
                    for fi, (f0, fn_) in enumerate(frs):
                        tick = st["tick"]
                        st["tick"] = tick + 1
                        pb = p_q0 if tick % 2 == 0 else p_q1
                        w0 = [st["bank"][tick - 2]] if tick - 2 in st["bank"] \
                            else []
                        if fi == 0 and g == 0:
                            w0 = w0 + [v_xn]
                        for dc in range(DC):
                            op("tensor", lambda e, oc=oc, dc=dc, f0=f0, fn_=fn_,
                               pb=pb: e.matmul(
                                out=pb[:, 0:fn_],
                                lhsT=wT[:, (oc * DC + dc) * 128:(oc * DC + dc + 1) * 128],
                                rhs=xn_fm[:, dc * SEQ + f0:dc * SEQ + f0 + fn_],
                                start=(dc == 0), stop=(dc == DC - 1)),
                                waits=w0 if dc == 0 else [],
                                inc="pe" if dc == DC - 1 else None)
                        v_mm = cnt["pe"]
                        ev = dest(fi, f0, fn_, pb, v_mm,
                                  first_w if fi == 0 else [])
                        st["bank"][tick] = ev
                        evs.append(ev)
                    return evs

                def kv_ev(fi, f0, fn_, pb, v_mm, fw):
                    return ("ac", op("scalar", lambda e, f0=f0, fn_=fn_, pb=pb:
                                     e.activation(
                        out=kvf[:, f0:f0 + fn_], in_=pb[:, 0:fn_],
                        func=AF.Copy), waits=[("pe", v_mm)] + fw, inc="ac"))

                def a_ev(fi, f0, fn_, pb, v_mm, fw):
                    return ("ac", op("scalar", lambda e, f0=f0, fn_=fn_, pb=pb:
                                     e.activation(
                        out=af[:, f0:f0 + fn_], in_=pb[:, 0:fn_],
                        func=AF.Sigmoid), waits=[("pe", v_mm)] + fw, inc="ac"))

                def q_ev(fi, f0, fn_, pb, v_mm, fw):
                    return ("ac", op("scalar", lambda e, f0=f0, fn_=fn_, pb=pb:
                                     e.activation(
                        out=qb[:, f0:f0 + fn_], in_=pb[:, 0:fn_],
                        func=AF.Copy), waits=[("pe", v_mm)] + fw, inc="ac"))

                qkva(C.GC + g, kv_ev, [("dv", st["v9g"])])
                qkva(2 * C.GC + g, a_ev, [])
                v_a = cnt["ac"]
                op("vector", lambda e: e.tensor_tensor_scan(
                    out=yf[:, 0:SEQ], data0=af[:, 0:SEQ], data1=kvf[:, 0:SEQ],
                    initial=0.0, op0=ALU.mult, op1=ALU.add),
                    waits=[("ac", v_a)])
                qkva(g, q_ev, [])
                v_q = cnt["ac"]
                v9 = op("vector", lambda e: e.tensor_tensor(
                    out=of[:, 0:SEQ], in0=qb[:, 0:SEQ], in1=yf[:, 0:SEQ],
                    op=ALU.mult),
                    waits=[("ac", v_q), ("pe", st["oevg"])], inc="dv")
                st["v9g"] = v9
                o_ev = []
                for k in range(SEQT):
                    bk = p_sm[:, 0:128] if k % 2 == 0 else p_lgt[:, 0:128]
                    w0 = []
                    if k == 0:
                        w0 = [("dv", v9)]
                    elif k >= 2:
                        w0 = [("dv", o_ev[k - 2])]
                    vpe = op("tensor", lambda e, k=k, bk=bk: e.transpose(
                        out=bk, in_=of[:, k * 128:(k + 1) * 128],
                        identity=identf[:]), waits=w0, inc="pe")
                    o_ev.append(op("vector", lambda e, k=k, bk=bk: e.tensor_copy(
                        out=stg[:, k * 128:(k + 1) * 128], in_=bk),
                        waits=[("pe", vpe)] +
                              ([("gd", st["accum"])] if k == 0 else []),
                        inc="dv"))
                st["oevg"] = cnt["pe"]
                st["stg"] = ("dv", o_ev[-1])
                v10 = o_ev[-1]
                st["accum"] = op("gpsimd", lambda e, g=g, srow=srow: e.dma_start(
                    out=xcur[srow:srow + SEQ, g * 128:(g + 1) * 128].rearrange(
                        "(c p) f -> p c f", p=128),
                    in_=stg[:, 0:SEQ].rearrange("p (c f) -> p c f", f=128),
                    accum_op=ALU.add),
                    waits=[("dv", v10)], inc="gd", inc_k=16)
                st["inter"](s, g)

    def middle_phase(combine_only=False, no_inter=False, no_inter_stats=False):
        v_eo = cnt["ld"]
        wait("sync", "gd", cnt["gd"])
        v_wt = op("sync", lambda e: e.dma_start(out=wT, in_=wT_d),
                  waits=[("pe", cnt["pe"]), ("dv", cnt["dv"]),
                         ("gd", cnt["gd"])], inc="ld", inc_k=16)
        wait("tensor", "ld", v_wt)
        cm = {}
        st = {"v9g": 0, "oevg": 0, "accum": 0, "stg": ("dv", 0),
              "v8": 0, "v7": 0, "sc9": 0, "v_sm": 0, "bank": {}, "tick": 0}

        def inter(s, g):
            if no_inter:
                return
            s1 = s + 1
            if s1 >= C.NSEQ:
                return
            combine_tiles(xin, xcur,
                          range(s1 * SEQT + 2 * g, s1 * SEQT + 2 * g + 2),
                          cm, v_eo, cycle=(2, 3))
            if g >= 1 and not no_inter_stats:
                stats_tile(s1, 2 * g - 2, st, cm)
                stats_tile(s1, 2 * g - 1, st, cm)

        st["inter"] = inter
        if combine_only:
            combine_tiles(xin, xcur, range(RT), cm, v_eo, cycle=(2, 3))
            return
        combine_tiles(xin, xcur, range(SEQT), cm, v_eo, cycle=(2, 3))
        for k in range(SEQT):
            stats_tile(0, k, st, cm)
        for s in range(C.NSEQ):
            attention_seq(s, st, cm)
            if s + 1 < C.NSEQ:
                if no_inter:
                    combine_tiles(xin, xcur,
                                  range((s + 1) * SEQT, (s + 2) * SEQT),
                                  cm, v_eo, cycle=(2, 3))
                if no_inter or no_inter_stats:
                    for k in range(SEQT - 2):
                        stats_tile(s + 1, k, st, cm)
                stats_tile(s + 1, SEQT - 2, st, cm)
                stats_tile(s + 1, SEQT - 1, st, cm)

    pre_box = {}
    if "r1" in phases:
        routing_phase(xin, prefetch=lambda vg2: pre_box.update(
            m=emit_w0_prefetch([("gd", vg2), ("ld", v_wT_st)], [("gd", vg2)])))
    if "e1" in phases:
        expert_phase(pre=pre_box.pop("m", None))
    if "m" in phases:
        middle_phase()
    if "mq" in phases:
        middle_phase(combine_only=True)
    if "mn" in phases:
        middle_phase(no_inter=True)
    if "ms" in phases:
        middle_phase(no_inter_stats=True)
    if "r2" in phases:
        wait("sync", "gd", cnt["gd"])
        wait("sync", "ld", cnt["ld"])
        wait("gpsimd", "gd", cnt["gd"])
        wait("gpsimd", "ld", cnt["ld"])
        pe_r2 = cnt["pe"]
        routing_phase(xcur, prefetch=lambda vg2: pre_box.update(
            m=emit_w0_prefetch([("gd", vg2)], [("gd", vg2), ("pe", pe_r2)])))
    if "e2" in phases:
        expert_phase(pre=pre_box.pop("m", None))
    if "c2" in phases:
        wait("sync", "gd", cnt["gd"])
        combine_tiles(xcur, yout, range(RT), {}, cnt["ld"])
    if "dx" in phases:
        wait("gpsimd", "ld", cnt["ld"])
        wait("gpsimd", "gd", cnt["gd"])
        op("gpsimd", lambda e: e.dma_start(out=yout, in_=xcur),
           inc="gd", inc_k=16)
    if "dt" in phases:
        wait("gpsimd", "ld", cnt["ld"])
        wait("gpsimd", "gd", cnt["gd"])
        op("gpsimd", lambda e: e.dma_start(out=yout, in_=t_d),
           inc="gd", inc_k=16)
    if "de" in phases:
        wait("gpsimd", "ld", cnt["ld"])
        wait("gpsimd", "gd", cnt["gd"])
        op("gpsimd", lambda e: e.dma_start(
            out=yout[0:min(NT, E * CAP), :], in_=eout_d[0:min(NT, E * CAP), :]),
           inc="gd", inc_k=16)

    wait("gpsimd", "gd", cnt["gd"])
    wait("gpsimd", "ld", cnt["ld"])
    wait("sync", "ld", cnt["ld"])
    wait("sync", "gd", cnt["gd"])

    ctx.run()
    es.close()
    return nc


def _layout_inputs(C, x, wq, gamma, wr, w1, w2):
    gam_b = np.ascontiguousarray(
        np.tile(np.asarray(gamma, np.float32).reshape(1, C.D), (128, 1)))
    return dict(
        xin=np.ascontiguousarray(np.asarray(x, np.float32).reshape(C.NT, C.D)),
        wqkva=np.ascontiguousarray(np.asarray(wq, np.float32)),
        gammab=gam_b,
        wr_in=np.ascontiguousarray(np.asarray(wr, np.float32)),
        w1_in=np.ascontiguousarray(np.asarray(w1, np.float32)),
        w2_in=np.ascontiguousarray(np.asarray(w2, np.float32)),
    )


def kernel(L0_attn_x, L0_attn_W_qkva, L0_attn_gamma, L0_ffn_Wr, L0_ffn_W1,
           L0_ffn_W2):
    C = REAL
    nc = bass.Bass("TRN2", target_bir_lowering=False)
    build_program(nc, C)
    in_map = _layout_inputs(C, L0_attn_x, L0_attn_W_qkva, L0_attn_gamma,
                            L0_ffn_Wr, L0_ffn_W1, L0_ffn_W2)
    res = run_bass_kernel_spmd(nc, [in_map], core_ids=[0])
    return res.results[0]["yout"].reshape(16, 2048, 1024)


# version 12
# speedup vs baseline: 1.0144x; 1.0034x over previous
import math
import numpy as np
import concourse.bass as bass
import concourse.mybir as mybir
from concourse.bass import IndirectOffsetOnAxis
from concourse.bass_utils import run_bass_kernel_spmd

F32 = mybir.dt.float32
BF16 = mybir.dt.bfloat16
I32 = mybir.dt.int32
AF = mybir.ActivationFunctionType
ALU = mybir.AluOpType
AX = mybir.AxisListType

BIG = 1 << 20


class CFG:
    def __init__(self, ntok, d, dff, e, seq, cap):
        self.NT, self.D, self.DFF, self.E, self.SEQ, self.CAP = \
            ntok, d, dff, e, seq, cap
        self.NSEQ = ntok // seq
        self.DC = d // 128
        self.FC = dff // 128
        self.RT = ntok // 128
        self.OC = 3 * d // 128
        self.GC = d // 128
        assert ntok % 128 == 0 and d % 128 == 0 and dff % 128 == 0
        assert seq % 128 == 0 and ntok % seq == 0 and cap % 128 == 0


REAL = CFG(ntok=32768, d=1024, dff=4096, e=8, seq=2048, cap=5120)


def chunks_of(total, size):
    out, o = [], 0
    while o < total:
        out.append((o, min(size, total - o)))
        o += size
    return out


class Ctx:
    ENGINES = ("sync", "tensor", "vector", "scalar", "gpsimd")

    def __init__(self, nc):
        self.nc = nc
        self.streams = {e: [] for e in self.ENGINES}
        self.cnt = {}
        self.sems = {}
        for name in ("ld", "gd", "pe", "dv", "ac"):
            self.sems[name] = nc.alloc_semaphore(name)
            self.cnt[name] = 0

    AUTO = {"vector": "dv", "scalar": "ac"}
    SELFWAIT = {"vector"}

    def op(self, engine, build, waits=(), inc=None, inc_k=1):
        waits = list(waits)
        auto = self.AUTO.get(engine)
        if auto is not None:
            if engine in self.SELFWAIT:
                waits.append((auto, self.cnt[auto]))
            if inc is None:
                inc, inc_k = auto, 1
            assert inc == auto
        if inc is not None:
            self.cnt[inc] += inc_k
        sems = self.sems

        def fn(eng, build=build, waits=tuple(waits), inc=inc, inc_k=inc_k):
            best = {}
            for s, v in waits:
                if v > 0:
                    best[s] = max(best.get(s, 0), v)
            for s, v in best.items():
                eng.wait_ge(sems[s], v)
            ins = build(eng)
            if inc is not None and ins is not None:
                ins.then_inc(sems[inc], inc_k)
        self.streams[engine].append(fn)
        return self.cnt.get(inc) if inc is not None else None

    def wait(self, engine, name, val):
        sems = self.sems
        if val > 0:
            self.streams[engine].append(
                lambda eng, s=name, v=val: eng.wait_ge(sems[s], v))

    def run(self):
        with self.nc.Block() as block:
            @block.sync
            def _(eng):
                for fn in self.streams["sync"]:
                    fn(eng)

            @block.tensor
            def _(eng):
                for fn in self.streams["tensor"]:
                    fn(eng)

            @block.vector
            def _(eng):
                for fn in self.streams["vector"]:
                    fn(eng)

            @block.scalar
            def _(eng):
                for fn in self.streams["scalar"]:
                    fn(eng)

            @block.gpsimd
            def _(eng):
                for fn in self.streams["gpsimd"]:
                    fn(eng)


def build_program(nc, C, phases=("r1", "e1", "m", "r2", "e2", "c2")):
    NT, D, DFF, E, CAP, SEQ = C.NT, C.D, C.DFF, C.E, C.CAP, C.SEQ
    DC, FC, RT = C.DC, C.FC, C.RT

    xin = nc.dram_tensor("xin", [NT, D], F32, kind="ExternalInput").ap()
    wqkva = nc.dram_tensor("wqkva", [3 * D, D], F32, kind="ExternalInput").ap()
    gammab = nc.dram_tensor("gammab", [128, D], F32, kind="ExternalInput").ap()
    wr_in = nc.dram_tensor("wr_in", [D, E], F32, kind="ExternalInput").ap()
    w1_in = nc.dram_tensor("w1_in", [E, D, DFF], F32, kind="ExternalInput").ap()
    w2_in = nc.dram_tensor("w2_in", [E, DFF, D], F32, kind="ExternalInput").ap()
    yout = nc.dram_tensor("yout", [NT, D], F32, kind="ExternalOutput").ap()

    xcur = nc.dram_tensor("xcur", [NT, D], F32, kind="Internal").ap()
    t_d = nc.dram_tensor("t_d", [NT, D], BF16, kind="Internal").ap()
    disp = nc.dram_tensor("disp", [E * CAP + 1, D], BF16, kind="Internal").ap()
    eout_d = nc.dram_tensor("eout_d", [E * CAP, D], BF16, kind="Internal").ap()
    bases_d = nc.dram_tensor("bases_d", [1, RT * E], F32, kind="Internal").ap()
    wT_d = nc.dram_tensor("wT_d", [128, 3 * D * D // 128], BF16, kind="Internal").ap()

    ident_np = np.eye(128, dtype=np.float32)
    identb_d = nc.inline_tensor(ident_np, "identb_d").ap()
    identf_d = nc.inline_tensor(ident_np, "identf_d").ap()
    U_d = nc.inline_tensor(np.triu(np.ones((128, 128), np.float32)), "U_d").ap()
    iota8_d = nc.inline_tensor(
        np.tile(np.arange(E, dtype=np.float32), (128, 1)), "iota8_d").ap()
    ones_d = nc.inline_tensor(np.ones((128, 1), np.float32), "ones_d").ap()

    ctx = Ctx(nc)
    from contextlib import ExitStack
    es = ExitStack()

    def sb(name, shape, dtype):
        return es.enter_context(nc.sbuf_tensor(name, shape, dtype))

    def ps(name, shape, dtype):
        return es.enter_context(nc.psum_tensor(name, shape, dtype))

    identb = sb("identb", [128, 128], BF16)
    identf = sb("identf", [128, 128], F32)
    Ub = sb("Ub", [128, 128], BF16)
    iota8 = sb("iota8", [128, E], F32)
    ones8f = sb("ones8f", [128, E], F32)
    onesRT = sb("onesRT", [1, RT], F32)
    onesb = sb("onesb", [128, 1], BF16)
    gam = sb("gam", [128, D], F32)
    wrh = sb("wrh", [128, DC * E], BF16)
    wrl = sb("wrl", [128, DC * E], BF16)
    wrf = sb("wrf", [128, DC * E], F32)

    idxS = sb("idxS", [128, RT], F32)
    posS = sb("posS", [128, RT], F32)
    keptS = sb("keptS", [128, RT], F32)
    pmaxS = sb("pmaxS", [128, RT], F32)
    sfacS = pmaxS
    flatI = sb("flatI", [128, RT], F32)
    flatIi = sb("flatIi", [128, RT], I32)
    scatI = sb("scatI", [128, 2], I32)
    meanS = sb("meanS", [128, RT], F32)
    rstdS = sb("rstdS", [128, RT], F32)
    nmr = sb("nmr", [128, RT], F32)
    statsA = meanS
    statsB = rstdS
    wrhl = sb("wrhl", [128, DC * 2 * E], BF16)

    SEQT = SEQ // 128
    A1sz = max(DC * DFF, C.OC * DC * 128 + SEQ, 3 * E * RT * 2 + 2 * RT)
    big1 = sb("big1", [128, A1sz], BF16)
    A2sz = max(FC * D, DC * SEQ + 4 * 2 * SEQ)
    big2 = sb("big2", [128, A2sz], BF16)
    bb_off = ((3 * D + 4 * SEQT + 1) // 2) * 2
    A3sz = max(FC * 256, 2 * SEQ, bb_off + 2 * E * RT)
    bigC = sb("bigC", [128, A3sz], BF16)

    w1s = big1[:, 0:DC * DFF]
    w2s = big2[:, 0:FC * D]
    h_sb = bigC[:, 0:FC * 256]
    wT = big1[:, 0:C.OC * DC * 128]
    qb = big1[:, C.OC * DC * 128:C.OC * DC * 128 + SEQ]
    xn_fm = big2[:, 0:DC * SEQ]
    kvf = big2[:, DC * SEQ:DC * SEQ + 2 * SEQ].bitcast(F32)
    af = big2[:, DC * SEQ + 2 * SEQ:DC * SEQ + 4 * SEQ].bitcast(F32)
    yf = big2[:, DC * SEQ + 4 * SEQ:DC * SEQ + 6 * SEQ].bitcast(F32)
    of = big2[:, DC * SEQ + 6 * SEQ:DC * SEQ + 8 * SEQ].bitcast(F32)
    stg = bigC[:, 0:2 * SEQ].bitcast(F32)
    fmh = bigC[:, 0:D]
    fml = bigC[:, D:2 * D]
    tlo = bigC[:, 2 * D:3 * D]
    sscol = sb("sscol", [128, SEQT], F32)[:]
    smcol = sb("smcol", [128, SEQT], F32)[:]
    cnts = big1[0:1, 0:2 * E * RT].bitcast(F32)
    basesEM = big1[0:1, 2 * E * RT:4 * E * RT].bitcast(F32)
    basesTM = big1[0:1, 4 * E * RT:6 * E * RT].bitcast(F32)
    onesRT = big1[0:1, 6 * E * RT:6 * E * RT + 2 * RT].bitcast(F32)
    basesB = bigC[:, bb_off:bb_off + 2 * E * RT].bitcast(F32)

    xt = sb("xt", [128, D], F32)
    xt2 = sb("xt2", [128, D], F32)
    tf = sb("tf", [128, D], F32)
    thi = sb("thi", [128, D], BF16)
    rsc = sb("rsc", [128, 16 * E], F32)
    ohbf = sb("ohbf", [128, 2 * E], BF16)
    zz = sb("zz", [128, D], BF16)
    mark = sb("mark", [1, 1], F32)
    eo_g = sb("eo_g", [128, D], BF16)
    tt_g = sb("tt_g", [128, D], BF16)
    mucol = sb("mucol", [128, SEQT], F32)
    scal = sb("scal", [128, SEQT], F32)
    wtmp = sb("wtmp", [128, D], F32)
    wtmpb = sb("wtmpb", [128, D], BF16)
    disp_in_b = (xt[:, 0:D].bitcast(BF16), xt2[:, 0:D].bitcast(BF16))
    disp_fm_b = (tf[:, 0:D].bitcast(BF16), wtmp[:, 0:D].bitcast(BF16))
    eo_bf8 = sb("eo_bf8", [128, DC * 256], BF16)
    eo_stage = sb("eo_stage2", [128, 2 * D], BF16)
    sq_scr = tf
    dcomb = tf


    p_q0 = ps("p_q0", [128, 512], F32)
    p_q1 = ps("p_q1", [128, 512], F32)
    p_h0t = ps("p_h0t", [128, 256], F32)
    p_h1t = ps("p_h1t", [128, 256], F32)
    p_e0t = ps("p_e0t", [128, 256], F32)
    p_e1t = ps("p_e1t", [128, 256], F32)
    p_sm = ps("p_sm", [128, 512], F32)
    p_lgt = ps("p_lgt", [128, 512], F32)
    p_h0 = p_h0t[:, 0:256]
    p_h1 = p_h1t[:, 0:256]
    p_e0 = p_e0t[:, 0:256]
    p_e1 = p_e1t[:, 0:256]
    p_smb = p_sm[:, 0:512].bitcast(BF16)
    p_lgb = p_lgt[:, 0:512].bitcast(BF16)
    p_lgA = [p_lgt[:, 0:2 * E], p_lgt[:, 2 * E:4 * E]]
    p_lgB = [p_lgt[:, 4 * E:5 * E], p_lgt[:, 5 * E:6 * E]]
    p_cm = [p_lgt[:, 48:48 + E], p_lgt[:, 56:56 + E]]
    p_cs = [p_lgt[0:1, 64:64 + E], p_lgt[0:1, 72:72 + E]]

    op, wait, cnt = ctx.op, ctx.wait, ctx.cnt



    op("sync", lambda e: e.dma_start(out=identf[:], in_=identf_d), inc="ld", inc_k=16)
    op("sync", lambda e: e.dma_start(out=iota8[:], in_=iota8_d), inc="ld", inc_k=16)
    op("sync", lambda e: e.dma_start(out=gam[:], in_=gammab), inc="ld", inc_k=16)
    op("sync", lambda e: e.dma_start(
        out=wrf[:].rearrange("p (c e) -> p c e", e=E),
        in_=wr_in.rearrange("(c p) e -> p c e", p=128)),
        inc="ld", inc_k=16)
    op("gpsimd", lambda e: e.dma_start(out=identb[:], in_=identb_d), inc="gd", inc_k=16)
    op("gpsimd", lambda e: e.dma_start(out=Ub[:], in_=U_d), inc="gd", inc_k=16)
    op("gpsimd", lambda e: e.dma_start(out=onesb[:], in_=ones_d), inc="gd", inc_k=16)
    v_ld0, v_gd0 = cnt["ld"], cnt["gd"]
    op("vector", lambda e: e.tensor_copy(out=wrh[:], in_=wrf[:]),
       waits=[("ld", v_ld0), ("gd", v_gd0)])
    op("vector", lambda e: e.tensor_tensor(
        out=wrl[:], in0=wrf[:], in1=wrh[:], op=ALU.subtract))
    op("vector", lambda e: e.tensor_copy(
        out=wrhl[:].rearrange("p (c t e) -> p c t e", t=2, e=E)[:, :, 0, :],
        in_=wrh[:].rearrange("p (c e) -> p c e", e=E)))
    op("vector", lambda e: e.tensor_copy(
        out=wrhl[:].rearrange("p (c t e) -> p c t e", t=2, e=E)[:, :, 1, :],
        in_=wrl[:].rearrange("p (c e) -> p c e", e=E)))
    op("vector", lambda e: e.memset(zz[:], 0.0))
    op("vector", lambda e: e.memset(ones8f[:], 1.0))
    v_init_dv = op("vector", lambda e: e.memset(mark[:], 0.0), inc="dv")
    for r in range((E * CAP) // 128):
        op("gpsimd", lambda e, r=r: e.dma_start(
            out=disp[r * 128:(r + 1) * 128, :], in_=zz[:]),
            waits=[("dv", v_init_dv)] if r == 0 else [],
            inc="gd", inc_k=16)
    v_zero = cnt["gd"]

    wt_ev = 0
    for oc in range(C.OC):
        v = op("sync", lambda e, oc=oc: e.dma_start(
            out=wtmp[:], in_=wqkva[oc * 128:(oc + 1) * 128, :]),
            waits=[("pe", cnt["pe"])], inc="ld", inc_k=16)
        v_c = op("vector", lambda e: e.tensor_copy(out=wtmpb[:], in_=wtmp[:]),
                 waits=[("ld", v)], inc="dv")
        for dc in range(DC):
            op("tensor", lambda e, dc=dc: e.transpose(
                out=p_smb[:, dc * 128:(dc + 1) * 128],
                in_=wtmpb[:, dc * 128:(dc + 1) * 128], identity=identb[:]),
                waits=[("dv", v_c), ("dv", wt_ev)] if dc == 0 else [],
                inc="pe" if dc == DC - 1 else None)
        vpet = cnt["pe"]
        wt_ev = op("vector", lambda e, oc=oc: e.tensor_copy(
            out=wT[:, oc * DC * 128:(oc + 1) * DC * 128],
            in_=p_smb[:, 0:DC * 128]), waits=[("pe", vpet)], inc="dv")
    v_wT_dv = cnt["dv"]
    v_wT_st = op("sync", lambda e: e.dma_start(out=wT_d, in_=wT),
                 waits=[("dv", v_wT_dv)], inc="ld", inc_k=16)

    def routing_phase(xsrc, prefetch=None):
        xmark = {}
        for k in range(RT):
            x_t = xt if k % 2 == 0 else xt2
            w = [("ac", xmark[k - 2][0])] if k - 2 in xmark else []
            v = op("sync", lambda e, k=k, x_t=x_t: e.dma_start(
                out=x_t[:], in_=xsrc[k * 128:(k + 1) * 128, :]),
                waits=w, inc="ld", inc_k=16)
            va = op("scalar", lambda e, k=k, x_t=x_t: e.activation(
                out=sq_scr[:], in_=x_t[:], func=AF.Square,
                accum_out=statsB[:, k:k + 1]),
                waits=[("ld", v)], inc="ac")
            vd = op("scalar", lambda e, k=k, x_t=x_t: e.activation(
                out=sq_scr[:], in_=x_t[:], func=AF.Copy,
                accum_out=statsA[:, k:k + 1]),
                waits=[("ld", v)], inc="ac")
            xmark[k] = (vd, vd)
        v_stats_ac, v_stats_dv = cnt["ac"], cnt["dv"]
        op("vector", lambda e: e.tensor_scalar(
            out=meanS[:, 0:RT], in0=statsA[:, 0:RT], scalar1=1.0 / D,
            scalar2=None, op0=ALU.mult), waits=[("ac", v_stats_ac)])
        op("vector", lambda e: e.tensor_scalar(
            out=rstdS[:, 0:RT], in0=statsB[:, 0:RT], scalar1=1.0 / D,
            scalar2=None, op0=ALU.mult))
        op("vector", lambda e: e.tensor_tensor(
            out=posS[:, 0:RT], in0=meanS[:, 0:RT], in1=meanS[:, 0:RT],
            op=ALU.mult))
        op("vector", lambda e: e.tensor_tensor(
            out=rstdS[:, 0:RT], in0=rstdS[:, 0:RT], in1=posS[:, 0:RT],
            op=ALU.subtract))
        v1 = op("vector", lambda e: e.tensor_scalar(
            out=rstdS[:, 0:RT], in0=rstdS[:, 0:RT], scalar1=1e-5, scalar2=None,
            op0=ALU.add), inc="dv")
        va = op("scalar", lambda e: e.activation(
            out=rstdS[:, 0:RT], in_=rstdS[:, 0:RT], func=AF.Sqrt),
            waits=[("dv", v1)], inc="ac")
        op("vector", lambda e: e.reciprocal(
            out=rstdS[:, 0:RT], in_=rstdS[:, 0:RT]), waits=[("ac", va)])
        v_rstd = op("vector", lambda e: e.scalar_tensor_tensor(
            out=nmr[:, 0:RT], in0=meanS[:, 0:RT], scalar=-1.0,
            in1=rstdS[:, 0:RT], op0=ALU.mult, op1=ALU.mult), inc="dv")

        marks = {}
        for k in range(RT):
            p = k % 2
            x_t = xt if p == 0 else xt2
            pw = marks.get(k - 2, {})
            pm = marks.get(k - 1, {})
            v = op("sync", lambda e, k=k, x_t=x_t: e.dma_start(
                out=x_t[:], in_=xsrc[k * 128:(k + 1) * 128, :]),
                waits=[("gd", pw["tmp"])] if pw
                else [("ac", v_stats_ac)], inc="ld", inc_k=16)
            m = {}
            tfp = tf if p == 0 else wtmp
            m["tmp"] = op("gpsimd", lambda e, k=k, x_t=x_t, tfp=tfp:
                          e.tensor_scalar(
                out=tfp[:], in0=x_t[:], scalar1=rstdS[:, k:k + 1],
                scalar2=nmr[:, k:k + 1], op0=ALU.mult, op1=ALU.add),
                waits=[("ld", v), ("ac", pw.get("thi", 0)),
                       ("gd", pw.get("tlo", 0))] +
                      ([("dv", v_rstd)] if k < 2 else []),
                inc="gd", inc_k=1)
            m["thi"] = v_thi = op("scalar", lambda e, tfp=tfp: e.activation(
                out=thi[:], in_=tfp[:], func=AF.Copy),
                waits=[("gd", m["tmp"]), ("pe", pm.get("vpeh", 0)),
                       ("gd", max(pm.get("tstore", 0), pm.get("tlo", 0)))],
                inc="ac")
            m["tstore"] = op("gpsimd", lambda e, k=k: e.dma_start(
                out=t_d[k * 128:(k + 1) * 128, :], in_=thi[:]),
                waits=[("ac", v_thi)], inc="gd", inc_k=16)
            m["tlo"] = v_tlo = op("gpsimd", lambda e, tfp=tfp: e.tensor_tensor(
                out=tlo[:], in0=tfp[:], in1=thi[:], op=ALU.subtract),
                waits=[("ac", v_thi), ("pe", pm.get("vpel", 0))],
                inc="gd", inc_k=1)
            for dc in range(DC):
                op("tensor", lambda e, dc=dc: e.transpose(
                    out=p_smb[:, dc * 128:(dc + 1) * 128],
                    in_=thi[:, dc * 128:(dc + 1) * 128], identity=identb[:]),
                    waits=[("ac", v_thi),
                           ("dv", pm.get("fml", v_wT_dv if k == 0 else 0))]
                    if dc == 0 else [],
                    inc="pe" if dc == DC - 1 else None)
            m["vpeh"] = vpeh = cnt["pe"]
            v_fmh = op("scalar", lambda e: e.activation(
                out=fmh[:, 0:D], in_=p_smb[:, 0:D], func=AF.Copy),
                waits=[("pe", vpeh), ("pe", pm.get("lgmm", 0))], inc="ac")
            for dc in range(DC):
                op("tensor", lambda e, dc=dc: e.transpose(
                    out=p_smb[:, dc * 128:(dc + 1) * 128],
                    in_=tlo[:, dc * 128:(dc + 1) * 128], identity=identb[:]),
                    waits=[("gd", v_tlo), ("ac", v_fmh)] if dc == 0 else [],
                    inc="pe" if dc == DC - 1 else None)
            m["vpel"] = vpel = cnt["pe"]
            m["fml"] = v_fm = op("vector", lambda e: e.tensor_copy(
                out=fml[:, 0:D], in_=p_smb[:, 0:D]),
                waits=[("pe", vpel), ("pe", pm.get("lgmm", 0))], inc="dv")
            for dc in range(DC):
                op("tensor", lambda e, dc=dc, p=p: e.matmul(
                    out=p_lgA[p], lhsT=fmh[:, dc * 128:(dc + 1) * 128],
                    rhs=wrhl[:, dc * 2 * E:(dc + 1) * 2 * E],
                    start=(dc == 0), stop=(dc == DC - 1)),
                    waits=([("ac", v_fmh), ("dv", pw.get("lgread", 0))]
                           if dc == 0 else []),
                    inc="pe" if dc == DC - 1 else None)
            for dc in range(DC):
                op("tensor", lambda e, dc=dc, p=p: e.matmul(
                    out=p_lgB[p], lhsT=fml[:, dc * 128:(dc + 1) * 128],
                    rhs=wrh[:, dc * E:(dc + 1) * E],
                    start=(dc == 0), stop=(dc == DC - 1)),
                    waits=[("dv", v_fm)] if dc == 0 else [],
                    inc="pe" if dc == DC - 1 else None)
            m["lgmm"] = v_lgmm = cnt["pe"]
            rb = rsc[:, p * 8 * E:(p + 1) * 8 * E]
            lg = rb[:, 0:E]
            mx = rb[:, E:E + 1]
            s2 = rb[:, E + 1:E + 2]
            eq = rb[:, 2 * E:3 * E]
            pr = rb[:, 3 * E:4 * E]
            oh = rb[:, 4 * E:5 * E]
            ex = rb[:, 5 * E:6 * E]
            ohb = ohbf[:, p * E:(p + 1) * E]
            op("vector", lambda e, lg=lg, p=p: e.tensor_copy(
                out=lg, in_=p_lgA[p][:, 0:E]), waits=[("pe", v_lgmm)])
            op("vector", lambda e, lg=lg, p=p: e.tensor_tensor(
                out=lg, in0=lg, in1=p_lgA[p][:, E:2 * E], op=ALU.add))
            m["lgread"] = op("vector", lambda e, lg=lg, p=p: e.tensor_tensor(
                out=lg, in0=lg, in1=p_lgB[p], op=ALU.add), inc="dv")
            op("vector", lambda e, lg=lg, mx=mx: e.reduce_max(
                out=mx, in_=lg, axis=AX.X))
            op("vector", lambda e, lg=lg, mx=mx, eq=eq: e.tensor_scalar(
                out=eq, in0=lg, scalar1=mx, scalar2=None, op0=ALU.is_ge))
            op("vector", lambda e, eq=eq, pr=pr: e.tensor_tensor_scan(
                out=pr, data0=ones8f[:], data1=eq, initial=0.0,
                op0=ALU.mult, op1=ALU.add))
            op("vector", lambda e, pr=pr: e.tensor_scalar(
                out=pr, in0=pr, scalar1=1.0, scalar2=None, op0=ALU.is_le))
            op("vector", lambda e, eq=eq, pr=pr, oh=oh: e.tensor_tensor(
                out=oh, in0=eq, in1=pr, op=ALU.mult))
            op("vector", lambda e, oh=oh, ex=ex: e.tensor_tensor(
                out=ex, in0=oh, in1=iota8[:], op=ALU.mult))
            op("vector", lambda e, k=k, ex=ex: e.reduce_sum(
                out=idxS[:, k:k + 1], in_=ex, axis=AX.X))
            v2 = op("vector", lambda e, lg=lg, mx=mx, ex=ex: e.tensor_scalar(
                out=ex, in0=lg, scalar1=mx, scalar2=None, op0=ALU.subtract),
                inc="dv")
            va = op("scalar", lambda e, ex=ex: e.activation(
                out=ex, in_=ex, func=AF.Exp), waits=[("dv", v2)], inc="ac")
            op("vector", lambda e, ex=ex, s2=s2: e.reduce_sum(
                out=s2, in_=ex, axis=AX.X), waits=[("ac", va)])
            op("vector", lambda e, k=k, s2=s2: e.reciprocal(
                out=pmaxS[:, k:k + 1], in_=s2))
            v3 = op("vector", lambda e, oh=oh, ohb=ohb: e.tensor_copy(
                out=ohb, in_=oh), inc="dv")
            vcm = op("tensor", lambda e, p=p, ohb=ohb: e.matmul(
                out=p_cm[p], lhsT=Ub[:], rhs=ohb, start=True, stop=True),
                waits=[("dv", v3), ("dv", pw.get("cmread", 0)),
                       ("dv", pw.get("csread", 0))], inc="pe")
            vcs = op("tensor", lambda e, p=p, ohb=ohb: e.matmul(
                out=p_cs[p][:], lhsT=onesb[:], rhs=ohb, start=True, stop=True),
                inc="pe")
            m["csread"] = op("vector", lambda e, k=k, p=p: e.tensor_copy(
                out=cnts[0:1, k:k + (E - 1) * RT + 1:RT], in_=p_cs[p]),
                waits=[("pe", vcs)], inc="dv")
            m["cmread"] = op("vector", lambda e, ex=ex, p=p: e.tensor_copy(
                out=ex, in_=p_cm[p]), waits=[("pe", vcm)], inc="dv")
            op("vector", lambda e, ex=ex, oh=oh: e.tensor_tensor(
                out=ex, in0=ex, in1=oh, op=ALU.mult))
            m["tdone"] = op("vector", lambda e, k=k, ex=ex: e.reduce_sum(
                out=posS[:, k:k + 1], in_=ex, axis=AX.X), inc="dv")
            marks[k] = m
        v_passA = cnt["dv"]

        op("vector", lambda e: e.memset(onesRT[0:1, 0:RT], 1.0))
        for ee in range(E):
            op("vector", lambda e, ee=ee: e.tensor_tensor_scan(
                out=basesEM[0:1, ee * RT:(ee + 1) * RT],
                data0=onesRT[0:1, 0:RT],
                data1=cnts[0:1, ee * RT:(ee + 1) * RT],
                initial=0.0, op0=ALU.mult, op1=ALU.add),
                waits=[("dv", v_passA)] if ee == 0 else [])
        op("vector", lambda e: e.tensor_tensor(
            out=basesEM[0:1, :], in0=basesEM[0:1, :], in1=cnts[0:1, :],
            op=ALU.subtract))
        v_b = op("vector", lambda e: e.tensor_copy(
            out=basesTM[0:1, :].rearrange("p (k e) -> p k e", e=E),
            in_=basesEM[0:1, :].rearrange("p (e k) -> p k e", e=E)),
            inc="dv")
        vg1 = op("gpsimd", lambda e: e.dma_start(
            out=bases_d, in_=basesTM[0:1, :]),
            waits=[("dv", v_b)], inc="gd", inc_k=16)
        vg2 = op("gpsimd", lambda e: e.dma_start(
            out=basesB[:], in_=bases_d.to_broadcast([128, RT * E])),
            inc="gd", inc_k=16)
        if prefetch is not None:
            prefetch(vg2)

        oh2 = rsc[:, 0:E]
        bsl = rsc[:, E:2 * E]
        sc1 = rsc[:, 2 * E:2 * E + 1]
        sc2 = rsc[:, 2 * E + 1:2 * E + 2]
        op("vector", lambda e: e.tensor_scalar(
            out=flatI[:, 0:RT], in0=idxS[:, 0:RT], scalar1=float(CAP),
            scalar2=None, op0=ALU.mult),
            waits=[("dv", v_passA), ("gd", vg2)])
        tt_bufs = (tt_g, wtmpb)
        scat_gate = cnt["gd"]
        smark = {}
        for k in range(RT):
            p = k % 2
            op("vector", lambda e, k=k: e.tensor_scalar(
                out=oh2, in0=iota8[:], scalar1=idxS[:, k:k + 1],
                scalar2=None, op0=ALU.is_equal))
            op("vector", lambda e, k=k: e.tensor_tensor(
                out=bsl, in0=oh2, in1=basesB[:, k * E:(k + 1) * E],
                op=ALU.mult))
            op("vector", lambda e: e.reduce_sum(out=sc1, in_=bsl, axis=AX.X))
            op("vector", lambda e, k=k: e.scalar_tensor_tensor(
                out=posS[:, k:k + 1], in0=sc1, scalar=1.0,
                in1=posS[:, k:k + 1], op0=ALU.subtract, op1=ALU.add))
            op("vector", lambda e, k=k: e.tensor_scalar(
                out=keptS[:, k:k + 1], in0=posS[:, k:k + 1],
                scalar1=float(CAP), scalar2=None, op0=ALU.is_lt))
            op("vector", lambda e, k=k: e.scalar_tensor_tensor(
                out=flatI[:, k:k + 1], in0=posS[:, k:k + 1],
                scalar=float(CAP - 1), in1=flatI[:, k:k + 1],
                op0=ALU.min, op1=ALU.add))
            op("vector", lambda e, k=k: e.tensor_copy(
                out=flatIi[:, k:k + 1], in_=flatI[:, k:k + 1]))
            op("vector", lambda e, k=k: e.scalar_tensor_tensor(
                out=sc2, in0=flatI[:, k:k + 1], scalar=float(E * CAP),
                in1=keptS[:, k:k + 1], op0=ALU.subtract, op1=ALU.mult))
            op("vector", lambda e: e.tensor_scalar(
                out=sc1, in0=sc2, scalar1=float(E * CAP), scalar2=None,
                op0=ALU.add))
            v5 = op("vector", lambda e, k=k, p=p: e.tensor_copy(
                out=scatI[:, p:p + 1], in_=sc1),
                waits=[("gd", smark.get(k - 2, scat_gate))], inc="dv")
            vt = op("sync", lambda e, k=k, p=p: e.dma_start(
                out=tt_bufs[p][:], in_=t_d[k * 128:(k + 1) * 128, :]),
                waits=[("gd", smark.get(k - 2, scat_gate))] +
                      ([("dv", v_wT_dv)] if k < 2 else []),
                inc="ld", inc_k=16)
            smark[k] = op("gpsimd", lambda e, k=k, p=p: e.indirect_dma_start(
                out=disp,
                out_offset=IndirectOffsetOnAxis(scatI[:, p:p + 1], 0),
                in_=tt_bufs[p][:], in_offset=None),
                waits=[("dv", v5), ("ld", vt)], inc="gd", inc_k=16)
        op("vector", lambda e: e.tensor_scalar(
            out=sfacS[:, 0:RT], in0=pmaxS[:, 0:RT], scalar1=1.0, scalar2=None,
            op0=ALU.subtract))
        op("vector", lambda e: e.tensor_tensor(
            out=sfacS[:, 0:RT], in0=sfacS[:, 0:RT], in1=keptS[:, 0:RT],
            op=ALU.mult))
        op("vector", lambda e: e.tensor_scalar(
            out=sfacS[:, 0:RT], in0=sfacS[:, 0:RT], scalar1=1.0, scalar2=0.5,
            op0=ALU.add, op1=ALU.mult), inc="dv")

    def emit_w0_prefetch(w1_waits, w2_waits):
        FQ = (DFF // 128) // 4
        w1m, w2m = [], []
        for q in range(4):
            w1m.append(op("gpsimd", lambda e, q=q: e.dma_start(
                out=w1s[:, 0:DC * DFF].rearrange(
                    "p (c f) -> p c f", f=DFF)[:, :, q * FQ * 128:(q + 1) * FQ * 128],
                in_=w1_in[0].rearrange("(c p) f -> p c f", p=128)[
                    :, :, q * FQ * 128:(q + 1) * FQ * 128]),
                waits=w1_waits if q == 0 else [], inc="gd", inc_k=16))
        for q in range(4):
            w2m.append(op("gpsimd", lambda e, q=q: e.dma_start(
                out=w2s[:, q * (FC // 4) * D:(q + 1) * (FC // 4) * D],
                in_=w2_in[0].rearrange("(c p) f -> p c f", p=128)[
                    :, q * (FC // 4):(q + 1) * (FC // 4), :]),
                waits=w2_waits if q == 0 else [], inc="gd", inc_k=16))
        return (w1m, w2m)

    def expert_phase(pre=None):
        v_scat = cnt["gd"]
        tiles = [(ee, r0, rn) for ee in range(E) for (r0, rn) in chunks_of(CAP, 256)]
        import os
        _lim = int(os.environ.get("EXPERT_TILES", "0"))
        if _lim:
            tiles = tiles[:_lim]
        NTL = len(tiles)
        per_e = max(1, len(chunks_of(CAP, 256)))

        w1q = [[None] * 4 for _ in range(E)]
        w2q = [[None] * 4 for _ in range(E)]
        m_hlast = {}
        m_elast = {}
        FQ = FC // 4
        DQ = DC * FQ * 128

        def emit_w_loads(ee, pe_gate_h, pe_gate_e):
            for q in range(4):
                w1q[ee][q] = op("gpsimd", lambda e, ee=ee, q=q: e.dma_start(
                    out=w1s[:, 0:DC * DFF].rearrange(
                        "p (c f) -> p c f", f=DFF)[:, :, q * FQ * 128:(q + 1) * FQ * 128],
                    in_=w1_in[ee].rearrange("(c p) f -> p c f", p=128)[
                        :, :, q * FQ * 128:(q + 1) * FQ * 128]),
                    waits=[("pe", pe_gate_h)] if q == 0 else [],
                    inc="gd", inc_k=16)
            for q in range(4):
                w2q[ee][q] = op("gpsimd", lambda e, ee=ee, q=q: e.dma_start(
                    out=w2s[:, q * FQ * 8 * D // 8:0] if False else
                        w2s[:, q * FQ * D * 1:0] if False else
                        w2s[:, q * (FC // 4) * D:(q + 1) * (FC // 4) * D],
                    in_=w2_in[ee].rearrange("(c p) f -> p c f", p=128)[
                        :, q * (FC // 4):(q + 1) * (FC // 4), :]),
                    waits=[("pe", pe_gate_e)] if q == 0 else [],
                    inc="gd", inc_k=16)

        if pre is not None:
            w1q[0], w2q[0] = pre
        else:
            emit_w_loads(0, cnt["pe"], cnt["pe"])

        m_ld = {}
        m_dtrans = {}
        m_fmevac = {}
        m_eotrans = {}
        m_stgevA = {}
        m_stgev = {}
        m_store = {}
        m_relu = {}
        m_ev8 = {}

        def tinfo(T):
            ee, r0, rn = tiles[T]
            return ee, ee * CAP + r0, rn, chunks_of(rn, 128)

        def emit_disp_load(T):
            ee, base, rn, tcs = tinfo(T)
            w0 = []
            if T == 0:
                w0 = [("gd", v_scat), ("pe", cnt["pe"]),
                      ("dv", cnt["dv"]), ("ac", cnt["ac"])]
            elif T - 2 in m_dtrans:
                w0 = [("pe", m_dtrans[T - 2])]
            v = None
            for ti, (t0, tn) in enumerate(tcs):
                v = op("sync", lambda e, base=base, t0=t0, tn=tn, ti=ti, T=T:
                       e.dma_start(
                    out=disp_in_b[T % 2][0:tn, ti * D:(ti + 1) * D],
                    in_=disp[base + t0:base + t0 + tn, :]),
                    waits=w0 if ti == 0 else [], inc="ld", inc_k=16)
            m_ld[T] = v

        def emit_disp_trans(T):
            ee, base, rn, tcs = tinfo(T)
            p = T % 2
            for ti, (t0, tn) in enumerate(tcs):
                if ti == 0:
                    w0 = [("ld", m_ld[T]), ("dv", m_stgevA.get(T - 3, 0))]
                else:
                    w0 = [("dv", m_fmevac[T])]
                for dc in range(DC):
                    op("tensor", lambda e, ti=ti, dc=dc, tn=tn, p=p: e.transpose(
                        out=p_smb[0:128, dc * 128:dc * 128 + tn],
                        in_=disp_in_b[p][0:tn, ti * D + dc * 128:ti * D + (dc + 1) * 128],
                        identity=identb[:]),
                        waits=w0 if dc == 0 else [],
                        inc="pe" if dc == DC - 1 else None)
                vpet = cnt["pe"]
                m_fmevac[T] = op("vector", lambda e, ti=ti, tn=tn, rn=rn, p=p:
                                 e.tensor_copy(
                    out=disp_fm_b[p][:, 0:DC * rn].rearrange(
                        "p (dc f) -> p dc f", f=rn)[:, :, ti * 128:ti * 128 + tn],
                    in_=p_smb[:, 0:DC * 128].rearrange(
                        "p (dc f) -> p dc f", f=128)[:, :, 0:tn]),
                    waits=[("pe", vpet), ("pe", m_hlast.get(("t", T - 2), 0))],
                    inc="dv")
            m_dtrans[T] = cnt["pe"]

        def emit_h(T):
            ee, base, rn, tcs = tinfo(T)
            p = T % 2
            first_of_e = (T % per_e == 0)
            marks = []
            for ff in range(FC):
                pb = p_h0 if ff % 2 == 0 else p_h1
                w0 = []
                if ff == 0:
                    w0 = [("dv", m_fmevac[T]), ("ac", m_relu.get(T - 1, [0])[-1])]
                elif ff >= 2:
                    w0 = [("ac", marks[ff - 2])]
                if first_of_e and ff % FQ == 0:
                    w0 = w0 + [("gd", w1q[ee][ff // FQ])]
                for dc in range(DC):
                    op("tensor", lambda e, ff=ff, dc=dc, rn=rn, pb=pb, p=p: e.matmul(
                        out=pb[:, 0:rn],
                        lhsT=w1s[:, dc * DFF + ff * 128:dc * DFF + (ff + 1) * 128],
                        rhs=disp_fm_b[p][:, dc * rn:(dc + 1) * rn],
                        start=(dc == 0), stop=(dc == DC - 1)),
                        waits=w0 if dc == 0 else [],
                        inc="pe" if dc == DC - 1 else None)
                v_mm = cnt["pe"]
                marks.append(op("scalar", lambda e, ff=ff, rn=rn, pb=pb:
                                e.activation(
                    out=h_sb[:, ff * rn:(ff + 1) * rn], in_=pb[:, 0:rn],
                    func=AF.Relu), waits=[("pe", v_mm)], inc="ac"))
            m_relu[T] = marks
            m_hlast[("t", T)] = cnt["pe"]
            if (T + 1) % per_e == 0:
                m_hlast[ee] = cnt["pe"]
                if ee + 1 < E:
                    emit_w1_next(ee + 1)

        def emit_w1_next(ee):
            for q in range(4):
                w1q[ee][q] = op("gpsimd", lambda e, ee=ee, q=q: e.dma_start(
                    out=w1s[:, 0:DC * DFF].rearrange(
                        "p (c f) -> p c f", f=DFF)[:, :, q * FQ * 128:(q + 1) * FQ * 128],
                    in_=w1_in[ee].rearrange("(c p) f -> p c f", p=128)[
                        :, :, q * FQ * 128:(q + 1) * FQ * 128]),
                    waits=[("pe", m_hlast[ee - 1])] if q == 0 else [],
                    inc="gd", inc_k=16)

        def emit_w2_next(ee):
            for q in range(4):
                w2q[ee][q] = op("gpsimd", lambda e, ee=ee, q=q: e.dma_start(
                    out=w2s[:, q * (FC // 4) * D:(q + 1) * (FC // 4) * D],
                    in_=w2_in[ee].rearrange("(c p) f -> p c f", p=128)[
                        :, q * (FC // 4):(q + 1) * (FC // 4), :]),
                    waits=[("pe", m_elast[ee - 1])] if q == 0 else [],
                    inc="gd", inc_k=16)

        def emit_eout(T):
            ee, base, rn, tcs = tinfo(T)
            ev = []
            for dco in range(DC):
                pb = p_e0 if dco % 2 == 0 else p_e1
                w0 = []
                if dco == 0:
                    w0 = [("ac", m_relu[T][-1]), ("gd", w2q[ee][3]),
                          ("dv", m_ev8.get(T - 1, [0])[-1])]
                elif dco >= 2:
                    w0 = [("dv", ev[dco - 2])]
                for ff in range(FC):
                    op("tensor", lambda e, dco=dco, ff=ff, rn=rn, pb=pb: e.matmul(
                        out=pb[:, 0:rn],
                        lhsT=w2s[:, ff * D + dco * 128:ff * D + (dco + 1) * 128],
                        rhs=h_sb[:, ff * rn:(ff + 1) * rn],
                        start=(ff == 0), stop=(ff == FC - 1)),
                        waits=w0 if ff == 0 else [],
                        inc="pe" if ff == FC - 1 else None)
                v_mm = cnt["pe"]
                ev.append(op("vector", lambda e, rn=rn, pb=pb, dco=dco: e.tensor_copy(
                    out=eo_bf8[:, dco * 256:dco * 256 + rn], in_=pb[:, 0:rn]),
                    waits=[("pe", v_mm), ("pe", m_eotrans.get(T - 1, 0))],
                    inc="dv"))
            m_ev8[T] = ev
            m_elast[("t", T)] = cnt["pe"]
            if (T + 1) % per_e == 0:
                m_elast[ee] = cnt["pe"]
                if ee + 1 < E:
                    emit_w2_next(ee + 1)

        def emit_eout_trans(T):
            ee, base, rn, tcs = tinfo(T)
            ntc = len(tcs)
            half = DC // 2
            for b in range(2):
                bankv = p_smb if b == 0 else p_lgb
                dcos = list(range(b * half, (b + 1) * half))
                n_tr = len(dcos) * ntc
                i_tr = 0
                for dco in dcos:
                    for ti, (t0, tn) in enumerate(tcs):
                        w0 = []
                        if i_tr == 0:
                            w0 = [("dv", m_ev8[T][-1])]
                            if b == 0:
                                w0.append(("dv", m_fmevac.get(T + 2, 0)))
                        op("tensor", lambda e, dco=dco, ti=ti, tn=tn, bankv=bankv,
                           b=b, half=half: e.transpose(
                            out=bankv[0:tn, ((dco - b * half) * 2 + ti) * 128:
                                      ((dco - b * half) * 2 + ti) * 128 + 128],
                            in_=eo_bf8[:, dco * 256 + ti * 128:dco * 256 + ti * 128 + tn],
                            identity=identb[:]),
                            waits=w0,
                            inc="pe" if i_tr == n_tr - 1 else None)
                        i_tr += 1
                vpet = cnt["pe"]
                rm = min(rn, 128)
                bankv_cur = bankv
                m_stgev[T] = op("vector", lambda e, b=b, half=half, ntc=ntc,
                                rm=rm, bankv=bankv_cur: e.tensor_copy(
                    out=eo_stage[0:rm, 0:ntc * D].rearrange(
                        "p (ti dco j) -> p ti dco j", dco=DC, j=128)[
                        :, :, b * half:(b + 1) * half, :],
                    in_=bankv[0:rm, 0:half * ntc * 128].rearrange(
                        "p (dco ti j) -> p ti dco j", ti=ntc, j=128)),
                    waits=[("pe", vpet), ("ld", m_store.get(T - 1, 0))],
                    inc="dv")
                if b == 0:
                    m_stgevA[T] = m_stgev[T]
            m_eotrans[T] = cnt["pe"]
            v = None
            for ti, (t0, tn) in enumerate(tcs):
                v = op("sync", lambda e, base=base, t0=t0, tn=tn, ti=ti:
                       e.dma_start(
                    out=eout_d[base + t0:base + t0 + tn, :],
                    in_=eo_stage[0:tn, ti * D:(ti + 1) * D]),
                    waits=[("dv", m_stgev[T])] if ti == 0 else [],
                    inc="ld", inc_k=16)
            m_store[T] = v

        emit_disp_load(0)
        emit_disp_trans(0)
        if NTL > 1:
            emit_disp_load(1)
        for T in range(NTL):
            emit_h(T)
            if T + 1 < NTL:
                emit_disp_trans(T + 1)
            if T + 2 < NTL:
                emit_disp_load(T + 2)
            if T - 1 >= 0:
                emit_eout_trans(T - 1)
            emit_eout(T)
        emit_eout_trans(NTL - 1)

    cb_eo = (eo_g, thi,
             bigC[:, 4096:4096 + D], bigC[:, 4096 + D:4096 + 2 * D],
             zz, bigC[:, 2 * D:3 * D],
             bigC[:, 0:D], bigC[:, D:2 * D])
    cb_tw = (tt_g, wtmpb,
             bigC[:, 4096 + 2 * D:4096 + 3 * D],
             bigC[:, 4096 + 3 * D:4096 + 4 * D],
             big2[:, 0:D], big2[:, D:2 * D],
             big2[:, 2 * D:3 * D], big2[:, 3 * D:4 * D])
    cb_x = (xt, xt2,
            big1[:, 26624:26624 + 2048].bitcast(F32),
            big1[:, 26624 + 2048:26624 + 4096].bitcast(F32),
            big2[:, 24576:24576 + 2048].bitcast(F32),
            big2[:, 24576 + 2048:24576 + 4096].bitcast(F32),
            big1[:, 16384:16384 + 2048].bitcast(F32),
            big1[:, 16384 + 2048:16384 + 4096].bitcast(F32))
    cb_o = (tf, wtmp,
            eo_bf8[:, 0:2048].bitcast(F32),
            eo_stage[:, 0:2048].bitcast(F32),
            big2[:, 24576 + 4096:24576 + 6144].bitcast(F32),
            big2[:, 24576 + 6144:24576 + 8192].bitcast(F32),
            big1[:, 20480:20480 + 2048].bitcast(F32),
            big1[:, 20480 + 2048:20480 + 4096].bitcast(F32))

    def combine_tiles(xsrc, dst, tiles, cm, v_eo, cycle=(0, 1, 2, 3)):
        for k in tiles:
            p = cycle[cm.get("_i", 0) % len(cycle)]
            cm["_i"] = cm.get("_i", 0) + 1
            pw = cm.get(("p", p))
            vg = op("gpsimd", lambda e, k=k, p=p: e.indirect_dma_start(
                out=cb_eo[p][:, 0:D], out_offset=None,
                in_=eout_d,
                in_offset=IndirectOffsetOnAxis(flatIi[:, k:k + 1], 0)),
                waits=[("ld", v_eo), ("dv", pw[0] if pw else 0)],
                inc="gd", inc_k=16)
            vx = op("sync", lambda e, k=k, p=p: e.dma_start(
                out=cb_x[p][:], in_=xsrc[k * 128:(k + 1) * 128, :]),
                waits=([("dv", pw[1])] if pw else [("ld", v_eo)]),
                inc="ld", inc_k=16)
            vtw = op("gpsimd", lambda e, k=k, p=p: e.tensor_scalar(
                out=cb_tw[p][:], in0=cb_x[p][:], scalar1=rstdS[:, k:k + 1],
                scalar2=nmr[:, k:k + 1], op0=ALU.mult, op1=ALU.add),
                waits=[("ld", vx)] + ([("dv", pw[1])] if pw else []),
                inc="gd", inc_k=1)
            op("vector", lambda e, p=p: e.tensor_tensor(
                out=cb_o[p][:], in0=cb_eo[p][:, 0:D], in1=cb_tw[p][:],
                op=ALU.subtract),
                waits=[("gd", vg), ("gd", vtw), ("ld", pw[2] if pw else 0)])
            op("vector", lambda e, k=k, p=p: e.scalar_tensor_tensor(
                out=cb_o[p][:], in0=cb_o[p][:], scalar=keptS[:, k:k + 1],
                in1=cb_tw[p][:], op0=ALU.mult, op1=ALU.add))
            v6 = op("vector", lambda e, k=k, p=p: e.scalar_tensor_tensor(
                out=cb_o[p][:], in0=cb_o[p][:], scalar=sfacS[:, k:k + 1],
                in1=cb_x[p][:], op0=ALU.mult, op1=ALU.add), inc="dv")
            vs = op("sync", lambda e, k=k, p=p, dst=dst: e.dma_start(
                out=dst[k * 128:(k + 1) * 128, :], in_=cb_o[p][:]),
                waits=[("dv", v6)], inc="ld", inc_k=16)
            cm[k] = cm[("p", p)] = (v6, v6, vs)

    def stats_tile(s, k, st, cm):
        srow = s * SEQ
        amark = st.setdefault(("am", s), {})
        x_t = xt if k % 2 == 0 else xt2
        w0 = [("ld", cm[s * SEQT + k][2])]
        w0 += ([("dv", st["v8"])] if k < 2 else
               [("ac", amark[k - 2][0]), ("dv", amark[k - 2][1])])
        v = op("sync", lambda e, k=k, srow=srow, x_t=x_t: e.dma_start(
            out=x_t[:], in_=xcur[srow + k * 128:srow + (k + 1) * 128, :]),
            waits=w0, inc="ld", inc_k=16)
        va = op("scalar", lambda e, k=k, x_t=x_t: e.activation(
            out=sq_scr[:], in_=x_t[:], func=AF.Square,
            accum_out=sscol[:, k:k + 1]),
            waits=[("ld", v)] + ([("dv", st["v7"])] if k == 0 else []),
            inc="ac")
        vd = op("vector", lambda e, k=k, x_t=x_t: e.reduce_sum(
            out=smcol[:, k:k + 1], in_=x_t[:], axis=AX.X),
            waits=[("ld", v)], inc="dv")
        st["v_sm"] = 0
        amark[k] = (va, vd)

    def attention_seq(s, st, cm):
        frs = chunks_of(SEQ, 512)
        srow = s * SEQ
        if True:
            v_ss = cnt["ac"]
            op("vector", lambda e: e.tensor_scalar(
                out=mucol[:, 0:SEQT], in0=smcol[:, 0:SEQT], scalar1=1.0 / D,
                scalar2=None, op0=ALU.mult),
                waits=[("ac", v_ss), ("gd", st["v_sm"])])
            op("vector", lambda e: e.tensor_scalar(
                out=scal[:, 0:SEQT], in0=sscol[:, 0:SEQT], scalar1=1.0 / D,
                scalar2=None, op0=ALU.mult))
            op("vector", lambda e: e.tensor_tensor(
                out=smcol[:, 0:SEQT], in0=mucol[:, 0:SEQT],
                in1=mucol[:, 0:SEQT], op=ALU.mult))
            v1a = op("vector", lambda e: e.tensor_tensor(
                out=scal[:, 0:SEQT], in0=scal[:, 0:SEQT], in1=smcol[:, 0:SEQT],
                op=ALU.subtract), inc="dv")
            v1b = op("vector", lambda e: e.tensor_scalar(
                out=smcol[:, 0:SEQT], in0=scal[:, 0:SEQT], scalar1=1e-5,
                scalar2=None, op0=ALU.add), inc="dv")
            va = op("scalar", lambda e: e.activation(
                out=smcol[:, 0:SEQT], in_=smcol[:, 0:SEQT], func=AF.Sqrt),
                waits=[("dv", v1b)], inc="ac")
            op("vector", lambda e: e.reciprocal(
                out=smcol[:, 0:SEQT], in_=smcol[:, 0:SEQT]), waits=[("ac", va)])
            op("vector", lambda e: e.tensor_tensor(
                out=scal[:, 0:SEQT], in0=scal[:, 0:SEQT], in1=smcol[:, 0:SEQT],
                op=ALU.mult))
            op("vector", lambda e: e.tensor_tensor(
                out=scal[:, 0:SEQT], in0=scal[:, 0:SEQT], in1=smcol[:, 0:SEQT],
                op=ALU.mult))
            v1c = op("vector", lambda e: e.tensor_scalar(
                out=scal[:, 0:SEQT], in0=scal[:, 0:SEQT], scalar1=float(D),
                scalar2=None, op0=ALU.mult), inc="dv")
            va = op("scalar", lambda e: e.activation(
                out=scal[:, 0:SEQT], in_=scal[:, 0:SEQT], func=AF.Sqrt),
                waits=[("dv", v1c)], inc="ac")
            op("vector", lambda e: e.tensor_scalar(
                out=scal[:, 0:SEQT], in0=scal[:, 0:SEQT], scalar1=1e-12,
                scalar2=None, op0=ALU.max), waits=[("ac", va)])
            op("vector", lambda e: e.reciprocal(
                out=scal[:, 0:SEQT], in_=scal[:, 0:SEQT]))
            op("vector", lambda e: e.tensor_scalar(
                out=scal[:, 0:SEQT], in0=scal[:, 0:SEQT],
                scalar1=float(math.sqrt(D)), scalar2=None, op0=ALU.mult))
            v7 = op("vector", lambda e: e.tensor_tensor(
                out=scal[:, 0:SEQT], in0=scal[:, 0:SEQT], in1=smcol[:, 0:SEQT],
                op=ALU.mult), inc="dv")
            st["v7"] = v7
            xn_ev = [st["stg"]]
            xmark2 = {}
            for k in range(SEQT):
                x_t = xt if k % 2 == 0 else xt2
                w0 = ([("dv", v7)] if k == 0 else
                      ([("dv", xmark2[k - 2])] if k - 2 in xmark2 else []))
                v = op("sync", lambda e, k=k, srow=srow, x_t=x_t: e.dma_start(
                    out=x_t[:], in_=xcur[srow + k * 128:srow + (k + 1) * 128, :]),
                    waits=w0, inc="ld", inc_k=16)
                op("vector", lambda e, k=k, x_t=x_t: e.tensor_scalar(
                    out=tf[:], in0=x_t[:], scalar1=mucol[:, k:k + 1],
                    scalar2=scal[:, k:k + 1], op0=ALU.subtract, op1=ALU.mult),
                    waits=[("ld", v), ("dv", v7)])
                op("vector", lambda e: e.tensor_tensor(
                    out=tf[:], in0=tf[:], in1=gam[:], op=ALU.mult))
                v8 = op("vector", lambda e: e.tensor_copy(out=thi[:], in_=tf[:]),
                        inc="dv")
                xmark2[k] = v8
                for dc in range(DC):
                    op("tensor", lambda e, dc=dc: e.transpose(
                        out=p_smb[:, dc * 128:(dc + 1) * 128],
                        in_=thi[:, dc * 128:(dc + 1) * 128], identity=identb[:]),
                        waits=[("dv", v8), xn_ev[0]] if dc == 0 else [],
                        inc="pe" if dc == DC - 1 else None)
                vpet = cnt["pe"]
                xn_ev[0] = ("dv", op("vector", lambda e, k=k: e.tensor_copy(
                    out=xn_fm[:, 0:DC * SEQ].rearrange(
                        "p (dc s) -> p dc s", s=SEQ)[:, :, k * 128:(k + 1) * 128],
                    in_=p_smb[:, 0:DC * 128].rearrange(
                        "p (dc f) -> p dc f", f=128)),
                    waits=[("pe", vpet)], inc="dv"))
            st["v8"] = xmark2[SEQT - 1]
            st["xnpe"] = cnt["pe"]
            v_xn = xn_ev[0]
            for g in range(C.GC):
                def qkva(oc, dest, first_w):
                    evs = []
                    for fi, (f0, fn_) in enumerate(frs):
                        tick = st["tick"]
                        st["tick"] = tick + 1
                        pb = p_q0 if tick % 2 == 0 else p_q1
                        w0 = [st["bank"][tick - 2]] if tick - 2 in st["bank"] \
                            else []
                        if fi == 0 and g == 0:
                            w0 = w0 + [v_xn]
                        for dc in range(DC):
                            op("tensor", lambda e, oc=oc, dc=dc, f0=f0, fn_=fn_,
                               pb=pb: e.matmul(
                                out=pb[:, 0:fn_],
                                lhsT=wT[:, (oc * DC + dc) * 128:(oc * DC + dc + 1) * 128],
                                rhs=xn_fm[:, dc * SEQ + f0:dc * SEQ + f0 + fn_],
                                start=(dc == 0), stop=(dc == DC - 1)),
                                waits=w0 if dc == 0 else [],
                                inc="pe" if dc == DC - 1 else None)
                        v_mm = cnt["pe"]
                        ev = dest(fi, f0, fn_, pb, v_mm,
                                  first_w if fi == 0 else [])
                        st["bank"][tick] = ev
                        evs.append(ev)
                    return evs

                def kv_ev(fi, f0, fn_, pb, v_mm, fw):
                    return ("ac", op("scalar", lambda e, f0=f0, fn_=fn_, pb=pb:
                                     e.activation(
                        out=kvf[:, f0:f0 + fn_], in_=pb[:, 0:fn_],
                        func=AF.Copy), waits=[("pe", v_mm)] + fw, inc="ac"))

                def a_ev(fi, f0, fn_, pb, v_mm, fw):
                    return ("ac", op("scalar", lambda e, f0=f0, fn_=fn_, pb=pb:
                                     e.activation(
                        out=af[:, f0:f0 + fn_], in_=pb[:, 0:fn_],
                        func=AF.Sigmoid), waits=[("pe", v_mm)] + fw, inc="ac"))

                def q_ev(fi, f0, fn_, pb, v_mm, fw):
                    return ("ac", op("scalar", lambda e, f0=f0, fn_=fn_, pb=pb:
                                     e.activation(
                        out=qb[:, f0:f0 + fn_], in_=pb[:, 0:fn_],
                        func=AF.Copy), waits=[("pe", v_mm)] + fw, inc="ac"))

                qkva(C.GC + g, kv_ev, [("dv", st["v9g"])])
                qkva(2 * C.GC + g, a_ev, [])
                v_a = cnt["ac"]
                op("vector", lambda e: e.tensor_tensor_scan(
                    out=yf[:, 0:SEQ], data0=af[:, 0:SEQ], data1=kvf[:, 0:SEQ],
                    initial=0.0, op0=ALU.mult, op1=ALU.add),
                    waits=[("ac", v_a)])
                qkva(g, q_ev, [])
                v_q = cnt["ac"]
                v9 = op("vector", lambda e: e.tensor_tensor(
                    out=of[:, 0:SEQ], in0=qb[:, 0:SEQ], in1=yf[:, 0:SEQ],
                    op=ALU.mult),
                    waits=[("ac", v_q), ("pe", st["oevg"])], inc="dv")
                st["v9g"] = v9
                o_ev = []
                for k in range(SEQT):
                    bk = p_sm[:, 0:128] if k % 2 == 0 else p_lgt[:, 0:128]
                    w0 = []
                    if k == 0:
                        w0 = [("dv", v9)]
                    elif k >= 2:
                        w0 = [("dv", o_ev[k - 2])]
                    vpe = op("tensor", lambda e, k=k, bk=bk: e.transpose(
                        out=bk, in_=of[:, k * 128:(k + 1) * 128],
                        identity=identf[:]), waits=w0, inc="pe")
                    o_ev.append(op("vector", lambda e, k=k, bk=bk: e.tensor_copy(
                        out=stg[:, k * 128:(k + 1) * 128], in_=bk),
                        waits=[("pe", vpe)] +
                              ([("gd", st["accum"])] if k == 0 else []),
                        inc="dv"))
                st["oevg"] = cnt["pe"]
                st["stg"] = ("dv", o_ev[-1])
                v10 = o_ev[-1]
                st["accum"] = op("gpsimd", lambda e, g=g, srow=srow: e.dma_start(
                    out=xcur[srow:srow + SEQ, g * 128:(g + 1) * 128].rearrange(
                        "(c p) f -> p c f", p=128),
                    in_=stg[:, 0:SEQ].rearrange("p (c f) -> p c f", f=128),
                    accum_op=ALU.add),
                    waits=[("dv", v10)], inc="gd", inc_k=16)
                st["inter"](s, g)

    def middle_phase(combine_only=False, no_inter=False, no_inter_stats=False):
        v_eo = cnt["ld"]
        wait("sync", "gd", cnt["gd"])
        v_wt = op("sync", lambda e: e.dma_start(out=wT, in_=wT_d),
                  waits=[("pe", cnt["pe"]), ("dv", cnt["dv"]),
                         ("gd", cnt["gd"])], inc="ld", inc_k=16)
        wait("tensor", "ld", v_wt)
        cm = {}
        st = {"v9g": 0, "oevg": 0, "accum": 0, "stg": ("dv", 0),
              "v8": 0, "v7": 0, "sc9": 0, "v_sm": 0, "bank": {}, "tick": 0}

        def inter(s, g):
            if no_inter:
                return
            s1 = s + 1
            if s1 >= C.NSEQ:
                return
            combine_tiles(xin, xcur,
                          range(s1 * SEQT + 2 * g, s1 * SEQT + 2 * g + 2),
                          cm, v_eo, cycle=(2, 3))
            if g >= 1 and not no_inter_stats:
                stats_tile(s1, 2 * g - 2, st, cm)
                stats_tile(s1, 2 * g - 1, st, cm)

        st["inter"] = inter
        if combine_only:
            combine_tiles(xin, xcur, range(RT), cm, v_eo, cycle=(2, 3))
            return
        combine_tiles(xin, xcur, range(SEQT), cm, v_eo, cycle=(2, 3))
        for k in range(SEQT):
            stats_tile(0, k, st, cm)
        for s in range(C.NSEQ):
            attention_seq(s, st, cm)
            if s + 1 < C.NSEQ:
                if no_inter:
                    combine_tiles(xin, xcur,
                                  range((s + 1) * SEQT, (s + 2) * SEQT),
                                  cm, v_eo, cycle=(2, 3))
                if no_inter or no_inter_stats:
                    for k in range(SEQT - 2):
                        stats_tile(s + 1, k, st, cm)
                stats_tile(s + 1, SEQT - 2, st, cm)
                stats_tile(s + 1, SEQT - 1, st, cm)

    pre_box = {}
    if "r1" in phases:
        routing_phase(xin, prefetch=lambda vg2: pre_box.update(
            m=emit_w0_prefetch([("gd", vg2), ("ld", v_wT_st)], [("gd", vg2)])))
    if "e1" in phases:
        expert_phase(pre=pre_box.pop("m", None))
    if "m" in phases:
        middle_phase()
    if "mq" in phases:
        middle_phase(combine_only=True)
    if "mn" in phases:
        middle_phase(no_inter=True)
    if "ms" in phases:
        middle_phase(no_inter_stats=True)
    if "r2" in phases:
        wait("sync", "gd", cnt["gd"])
        wait("sync", "ld", cnt["ld"])
        wait("gpsimd", "gd", cnt["gd"])
        wait("gpsimd", "ld", cnt["ld"])
        pe_r2 = cnt["pe"]
        routing_phase(xcur, prefetch=lambda vg2: pre_box.update(
            m=emit_w0_prefetch([("gd", vg2)], [("gd", vg2), ("pe", pe_r2)])))
    if "e2" in phases:
        expert_phase(pre=pre_box.pop("m", None))
    if "c2" in phases:
        wait("sync", "gd", cnt["gd"])
        combine_tiles(xcur, yout, range(RT), {}, cnt["ld"],
                      cycle=tuple(range(8)))
    if "dx" in phases:
        wait("gpsimd", "ld", cnt["ld"])
        wait("gpsimd", "gd", cnt["gd"])
        op("gpsimd", lambda e: e.dma_start(out=yout, in_=xcur),
           inc="gd", inc_k=16)
    if "dt" in phases:
        wait("gpsimd", "ld", cnt["ld"])
        wait("gpsimd", "gd", cnt["gd"])
        op("gpsimd", lambda e: e.dma_start(out=yout, in_=t_d),
           inc="gd", inc_k=16)
    if "de" in phases:
        wait("gpsimd", "ld", cnt["ld"])
        wait("gpsimd", "gd", cnt["gd"])
        op("gpsimd", lambda e: e.dma_start(
            out=yout[0:min(NT, E * CAP), :], in_=eout_d[0:min(NT, E * CAP), :]),
           inc="gd", inc_k=16)

    wait("gpsimd", "gd", cnt["gd"])
    wait("gpsimd", "ld", cnt["ld"])
    wait("sync", "ld", cnt["ld"])
    wait("sync", "gd", cnt["gd"])

    ctx.run()
    es.close()
    return nc


def _layout_inputs(C, x, wq, gamma, wr, w1, w2):
    gam_b = np.ascontiguousarray(
        np.tile(np.asarray(gamma, np.float32).reshape(1, C.D), (128, 1)))
    return dict(
        xin=np.ascontiguousarray(np.asarray(x, np.float32).reshape(C.NT, C.D)),
        wqkva=np.ascontiguousarray(np.asarray(wq, np.float32)),
        gammab=gam_b,
        wr_in=np.ascontiguousarray(np.asarray(wr, np.float32)),
        w1_in=np.ascontiguousarray(np.asarray(w1, np.float32)),
        w2_in=np.ascontiguousarray(np.asarray(w2, np.float32)),
    )


def kernel(L0_attn_x, L0_attn_W_qkva, L0_attn_gamma, L0_ffn_Wr, L0_ffn_W1,
           L0_ffn_W2):
    C = REAL
    nc = bass.Bass("TRN2", target_bir_lowering=False)
    build_program(nc, C)
    in_map = _layout_inputs(C, L0_attn_x, L0_attn_W_qkva, L0_attn_gamma,
                            L0_ffn_Wr, L0_ffn_W1, L0_ffn_W2)
    res = run_bass_kernel_spmd(nc, [in_map], core_ids=[0])
    return res.results[0]["yout"].reshape(16, 2048, 1024)


# version 13 (final)
# speedup vs baseline: 1.0623x; 1.0472x over previous
import math
import numpy as np
import concourse.bass as bass
import concourse.mybir as mybir
from concourse.bass import IndirectOffsetOnAxis
from concourse.bass_utils import run_bass_kernel_spmd

F32 = mybir.dt.float32
BF16 = mybir.dt.bfloat16
I32 = mybir.dt.int32
AF = mybir.ActivationFunctionType
ALU = mybir.AluOpType
AX = mybir.AxisListType

BIG = 1 << 20


class CFG:
    def __init__(self, ntok, d, dff, e, seq, cap):
        self.NT, self.D, self.DFF, self.E, self.SEQ, self.CAP = \
            ntok, d, dff, e, seq, cap
        self.NSEQ = ntok // seq
        self.DC = d // 128
        self.FC = dff // 128
        self.RT = ntok // 128
        self.OC = 3 * d // 128
        self.GC = d // 128
        assert ntok % 128 == 0 and d % 128 == 0 and dff % 128 == 0
        assert seq % 128 == 0 and ntok % seq == 0 and cap % 128 == 0


REAL = CFG(ntok=32768, d=1024, dff=4096, e=8, seq=2048, cap=5120)


def chunks_of(total, size):
    out, o = [], 0
    while o < total:
        out.append((o, min(size, total - o)))
        o += size
    return out


class Ctx:
    ENGINES = ("sync", "tensor", "vector", "scalar", "gpsimd")

    def __init__(self, nc):
        self.nc = nc
        self.streams = {e: [] for e in self.ENGINES}
        self.cnt = {}
        self.sems = {}
        for name in ("ld", "gd", "pe", "dv", "ac"):
            self.sems[name] = nc.alloc_semaphore(name)
            self.cnt[name] = 0

    AUTO = {"vector": "dv", "scalar": "ac"}
    SELFWAIT = {"vector"}

    def op(self, engine, build, waits=(), inc=None, inc_k=1):
        waits = list(waits)
        auto = self.AUTO.get(engine)
        if auto is not None:
            if engine in self.SELFWAIT:
                waits.append((auto, self.cnt[auto]))
            if inc is None:
                inc, inc_k = auto, 1
            assert inc == auto
        if inc is not None:
            self.cnt[inc] += inc_k
        sems = self.sems

        def fn(eng, build=build, waits=tuple(waits), inc=inc, inc_k=inc_k):
            best = {}
            for s, v in waits:
                if v > 0:
                    best[s] = max(best.get(s, 0), v)
            for s, v in best.items():
                eng.wait_ge(sems[s], v)
            ins = build(eng)
            if inc is not None and ins is not None:
                ins.then_inc(sems[inc], inc_k)
        self.streams[engine].append(fn)
        return self.cnt.get(inc) if inc is not None else None

    def wait(self, engine, name, val):
        sems = self.sems
        if val > 0:
            self.streams[engine].append(
                lambda eng, s=name, v=val: eng.wait_ge(sems[s], v))

    def run(self):
        with self.nc.Block() as block:
            @block.sync
            def _(eng):
                for fn in self.streams["sync"]:
                    fn(eng)

            @block.tensor
            def _(eng):
                for fn in self.streams["tensor"]:
                    fn(eng)

            @block.vector
            def _(eng):
                for fn in self.streams["vector"]:
                    fn(eng)

            @block.scalar
            def _(eng):
                for fn in self.streams["scalar"]:
                    fn(eng)

            @block.gpsimd
            def _(eng):
                for fn in self.streams["gpsimd"]:
                    fn(eng)


def build_program(nc, C, phases=("r1", "e1", "m", "r2", "e2", "c2")):
    NT, D, DFF, E, CAP, SEQ = C.NT, C.D, C.DFF, C.E, C.CAP, C.SEQ
    DC, FC, RT = C.DC, C.FC, C.RT

    xin = nc.dram_tensor("xin", [NT, D], F32, kind="ExternalInput").ap()
    wqkva = nc.dram_tensor("wqkva", [3 * D, D], F32, kind="ExternalInput").ap()
    gammab = nc.dram_tensor("gammab", [128, D], F32, kind="ExternalInput").ap()
    wr_in = nc.dram_tensor("wr_in", [D, E], F32, kind="ExternalInput").ap()
    w1_in = nc.dram_tensor("w1_in", [E, D, DFF], F32, kind="ExternalInput").ap()
    w2_in = nc.dram_tensor("w2_in", [E, DFF, D], F32, kind="ExternalInput").ap()
    yout = nc.dram_tensor("yout", [NT, D], F32, kind="ExternalOutput").ap()

    xcur = nc.dram_tensor("xcur", [NT, D], F32, kind="Internal").ap()
    t_d = nc.dram_tensor("t_d", [NT, D], BF16, kind="Internal").ap()
    disp = nc.dram_tensor("disp", [E * CAP + 1, D], BF16, kind="Internal").ap()
    eout_d = nc.dram_tensor("eout_d", [E * CAP, D], BF16, kind="Internal").ap()
    bases_d = nc.dram_tensor("bases_d", [1, RT * E], F32, kind="Internal").ap()
    wT_d = nc.dram_tensor("wT_d", [128, 3 * D * D // 128], BF16, kind="Internal").ap()

    ident_np = np.eye(128, dtype=np.float32)
    identb_d = nc.inline_tensor(ident_np, "identb_d").ap()
    identf_d = nc.inline_tensor(ident_np, "identf_d").ap()
    U_d = nc.inline_tensor(np.triu(np.ones((128, 128), np.float32)), "U_d").ap()
    iota8_d = nc.inline_tensor(
        np.tile(np.arange(E, dtype=np.float32), (128, 1)), "iota8_d").ap()
    ones_d = nc.inline_tensor(np.ones((128, 1), np.float32), "ones_d").ap()

    ctx = Ctx(nc)
    from contextlib import ExitStack
    es = ExitStack()

    def sb(name, shape, dtype):
        return es.enter_context(nc.sbuf_tensor(name, shape, dtype))

    def ps(name, shape, dtype):
        return es.enter_context(nc.psum_tensor(name, shape, dtype))

    identb = sb("identb", [128, 128], BF16)
    identf = sb("identf", [128, 128], F32)
    Ub = sb("Ub", [128, 128], BF16)
    iota8 = sb("iota8", [128, E], F32)
    ones8f = sb("ones8f", [128, E], F32)
    onesRT = sb("onesRT", [1, RT], F32)
    onesb = sb("onesb", [128, 1], BF16)
    gam = sb("gam", [128, D], F32)
    wrh = sb("wrh", [128, DC * E], BF16)
    wrl = sb("wrl", [128, DC * E], BF16)
    wrf = sb("wrf", [128, DC * E], F32)

    idxS = sb("idxS", [128, RT], F32)
    posS = sb("posS", [128, RT], F32)
    keptS = sb("keptS", [128, RT], F32)
    pmaxS = sb("pmaxS", [128, RT], F32)
    sfacS = pmaxS
    flatI = sb("flatI", [128, RT], F32)
    flatIi = sb("flatIi", [128, RT], I32)
    scatI = sb("scatI", [128, 2], I32)
    meanS = sb("meanS", [128, RT], F32)
    rstdS = sb("rstdS", [128, RT], F32)
    nmr = sb("nmr", [128, RT], F32)
    statsA = meanS
    statsB = rstdS
    wrhl = sb("wrhl", [128, DC * 2 * E], BF16)

    SEQT = SEQ // 128
    A1sz = max(DC * DFF, C.OC * DC * 128 + SEQ, 3 * E * RT * 2 + 2 * RT)
    big1 = sb("big1", [128, A1sz], BF16)
    A2sz = max(FC * D, DC * SEQ + 4 * 2 * SEQ)
    big2 = sb("big2", [128, A2sz], BF16)
    bb_off = ((3 * D + 4 * SEQT + 1) // 2) * 2
    A3sz = max(FC * 256, 2 * SEQ, bb_off + 2 * E * RT)
    bigC = sb("bigC", [128, A3sz], BF16)

    w1s = big1[:, 0:DC * DFF]
    w2s = big2[:, 0:FC * D]
    h_sb = bigC[:, 0:FC * 256]
    wT = big1[:, 0:C.OC * DC * 128]
    qb = big1[:, C.OC * DC * 128:C.OC * DC * 128 + SEQ]
    xn_fm = big2[:, 0:DC * SEQ]
    kvf = big2[:, DC * SEQ:DC * SEQ + 2 * SEQ].bitcast(F32)
    af = big2[:, DC * SEQ + 2 * SEQ:DC * SEQ + 4 * SEQ].bitcast(F32)
    yf = big2[:, DC * SEQ + 4 * SEQ:DC * SEQ + 6 * SEQ].bitcast(F32)
    of = big2[:, DC * SEQ + 6 * SEQ:DC * SEQ + 8 * SEQ].bitcast(F32)
    stg = bigC[:, 0:2 * SEQ].bitcast(F32)
    fmh = bigC[:, 0:D]
    fml = bigC[:, D:2 * D]
    tlo = bigC[:, 2 * D:3 * D]
    sscol = sb("sscol", [128, SEQT], F32)[:]
    smcol = sb("smcol", [128, SEQT], F32)[:]
    cnts = big1[0:1, 0:2 * E * RT].bitcast(F32)
    basesEM = big1[0:1, 2 * E * RT:4 * E * RT].bitcast(F32)
    basesTM = big1[0:1, 4 * E * RT:6 * E * RT].bitcast(F32)
    onesRT = big1[0:1, 6 * E * RT:6 * E * RT + 2 * RT].bitcast(F32)
    basesB = bigC[:, bb_off:bb_off + 2 * E * RT].bitcast(F32)

    xt = sb("xt", [128, D], F32)
    xt2 = sb("xt2", [128, D], F32)
    tf = sb("tf", [128, D], F32)
    thi = sb("thi", [128, D], BF16)
    rsc = sb("rsc", [128, 16 * E], F32)
    ohbf = sb("ohbf", [128, 2 * E], BF16)
    zz = sb("zz", [128, D], BF16)
    mark = sb("mark", [1, 1], F32)
    eo_g = sb("eo_g", [128, D], BF16)
    tt_g = sb("tt_g", [128, D], BF16)
    mucol = sb("mucol", [128, SEQT], F32)
    scal = sb("scal", [128, SEQT], F32)
    wtmp = sb("wtmp", [128, D], F32)
    wtmpb = sb("wtmpb", [128, D], BF16)
    disp_in_b = (xt[:, 0:D].bitcast(BF16), xt2[:, 0:D].bitcast(BF16))
    disp_fm_b = (tf[:, 0:D].bitcast(BF16), wtmp[:, 0:D].bitcast(BF16))
    eo_bf8 = sb("eo_bf8", [128, DC * 256], BF16)
    eo_stage = sb("eo_stage2", [128, 2 * D], BF16)
    sq_scr = tf
    dcomb = tf


    p_q0 = ps("p_q0", [128, 512], F32)
    p_q1 = ps("p_q1", [128, 512], F32)
    p_h0t = ps("p_h0t", [128, 256], F32)
    p_h1t = ps("p_h1t", [128, 256], F32)
    p_e0t = ps("p_e0t", [128, 256], F32)
    p_e1t = ps("p_e1t", [128, 256], F32)
    p_sm = ps("p_sm", [128, 512], F32)
    p_lgt = ps("p_lgt", [128, 512], F32)
    p_h0 = p_h0t[:, 0:256]
    p_h1 = p_h1t[:, 0:256]
    p_e0 = p_e0t[:, 0:256]
    p_e1 = p_e1t[:, 0:256]
    p_smb = p_sm[:, 0:512].bitcast(BF16)
    p_lgb = p_lgt[:, 0:512].bitcast(BF16)
    p_lgA = [p_lgt[:, 0:2 * E], p_lgt[:, 2 * E:4 * E]]
    p_lgB = [p_lgt[:, 4 * E:5 * E], p_lgt[:, 5 * E:6 * E]]
    p_cm = [p_lgt[:, 48:48 + E], p_lgt[:, 56:56 + E]]
    p_cs = [p_lgt[0:1, 64:64 + E], p_lgt[0:1, 72:72 + E]]

    op, wait, cnt = ctx.op, ctx.wait, ctx.cnt



    op("sync", lambda e: e.dma_start(out=identf[:], in_=identf_d), inc="ld", inc_k=16)
    op("sync", lambda e: e.dma_start(out=iota8[:], in_=iota8_d), inc="ld", inc_k=16)
    op("sync", lambda e: e.dma_start(out=gam[:], in_=gammab), inc="ld", inc_k=16)
    op("sync", lambda e: e.dma_start(
        out=wrf[:].rearrange("p (c e) -> p c e", e=E),
        in_=wr_in.rearrange("(c p) e -> p c e", p=128)),
        inc="ld", inc_k=16)
    op("gpsimd", lambda e: e.dma_start(out=identb[:], in_=identb_d), inc="gd", inc_k=16)
    op("gpsimd", lambda e: e.dma_start(out=Ub[:], in_=U_d), inc="gd", inc_k=16)
    op("gpsimd", lambda e: e.dma_start(out=onesb[:], in_=ones_d), inc="gd", inc_k=16)
    v_ld0, v_gd0 = cnt["ld"], cnt["gd"]
    op("vector", lambda e: e.tensor_copy(out=wrh[:], in_=wrf[:]),
       waits=[("ld", v_ld0), ("gd", v_gd0)])
    op("vector", lambda e: e.tensor_tensor(
        out=wrl[:], in0=wrf[:], in1=wrh[:], op=ALU.subtract))
    op("vector", lambda e: e.tensor_copy(
        out=wrhl[:].rearrange("p (c t e) -> p c t e", t=2, e=E)[:, :, 0, :],
        in_=wrh[:].rearrange("p (c e) -> p c e", e=E)))
    op("vector", lambda e: e.tensor_copy(
        out=wrhl[:].rearrange("p (c t e) -> p c t e", t=2, e=E)[:, :, 1, :],
        in_=wrl[:].rearrange("p (c e) -> p c e", e=E)))
    op("vector", lambda e: e.memset(zz[:], 0.0))
    op("vector", lambda e: e.memset(ones8f[:], 1.0))
    v_init_dv = op("vector", lambda e: e.memset(mark[:], 0.0), inc="dv")
    for r in range((E * CAP) // 128):
        op("gpsimd", lambda e, r=r: e.dma_start(
            out=disp[r * 128:(r + 1) * 128, :], in_=zz[:]),
            waits=[("dv", v_init_dv)] if r == 0 else [],
            inc="gd", inc_k=16)
    v_zero = cnt["gd"]

    wt_ev = 0
    for oc in range(C.OC):
        v = op("sync", lambda e, oc=oc: e.dma_start(
            out=wtmp[:], in_=wqkva[oc * 128:(oc + 1) * 128, :]),
            waits=[("pe", cnt["pe"])], inc="ld", inc_k=16)
        v_c = op("vector", lambda e: e.tensor_copy(out=wtmpb[:], in_=wtmp[:]),
                 waits=[("ld", v)], inc="dv")
        for dc in range(DC):
            op("tensor", lambda e, dc=dc: e.transpose(
                out=p_smb[:, dc * 128:(dc + 1) * 128],
                in_=wtmpb[:, dc * 128:(dc + 1) * 128], identity=identb[:]),
                waits=[("dv", v_c), ("dv", wt_ev)] if dc == 0 else [],
                inc="pe" if dc == DC - 1 else None)
        vpet = cnt["pe"]
        wt_ev = op("vector", lambda e, oc=oc: e.tensor_copy(
            out=wT[:, oc * DC * 128:(oc + 1) * DC * 128],
            in_=p_smb[:, 0:DC * 128]), waits=[("pe", vpet)], inc="dv")
    v_wT_dv = cnt["dv"]
    v_wT_st = op("sync", lambda e: e.dma_start(out=wT_d, in_=wT),
                 waits=[("dv", v_wT_dv)], inc="ld", inc_k=16)

    def routing_phase(xsrc, prefetch=None):
        xmark = {}
        for k in range(RT):
            x_t = xt if k % 2 == 0 else xt2
            w = [("ac", xmark[k - 2][0])] if k - 2 in xmark else []
            v = op("sync", lambda e, k=k, x_t=x_t: e.dma_start(
                out=x_t[:], in_=xsrc[k * 128:(k + 1) * 128, :]),
                waits=w, inc="ld", inc_k=16)
            va = op("scalar", lambda e, k=k, x_t=x_t: e.activation(
                out=sq_scr[:], in_=x_t[:], func=AF.Square,
                accum_out=statsB[:, k:k + 1]),
                waits=[("ld", v)], inc="ac")
            vd = op("scalar", lambda e, k=k, x_t=x_t: e.activation(
                out=sq_scr[:], in_=x_t[:], func=AF.Copy,
                accum_out=statsA[:, k:k + 1]),
                waits=[("ld", v)], inc="ac")
            xmark[k] = (vd, vd)
        v_stats_ac, v_stats_dv = cnt["ac"], cnt["dv"]
        op("vector", lambda e: e.tensor_scalar(
            out=meanS[:, 0:RT], in0=statsA[:, 0:RT], scalar1=1.0 / D,
            scalar2=None, op0=ALU.mult), waits=[("ac", v_stats_ac)])
        op("vector", lambda e: e.tensor_scalar(
            out=rstdS[:, 0:RT], in0=statsB[:, 0:RT], scalar1=1.0 / D,
            scalar2=None, op0=ALU.mult))
        op("vector", lambda e: e.tensor_tensor(
            out=posS[:, 0:RT], in0=meanS[:, 0:RT], in1=meanS[:, 0:RT],
            op=ALU.mult))
        op("vector", lambda e: e.tensor_tensor(
            out=rstdS[:, 0:RT], in0=rstdS[:, 0:RT], in1=posS[:, 0:RT],
            op=ALU.subtract))
        v1 = op("vector", lambda e: e.tensor_scalar(
            out=rstdS[:, 0:RT], in0=rstdS[:, 0:RT], scalar1=1e-5, scalar2=None,
            op0=ALU.add), inc="dv")
        va = op("scalar", lambda e: e.activation(
            out=rstdS[:, 0:RT], in_=rstdS[:, 0:RT], func=AF.Sqrt),
            waits=[("dv", v1)], inc="ac")
        op("vector", lambda e: e.reciprocal(
            out=rstdS[:, 0:RT], in_=rstdS[:, 0:RT]), waits=[("ac", va)])
        v_rstd = op("vector", lambda e: e.scalar_tensor_tensor(
            out=nmr[:, 0:RT], in0=meanS[:, 0:RT], scalar=-1.0,
            in1=rstdS[:, 0:RT], op0=ALU.mult, op1=ALU.mult), inc="dv")

        marks = {}
        for k in range(RT):
            p = k % 2
            x_t = xt if p == 0 else xt2
            pw = marks.get(k - 2, {})
            pm = marks.get(k - 1, {})
            v = op("sync", lambda e, k=k, x_t=x_t: e.dma_start(
                out=x_t[:], in_=xsrc[k * 128:(k + 1) * 128, :]),
                waits=[("gd", pw["tmp"])] if pw
                else [("ac", v_stats_ac)], inc="ld", inc_k=16)
            m = {}
            tfp = tf if p == 0 else wtmp
            m["tmp"] = op("gpsimd", lambda e, k=k, x_t=x_t, tfp=tfp:
                          e.tensor_scalar(
                out=tfp[:], in0=x_t[:], scalar1=rstdS[:, k:k + 1],
                scalar2=nmr[:, k:k + 1], op0=ALU.mult, op1=ALU.add),
                waits=[("ld", v), ("ac", pw.get("thi", 0)),
                       ("gd", pw.get("tlo", 0))] +
                      ([("dv", v_rstd)] if k < 2 else []),
                inc="gd", inc_k=1)
            m["thi"] = v_thi = op("scalar", lambda e, tfp=tfp: e.activation(
                out=thi[:], in_=tfp[:], func=AF.Copy),
                waits=[("gd", m["tmp"]), ("pe", pm.get("vpeh", 0)),
                       ("gd", max(pm.get("tstore", 0), pm.get("tlo", 0)))],
                inc="ac")
            m["tstore"] = op("gpsimd", lambda e, k=k: e.dma_start(
                out=t_d[k * 128:(k + 1) * 128, :], in_=thi[:]),
                waits=[("ac", v_thi)], inc="gd", inc_k=16)
            m["tlo"] = v_tlo = op("gpsimd", lambda e, tfp=tfp: e.tensor_tensor(
                out=tlo[:], in0=tfp[:], in1=thi[:], op=ALU.subtract),
                waits=[("ac", v_thi), ("pe", pm.get("vpel", 0))],
                inc="gd", inc_k=1)
            for dc in range(DC):
                op("tensor", lambda e, dc=dc: e.transpose(
                    out=p_smb[:, dc * 128:(dc + 1) * 128],
                    in_=thi[:, dc * 128:(dc + 1) * 128], identity=identb[:]),
                    waits=[("ac", v_thi),
                           ("dv", pm.get("fml", v_wT_dv if k == 0 else 0))]
                    if dc == 0 else [],
                    inc="pe" if dc == DC - 1 else None)
            m["vpeh"] = vpeh = cnt["pe"]
            v_fmh = op("scalar", lambda e: e.activation(
                out=fmh[:, 0:D], in_=p_smb[:, 0:D], func=AF.Copy),
                waits=[("pe", vpeh), ("pe", pm.get("lgmm", 0))], inc="ac")
            for dc in range(DC):
                op("tensor", lambda e, dc=dc: e.transpose(
                    out=p_smb[:, dc * 128:(dc + 1) * 128],
                    in_=tlo[:, dc * 128:(dc + 1) * 128], identity=identb[:]),
                    waits=[("gd", v_tlo), ("ac", v_fmh)] if dc == 0 else [],
                    inc="pe" if dc == DC - 1 else None)
            m["vpel"] = vpel = cnt["pe"]
            m["fml"] = v_fm = op("vector", lambda e: e.tensor_copy(
                out=fml[:, 0:D], in_=p_smb[:, 0:D]),
                waits=[("pe", vpel), ("pe", pm.get("lgmm", 0))], inc="dv")
            for dc in range(DC):
                op("tensor", lambda e, dc=dc, p=p: e.matmul(
                    out=p_lgA[p], lhsT=fmh[:, dc * 128:(dc + 1) * 128],
                    rhs=wrhl[:, dc * 2 * E:(dc + 1) * 2 * E],
                    start=(dc == 0), stop=(dc == DC - 1)),
                    waits=([("ac", v_fmh), ("dv", pw.get("lgread", 0))]
                           if dc == 0 else []),
                    inc="pe" if dc == DC - 1 else None)
            for dc in range(DC):
                op("tensor", lambda e, dc=dc, p=p: e.matmul(
                    out=p_lgB[p], lhsT=fml[:, dc * 128:(dc + 1) * 128],
                    rhs=wrh[:, dc * E:(dc + 1) * E],
                    start=(dc == 0), stop=(dc == DC - 1)),
                    waits=[("dv", v_fm)] if dc == 0 else [],
                    inc="pe" if dc == DC - 1 else None)
            m["lgmm"] = v_lgmm = cnt["pe"]
            rb = rsc[:, p * 8 * E:(p + 1) * 8 * E]
            lg = rb[:, 0:E]
            mx = rb[:, E:E + 1]
            s2 = rb[:, E + 1:E + 2]
            eq = rb[:, 2 * E:3 * E]
            pr = rb[:, 3 * E:4 * E]
            oh = rb[:, 4 * E:5 * E]
            ex = rb[:, 5 * E:6 * E]
            ohb = ohbf[:, p * E:(p + 1) * E]
            op("vector", lambda e, lg=lg, p=p: e.tensor_copy(
                out=lg, in_=p_lgA[p][:, 0:E]), waits=[("pe", v_lgmm)])
            op("vector", lambda e, lg=lg, p=p: e.tensor_tensor(
                out=lg, in0=lg, in1=p_lgA[p][:, E:2 * E], op=ALU.add))
            m["lgread"] = op("vector", lambda e, lg=lg, p=p: e.tensor_tensor(
                out=lg, in0=lg, in1=p_lgB[p], op=ALU.add), inc="dv")
            op("vector", lambda e, lg=lg, mx=mx: e.reduce_max(
                out=mx, in_=lg, axis=AX.X))
            op("vector", lambda e, lg=lg, mx=mx, eq=eq: e.tensor_scalar(
                out=eq, in0=lg, scalar1=mx, scalar2=None, op0=ALU.is_ge))
            op("vector", lambda e, eq=eq, pr=pr: e.tensor_tensor_scan(
                out=pr, data0=ones8f[:], data1=eq, initial=0.0,
                op0=ALU.mult, op1=ALU.add))
            op("vector", lambda e, pr=pr: e.tensor_scalar(
                out=pr, in0=pr, scalar1=1.0, scalar2=None, op0=ALU.is_le))
            op("vector", lambda e, eq=eq, pr=pr, oh=oh: e.tensor_tensor(
                out=oh, in0=eq, in1=pr, op=ALU.mult))
            op("vector", lambda e, oh=oh, ex=ex: e.tensor_tensor(
                out=ex, in0=oh, in1=iota8[:], op=ALU.mult))
            op("vector", lambda e, k=k, ex=ex: e.reduce_sum(
                out=idxS[:, k:k + 1], in_=ex, axis=AX.X))
            v2 = op("vector", lambda e, lg=lg, mx=mx, ex=ex: e.tensor_scalar(
                out=ex, in0=lg, scalar1=mx, scalar2=None, op0=ALU.subtract),
                inc="dv")
            va = op("scalar", lambda e, ex=ex: e.activation(
                out=ex, in_=ex, func=AF.Exp), waits=[("dv", v2)], inc="ac")
            op("vector", lambda e, ex=ex, s2=s2: e.reduce_sum(
                out=s2, in_=ex, axis=AX.X), waits=[("ac", va)])
            op("vector", lambda e, k=k, s2=s2: e.reciprocal(
                out=pmaxS[:, k:k + 1], in_=s2))
            v3 = op("vector", lambda e, oh=oh, ohb=ohb: e.tensor_copy(
                out=ohb, in_=oh), inc="dv")
            vcm = op("tensor", lambda e, p=p, ohb=ohb: e.matmul(
                out=p_cm[p], lhsT=Ub[:], rhs=ohb, start=True, stop=True),
                waits=[("dv", v3), ("dv", pw.get("cmread", 0)),
                       ("dv", pw.get("csread", 0))], inc="pe")
            vcs = op("tensor", lambda e, p=p, ohb=ohb: e.matmul(
                out=p_cs[p][:], lhsT=onesb[:], rhs=ohb, start=True, stop=True),
                inc="pe")
            m["csread"] = op("vector", lambda e, k=k, p=p: e.tensor_copy(
                out=cnts[0:1, k:k + (E - 1) * RT + 1:RT], in_=p_cs[p]),
                waits=[("pe", vcs)], inc="dv")
            m["cmread"] = op("vector", lambda e, ex=ex, p=p: e.tensor_copy(
                out=ex, in_=p_cm[p]), waits=[("pe", vcm)], inc="dv")
            op("vector", lambda e, ex=ex, oh=oh: e.tensor_tensor(
                out=ex, in0=ex, in1=oh, op=ALU.mult))
            m["tdone"] = op("vector", lambda e, k=k, ex=ex: e.reduce_sum(
                out=posS[:, k:k + 1], in_=ex, axis=AX.X), inc="dv")
            marks[k] = m
        v_passA = cnt["dv"]

        op("vector", lambda e: e.memset(onesRT[0:1, 0:RT], 1.0))
        for ee in range(E):
            op("vector", lambda e, ee=ee: e.tensor_tensor_scan(
                out=basesEM[0:1, ee * RT:(ee + 1) * RT],
                data0=onesRT[0:1, 0:RT],
                data1=cnts[0:1, ee * RT:(ee + 1) * RT],
                initial=0.0, op0=ALU.mult, op1=ALU.add),
                waits=[("dv", v_passA)] if ee == 0 else [])
        op("vector", lambda e: e.tensor_tensor(
            out=basesEM[0:1, :], in0=basesEM[0:1, :], in1=cnts[0:1, :],
            op=ALU.subtract))
        v_b = op("vector", lambda e: e.tensor_copy(
            out=basesTM[0:1, :].rearrange("p (k e) -> p k e", e=E),
            in_=basesEM[0:1, :].rearrange("p (e k) -> p k e", e=E)),
            inc="dv")
        vg1 = op("gpsimd", lambda e: e.dma_start(
            out=bases_d, in_=basesTM[0:1, :]),
            waits=[("dv", v_b)], inc="gd", inc_k=16)
        vg2 = op("gpsimd", lambda e: e.dma_start(
            out=basesB[:], in_=bases_d.to_broadcast([128, RT * E])),
            inc="gd", inc_k=16)
        if prefetch is not None:
            prefetch(vg2)

        oh2 = rsc[:, 0:E]
        bsl = rsc[:, E:2 * E]
        sc1 = rsc[:, 2 * E:2 * E + 1]
        sc2 = rsc[:, 2 * E + 1:2 * E + 2]
        op("vector", lambda e: e.tensor_scalar(
            out=flatI[:, 0:RT], in0=idxS[:, 0:RT], scalar1=float(CAP),
            scalar2=None, op0=ALU.mult),
            waits=[("dv", v_passA), ("gd", vg2)])
        tt_bufs = (tt_g, wtmpb)
        scat_gate = cnt["gd"]
        smark = {}
        for k in range(RT):
            p = k % 2
            op("vector", lambda e, k=k: e.tensor_scalar(
                out=oh2, in0=iota8[:], scalar1=idxS[:, k:k + 1],
                scalar2=None, op0=ALU.is_equal))
            op("vector", lambda e, k=k: e.tensor_tensor(
                out=bsl, in0=oh2, in1=basesB[:, k * E:(k + 1) * E],
                op=ALU.mult))
            op("vector", lambda e: e.reduce_sum(out=sc1, in_=bsl, axis=AX.X))
            op("vector", lambda e, k=k: e.scalar_tensor_tensor(
                out=posS[:, k:k + 1], in0=sc1, scalar=1.0,
                in1=posS[:, k:k + 1], op0=ALU.subtract, op1=ALU.add))
            op("vector", lambda e, k=k: e.tensor_scalar(
                out=keptS[:, k:k + 1], in0=posS[:, k:k + 1],
                scalar1=float(CAP), scalar2=None, op0=ALU.is_lt))
            op("vector", lambda e, k=k: e.scalar_tensor_tensor(
                out=flatI[:, k:k + 1], in0=posS[:, k:k + 1],
                scalar=float(CAP - 1), in1=flatI[:, k:k + 1],
                op0=ALU.min, op1=ALU.add))
            op("vector", lambda e, k=k: e.tensor_copy(
                out=flatIi[:, k:k + 1], in_=flatI[:, k:k + 1]))
            op("vector", lambda e, k=k: e.scalar_tensor_tensor(
                out=sc2, in0=flatI[:, k:k + 1], scalar=float(E * CAP),
                in1=keptS[:, k:k + 1], op0=ALU.subtract, op1=ALU.mult))
            op("vector", lambda e: e.tensor_scalar(
                out=sc1, in0=sc2, scalar1=float(E * CAP), scalar2=None,
                op0=ALU.add))
            v5 = op("vector", lambda e, k=k, p=p: e.tensor_copy(
                out=scatI[:, p:p + 1], in_=sc1),
                waits=[("gd", smark.get(k - 2, scat_gate))], inc="dv")
            vt = op("sync", lambda e, k=k, p=p: e.dma_start(
                out=tt_bufs[p][:], in_=t_d[k * 128:(k + 1) * 128, :]),
                waits=[("gd", smark.get(k - 2, scat_gate))] +
                      ([("dv", v_wT_dv)] if k < 2 else []),
                inc="ld", inc_k=16)
            smark[k] = op("gpsimd", lambda e, k=k, p=p: e.indirect_dma_start(
                out=disp,
                out_offset=IndirectOffsetOnAxis(scatI[:, p:p + 1], 0),
                in_=tt_bufs[p][:], in_offset=None),
                waits=[("dv", v5), ("ld", vt)], inc="gd", inc_k=16)
        op("vector", lambda e: e.tensor_scalar(
            out=sfacS[:, 0:RT], in0=pmaxS[:, 0:RT], scalar1=1.0, scalar2=None,
            op0=ALU.subtract))
        op("vector", lambda e: e.tensor_tensor(
            out=sfacS[:, 0:RT], in0=sfacS[:, 0:RT], in1=keptS[:, 0:RT],
            op=ALU.mult))
        op("vector", lambda e: e.tensor_scalar(
            out=sfacS[:, 0:RT], in0=sfacS[:, 0:RT], scalar1=1.0, scalar2=0.5,
            op0=ALU.add, op1=ALU.mult), inc="dv")

    def emit_w0_prefetch(w1_waits, w2_waits):
        FQ = (DFF // 128) // 4
        w1m, w2m = [], []
        for q in range(4):
            w1m.append(op("gpsimd", lambda e, q=q: e.dma_start(
                out=w1s[:, 0:DC * DFF].rearrange(
                    "p (c f) -> p c f", f=DFF)[:, :, q * FQ * 128:(q + 1) * FQ * 128],
                in_=w1_in[0].rearrange("(c p) f -> p c f", p=128)[
                    :, :, q * FQ * 128:(q + 1) * FQ * 128]),
                waits=w1_waits if q == 0 else [], inc="gd", inc_k=16))
        for q in range(4):
            w2m.append(op("gpsimd", lambda e, q=q: e.dma_start(
                out=w2s[:, q * (FC // 4) * D:(q + 1) * (FC // 4) * D],
                in_=w2_in[0].rearrange("(c p) f -> p c f", p=128)[
                    :, q * (FC // 4):(q + 1) * (FC // 4), :]),
                waits=w2_waits if q == 0 else [], inc="gd", inc_k=16))
        return (w1m, w2m)

    def expert_phase(pre=None):
        v_scat = cnt["gd"]
        tiles = [(ee, r0, rn) for ee in range(E) for (r0, rn) in chunks_of(CAP, 256)]
        import os
        _lim = int(os.environ.get("EXPERT_TILES", "0"))
        if _lim:
            tiles = tiles[:_lim]
        NTL = len(tiles)
        per_e = max(1, len(chunks_of(CAP, 256)))

        w1q = [[None] * 4 for _ in range(E)]
        w2q = [[None] * 4 for _ in range(E)]
        m_hlast = {}
        m_elast = {}
        FQ = FC // 4
        DQ = DC * FQ * 128

        def emit_w_loads(ee, pe_gate_h, pe_gate_e):
            for q in range(4):
                w1q[ee][q] = op("gpsimd", lambda e, ee=ee, q=q: e.dma_start(
                    out=w1s[:, 0:DC * DFF].rearrange(
                        "p (c f) -> p c f", f=DFF)[:, :, q * FQ * 128:(q + 1) * FQ * 128],
                    in_=w1_in[ee].rearrange("(c p) f -> p c f", p=128)[
                        :, :, q * FQ * 128:(q + 1) * FQ * 128]),
                    waits=[("pe", pe_gate_h)] if q == 0 else [],
                    inc="gd", inc_k=16)
            for q in range(4):
                w2q[ee][q] = op("gpsimd", lambda e, ee=ee, q=q: e.dma_start(
                    out=w2s[:, q * FQ * 8 * D // 8:0] if False else
                        w2s[:, q * FQ * D * 1:0] if False else
                        w2s[:, q * (FC // 4) * D:(q + 1) * (FC // 4) * D],
                    in_=w2_in[ee].rearrange("(c p) f -> p c f", p=128)[
                        :, q * (FC // 4):(q + 1) * (FC // 4), :]),
                    waits=[("pe", pe_gate_e)] if q == 0 else [],
                    inc="gd", inc_k=16)

        if pre is not None:
            w1q[0], w2q[0] = pre
        else:
            emit_w_loads(0, cnt["pe"], cnt["pe"])

        m_ld = {}
        m_dtrans = {}
        m_fmevac = {}
        m_eotrans = {}
        m_stgevA = {}
        m_stgev = {}
        m_store = {}
        m_relu = {}
        m_ev8 = {}

        def tinfo(T):
            ee, r0, rn = tiles[T]
            return ee, ee * CAP + r0, rn, chunks_of(rn, 128)

        def emit_disp_load(T):
            ee, base, rn, tcs = tinfo(T)
            w0 = []
            if T == 0:
                w0 = [("gd", v_scat), ("pe", cnt["pe"]),
                      ("dv", cnt["dv"]), ("ac", cnt["ac"])]
            elif T - 2 in m_dtrans:
                w0 = [("pe", m_dtrans[T - 2])]
            v = None
            for ti, (t0, tn) in enumerate(tcs):
                v = op("sync", lambda e, base=base, t0=t0, tn=tn, ti=ti, T=T:
                       e.dma_start(
                    out=disp_in_b[T % 2][0:tn, ti * D:(ti + 1) * D],
                    in_=disp[base + t0:base + t0 + tn, :]),
                    waits=w0 if ti == 0 else [], inc="ld", inc_k=16)
            m_ld[T] = v

        def emit_disp_trans(T):
            ee, base, rn, tcs = tinfo(T)
            p = T % 2
            for ti, (t0, tn) in enumerate(tcs):
                if ti == 0:
                    w0 = [("ld", m_ld[T]), ("dv", m_stgevA.get(T - 3, 0))]
                else:
                    w0 = [("dv", m_fmevac[T])]
                for dc in range(DC):
                    op("tensor", lambda e, ti=ti, dc=dc, tn=tn, p=p: e.transpose(
                        out=p_smb[0:128, dc * 128:dc * 128 + tn],
                        in_=disp_in_b[p][0:tn, ti * D + dc * 128:ti * D + (dc + 1) * 128],
                        identity=identb[:]),
                        waits=w0 if dc == 0 else [],
                        inc="pe" if dc == DC - 1 else None)
                vpet = cnt["pe"]
                m_fmevac[T] = op("vector", lambda e, ti=ti, tn=tn, rn=rn, p=p:
                                 e.tensor_copy(
                    out=disp_fm_b[p][:, 0:DC * rn].rearrange(
                        "p (dc f) -> p dc f", f=rn)[:, :, ti * 128:ti * 128 + tn],
                    in_=p_smb[:, 0:DC * 128].rearrange(
                        "p (dc f) -> p dc f", f=128)[:, :, 0:tn]),
                    waits=[("pe", vpet), ("pe", m_hlast.get(("t", T - 2), 0))],
                    inc="dv")
            m_dtrans[T] = cnt["pe"]

        def emit_h(T):
            ee, base, rn, tcs = tinfo(T)
            p = T % 2
            first_of_e = (T % per_e == 0)
            marks = []
            for ff in range(FC):
                pb = p_h0 if ff % 2 == 0 else p_h1
                w0 = []
                if ff == 0:
                    w0 = [("dv", m_fmevac[T]), ("ac", m_relu.get(T - 1, [0])[-1])]
                elif ff >= 2:
                    w0 = [("ac", marks[ff - 2])]
                if first_of_e and ff % FQ == 0:
                    w0 = w0 + [("gd", w1q[ee][ff // FQ])]
                for dc in range(DC):
                    op("tensor", lambda e, ff=ff, dc=dc, rn=rn, pb=pb, p=p: e.matmul(
                        out=pb[:, 0:rn],
                        lhsT=w1s[:, dc * DFF + ff * 128:dc * DFF + (ff + 1) * 128],
                        rhs=disp_fm_b[p][:, dc * rn:(dc + 1) * rn],
                        start=(dc == 0), stop=(dc == DC - 1)),
                        waits=w0 if dc == 0 else [],
                        inc="pe" if dc == DC - 1 else None)
                v_mm = cnt["pe"]
                marks.append(op("scalar", lambda e, ff=ff, rn=rn, pb=pb:
                                e.activation(
                    out=h_sb[:, ff * rn:(ff + 1) * rn], in_=pb[:, 0:rn],
                    func=AF.Relu), waits=[("pe", v_mm)], inc="ac"))
            m_relu[T] = marks
            m_hlast[("t", T)] = cnt["pe"]
            if (T + 1) % per_e == 0:
                m_hlast[ee] = cnt["pe"]
                if ee + 1 < E:
                    emit_w1_next(ee + 1)

        def emit_w1_next(ee):
            for q in range(4):
                w1q[ee][q] = op("gpsimd", lambda e, ee=ee, q=q: e.dma_start(
                    out=w1s[:, 0:DC * DFF].rearrange(
                        "p (c f) -> p c f", f=DFF)[:, :, q * FQ * 128:(q + 1) * FQ * 128],
                    in_=w1_in[ee].rearrange("(c p) f -> p c f", p=128)[
                        :, :, q * FQ * 128:(q + 1) * FQ * 128]),
                    waits=[("pe", m_hlast[ee - 1])] if q == 0 else [],
                    inc="gd", inc_k=16)

        def emit_w2_next(ee):
            for q in range(4):
                w2q[ee][q] = op("gpsimd", lambda e, ee=ee, q=q: e.dma_start(
                    out=w2s[:, q * (FC // 4) * D:(q + 1) * (FC // 4) * D],
                    in_=w2_in[ee].rearrange("(c p) f -> p c f", p=128)[
                        :, q * (FC // 4):(q + 1) * (FC // 4), :]),
                    waits=[("pe", m_elast[ee - 1])] if q == 0 else [],
                    inc="gd", inc_k=16)

        def emit_eout(T):
            ee, base, rn, tcs = tinfo(T)
            ev = []
            for dco in range(DC):
                pb = p_e0 if dco % 2 == 0 else p_e1
                w0 = []
                if dco == 0:
                    w0 = [("ac", m_relu[T][-1]), ("gd", w2q[ee][3]),
                          ("dv", m_ev8.get(T - 1, [0])[-1])]
                elif dco >= 2:
                    w0 = [("dv", ev[dco - 2])]
                for ff in range(FC):
                    op("tensor", lambda e, dco=dco, ff=ff, rn=rn, pb=pb: e.matmul(
                        out=pb[:, 0:rn],
                        lhsT=w2s[:, ff * D + dco * 128:ff * D + (dco + 1) * 128],
                        rhs=h_sb[:, ff * rn:(ff + 1) * rn],
                        start=(ff == 0), stop=(ff == FC - 1)),
                        waits=w0 if ff == 0 else [],
                        inc="pe" if ff == FC - 1 else None)
                v_mm = cnt["pe"]
                ev.append(op("vector", lambda e, rn=rn, pb=pb, dco=dco: e.tensor_copy(
                    out=eo_bf8[:, dco * 256:dco * 256 + rn], in_=pb[:, 0:rn]),
                    waits=[("pe", v_mm), ("pe", m_eotrans.get(T - 1, 0))],
                    inc="dv"))
            m_ev8[T] = ev
            m_elast[("t", T)] = cnt["pe"]
            if (T + 1) % per_e == 0:
                m_elast[ee] = cnt["pe"]
                if ee + 1 < E:
                    emit_w2_next(ee + 1)

        def emit_eout_trans(T):
            ee, base, rn, tcs = tinfo(T)
            ntc = len(tcs)
            half = DC // 2
            for b in range(2):
                bankv = p_smb if b == 0 else p_lgb
                dcos = list(range(b * half, (b + 1) * half))
                n_tr = len(dcos) * ntc
                i_tr = 0
                for dco in dcos:
                    for ti, (t0, tn) in enumerate(tcs):
                        w0 = []
                        if i_tr == 0:
                            w0 = [("dv", m_ev8[T][-1])]
                            if b == 0:
                                w0.append(("dv", m_fmevac.get(T + 2, 0)))
                        op("tensor", lambda e, dco=dco, ti=ti, tn=tn, bankv=bankv,
                           b=b, half=half: e.transpose(
                            out=bankv[0:tn, ((dco - b * half) * 2 + ti) * 128:
                                      ((dco - b * half) * 2 + ti) * 128 + 128],
                            in_=eo_bf8[:, dco * 256 + ti * 128:dco * 256 + ti * 128 + tn],
                            identity=identb[:]),
                            waits=w0,
                            inc="pe" if i_tr == n_tr - 1 else None)
                        i_tr += 1
                vpet = cnt["pe"]
                rm = min(rn, 128)
                bankv_cur = bankv
                m_stgev[T] = op("vector", lambda e, b=b, half=half, ntc=ntc,
                                rm=rm, bankv=bankv_cur: e.tensor_copy(
                    out=eo_stage[0:rm, 0:ntc * D].rearrange(
                        "p (ti dco j) -> p ti dco j", dco=DC, j=128)[
                        :, :, b * half:(b + 1) * half, :],
                    in_=bankv[0:rm, 0:half * ntc * 128].rearrange(
                        "p (dco ti j) -> p ti dco j", ti=ntc, j=128)),
                    waits=[("pe", vpet), ("ld", m_store.get(T - 1, 0))],
                    inc="dv")
                if b == 0:
                    m_stgevA[T] = m_stgev[T]
            m_eotrans[T] = cnt["pe"]
            v = None
            for ti, (t0, tn) in enumerate(tcs):
                v = op("sync", lambda e, base=base, t0=t0, tn=tn, ti=ti:
                       e.dma_start(
                    out=eout_d[base + t0:base + t0 + tn, :],
                    in_=eo_stage[0:tn, ti * D:(ti + 1) * D]),
                    waits=[("dv", m_stgev[T])] if ti == 0 else [],
                    inc="ld", inc_k=16)
            m_store[T] = v

        emit_disp_load(0)
        emit_disp_trans(0)
        if NTL > 1:
            emit_disp_load(1)
        for T in range(NTL):
            emit_h(T)
            if T + 1 < NTL:
                emit_disp_trans(T + 1)
            if T + 2 < NTL:
                emit_disp_load(T + 2)
            if T - 1 >= 0:
                emit_eout_trans(T - 1)
            emit_eout(T)
        emit_eout_trans(NTL - 1)

    cb_eo = (eo_g, thi,
             bigC[:, 4096:4096 + D], bigC[:, 4096 + D:4096 + 2 * D],
             zz, bigC[:, 2 * D:3 * D],
             bigC[:, 0:D], bigC[:, D:2 * D])
    cb_tw = (tt_g, wtmpb,
             bigC[:, 4096 + 2 * D:4096 + 3 * D],
             bigC[:, 4096 + 3 * D:4096 + 4 * D],
             big2[:, 0:D], big2[:, D:2 * D],
             big2[:, 2 * D:3 * D], big2[:, 3 * D:4 * D])
    cb_x = (xt, xt2,
            big1[:, 26624:26624 + 2048].bitcast(F32),
            big1[:, 26624 + 2048:26624 + 4096].bitcast(F32),
            big2[:, 24576:24576 + 2048].bitcast(F32),
            big2[:, 24576 + 2048:24576 + 4096].bitcast(F32),
            big1[:, 16384:16384 + 2048].bitcast(F32),
            big1[:, 16384 + 2048:16384 + 4096].bitcast(F32))
    cb_o = (tf, wtmp,
            eo_bf8[:, 0:2048].bitcast(F32),
            eo_stage[:, 0:2048].bitcast(F32),
            big2[:, 24576 + 4096:24576 + 6144].bitcast(F32),
            big2[:, 24576 + 6144:24576 + 8192].bitcast(F32),
            big1[:, 20480:20480 + 2048].bitcast(F32),
            big1[:, 20480 + 2048:20480 + 4096].bitcast(F32))

    def combine_tiles(xsrc, dst, tiles, cm, v_eo, cycle=(0, 1, 2, 3)):
        for k in tiles:
            p = cycle[cm.get("_i", 0) % len(cycle)]
            cm["_i"] = cm.get("_i", 0) + 1
            pw = cm.get(("p", p))
            vg = op("gpsimd", lambda e, k=k, p=p: e.indirect_dma_start(
                out=cb_eo[p][:, 0:D], out_offset=None,
                in_=eout_d,
                in_offset=IndirectOffsetOnAxis(flatIi[:, k:k + 1], 0)),
                waits=[("ld", v_eo), ("dv", pw[0] if pw else 0)],
                inc="gd", inc_k=16)
            vx = op("sync", lambda e, k=k, p=p: e.dma_start(
                out=cb_x[p][:], in_=xsrc[k * 128:(k + 1) * 128, :]),
                waits=([("dv", pw[1])] if pw else [("ld", v_eo)]),
                inc="ld", inc_k=16)
            vtw = op("gpsimd", lambda e, k=k, p=p: e.tensor_scalar(
                out=cb_tw[p][:], in0=cb_x[p][:], scalar1=rstdS[:, k:k + 1],
                scalar2=nmr[:, k:k + 1], op0=ALU.mult, op1=ALU.add),
                waits=[("ld", vx)] + ([("dv", pw[1])] if pw else []),
                inc="gd", inc_k=1)
            op("vector", lambda e, p=p: e.tensor_tensor(
                out=cb_o[p][:], in0=cb_eo[p][:, 0:D], in1=cb_tw[p][:],
                op=ALU.subtract),
                waits=[("gd", vg), ("gd", vtw), ("gd", pw[2] if pw else 0)])
            op("vector", lambda e, k=k, p=p: e.scalar_tensor_tensor(
                out=cb_o[p][:], in0=cb_o[p][:], scalar=keptS[:, k:k + 1],
                in1=cb_tw[p][:], op0=ALU.mult, op1=ALU.add))
            v6 = op("vector", lambda e, k=k, p=p: e.scalar_tensor_tensor(
                out=cb_o[p][:], in0=cb_o[p][:], scalar=sfacS[:, k:k + 1],
                in1=cb_x[p][:], op0=ALU.mult, op1=ALU.add), inc="dv")
            vs = op("gpsimd", lambda e, k=k, p=p, dst=dst: e.dma_start(
                out=dst[k * 128:(k + 1) * 128, :], in_=cb_o[p][:]),
                waits=[("dv", v6)], inc="gd", inc_k=16)
            cm[k] = cm[("p", p)] = (v6, v6, vs)

    def stats_tile(s, k, st, cm):
        srow = s * SEQ
        amark = st.setdefault(("am", s), {})
        x_t = xt if k % 2 == 0 else xt2
        w0 = [("gd", cm[s * SEQT + k][2])]
        w0 += ([("dv", st["v8"])] if k < 2 else
               [("ac", amark[k - 2][0]), ("dv", amark[k - 2][1])])
        v = op("sync", lambda e, k=k, srow=srow, x_t=x_t: e.dma_start(
            out=x_t[:], in_=xcur[srow + k * 128:srow + (k + 1) * 128, :]),
            waits=w0, inc="ld", inc_k=16)
        va = op("scalar", lambda e, k=k, x_t=x_t: e.activation(
            out=sq_scr[:], in_=x_t[:], func=AF.Square,
            accum_out=sscol[:, k:k + 1]),
            waits=[("ld", v)] + ([("dv", st["v7"])] if k == 0 else []),
            inc="ac")
        vd = op("vector", lambda e, k=k, x_t=x_t: e.reduce_sum(
            out=smcol[:, k:k + 1], in_=x_t[:], axis=AX.X),
            waits=[("ld", v)], inc="dv")
        st["v_sm"] = 0
        amark[k] = (va, vd)

    def attention_seq(s, st, cm):
        frs = chunks_of(SEQ, 512)
        srow = s * SEQ
        if True:
            v_ss = cnt["ac"]
            op("vector", lambda e: e.tensor_scalar(
                out=mucol[:, 0:SEQT], in0=smcol[:, 0:SEQT], scalar1=1.0 / D,
                scalar2=None, op0=ALU.mult),
                waits=[("ac", v_ss), ("gd", st["v_sm"])])
            op("vector", lambda e: e.tensor_scalar(
                out=scal[:, 0:SEQT], in0=sscol[:, 0:SEQT], scalar1=1.0 / D,
                scalar2=None, op0=ALU.mult))
            op("vector", lambda e: e.tensor_tensor(
                out=smcol[:, 0:SEQT], in0=mucol[:, 0:SEQT],
                in1=mucol[:, 0:SEQT], op=ALU.mult))
            v1a = op("vector", lambda e: e.tensor_tensor(
                out=scal[:, 0:SEQT], in0=scal[:, 0:SEQT], in1=smcol[:, 0:SEQT],
                op=ALU.subtract), inc="dv")
            v1b = op("vector", lambda e: e.tensor_scalar(
                out=smcol[:, 0:SEQT], in0=scal[:, 0:SEQT], scalar1=1e-5,
                scalar2=None, op0=ALU.add), inc="dv")
            va = op("scalar", lambda e: e.activation(
                out=smcol[:, 0:SEQT], in_=smcol[:, 0:SEQT], func=AF.Sqrt),
                waits=[("dv", v1b)], inc="ac")
            op("vector", lambda e: e.reciprocal(
                out=smcol[:, 0:SEQT], in_=smcol[:, 0:SEQT]), waits=[("ac", va)])
            op("vector", lambda e: e.tensor_tensor(
                out=scal[:, 0:SEQT], in0=scal[:, 0:SEQT], in1=smcol[:, 0:SEQT],
                op=ALU.mult))
            op("vector", lambda e: e.tensor_tensor(
                out=scal[:, 0:SEQT], in0=scal[:, 0:SEQT], in1=smcol[:, 0:SEQT],
                op=ALU.mult))
            v1c = op("vector", lambda e: e.tensor_scalar(
                out=scal[:, 0:SEQT], in0=scal[:, 0:SEQT], scalar1=float(D),
                scalar2=None, op0=ALU.mult), inc="dv")
            va = op("scalar", lambda e: e.activation(
                out=scal[:, 0:SEQT], in_=scal[:, 0:SEQT], func=AF.Sqrt),
                waits=[("dv", v1c)], inc="ac")
            op("vector", lambda e: e.tensor_scalar(
                out=scal[:, 0:SEQT], in0=scal[:, 0:SEQT], scalar1=1e-12,
                scalar2=None, op0=ALU.max), waits=[("ac", va)])
            op("vector", lambda e: e.reciprocal(
                out=scal[:, 0:SEQT], in_=scal[:, 0:SEQT]))
            op("vector", lambda e: e.tensor_scalar(
                out=scal[:, 0:SEQT], in0=scal[:, 0:SEQT],
                scalar1=float(math.sqrt(D)), scalar2=None, op0=ALU.mult))
            v7 = op("vector", lambda e: e.tensor_tensor(
                out=scal[:, 0:SEQT], in0=scal[:, 0:SEQT], in1=smcol[:, 0:SEQT],
                op=ALU.mult), inc="dv")
            st["v7"] = v7
            xn_ev = [st["stg"]]
            xmark2 = {}
            for k in range(SEQT):
                x_t = xt if k % 2 == 0 else xt2
                w0 = ([("dv", v7)] if k == 0 else
                      ([("dv", xmark2[k - 2])] if k - 2 in xmark2 else []))
                v = op("sync", lambda e, k=k, srow=srow, x_t=x_t: e.dma_start(
                    out=x_t[:], in_=xcur[srow + k * 128:srow + (k + 1) * 128, :]),
                    waits=w0, inc="ld", inc_k=16)
                op("vector", lambda e, k=k, x_t=x_t: e.tensor_scalar(
                    out=tf[:], in0=x_t[:], scalar1=mucol[:, k:k + 1],
                    scalar2=scal[:, k:k + 1], op0=ALU.subtract, op1=ALU.mult),
                    waits=[("ld", v), ("dv", v7)])
                op("vector", lambda e: e.tensor_tensor(
                    out=tf[:], in0=tf[:], in1=gam[:], op=ALU.mult))
                v8 = op("vector", lambda e: e.tensor_copy(out=thi[:], in_=tf[:]),
                        inc="dv")
                xmark2[k] = v8
                for dc in range(DC):
                    op("tensor", lambda e, dc=dc: e.transpose(
                        out=p_smb[:, dc * 128:(dc + 1) * 128],
                        in_=thi[:, dc * 128:(dc + 1) * 128], identity=identb[:]),
                        waits=[("dv", v8), xn_ev[0]] if dc == 0 else [],
                        inc="pe" if dc == DC - 1 else None)
                vpet = cnt["pe"]
                xn_ev[0] = ("dv", op("vector", lambda e, k=k: e.tensor_copy(
                    out=xn_fm[:, 0:DC * SEQ].rearrange(
                        "p (dc s) -> p dc s", s=SEQ)[:, :, k * 128:(k + 1) * 128],
                    in_=p_smb[:, 0:DC * 128].rearrange(
                        "p (dc f) -> p dc f", f=128)),
                    waits=[("pe", vpet)], inc="dv"))
            st["v8"] = xmark2[SEQT - 1]
            st["xnpe"] = cnt["pe"]
            v_xn = xn_ev[0]
            for g in range(C.GC):
                def qkva(oc, dest, first_w):
                    evs = []
                    for fi, (f0, fn_) in enumerate(frs):
                        tick = st["tick"]
                        st["tick"] = tick + 1
                        pb = p_q0 if tick % 2 == 0 else p_q1
                        w0 = [st["bank"][tick - 2]] if tick - 2 in st["bank"] \
                            else []
                        if fi == 0 and g == 0:
                            w0 = w0 + [v_xn]
                        for dc in range(DC):
                            op("tensor", lambda e, oc=oc, dc=dc, f0=f0, fn_=fn_,
                               pb=pb: e.matmul(
                                out=pb[:, 0:fn_],
                                lhsT=wT[:, (oc * DC + dc) * 128:(oc * DC + dc + 1) * 128],
                                rhs=xn_fm[:, dc * SEQ + f0:dc * SEQ + f0 + fn_],
                                start=(dc == 0), stop=(dc == DC - 1)),
                                waits=w0 if dc == 0 else [],
                                inc="pe" if dc == DC - 1 else None)
                        v_mm = cnt["pe"]
                        ev = dest(fi, f0, fn_, pb, v_mm,
                                  first_w if fi == 0 else [])
                        st["bank"][tick] = ev
                        evs.append(ev)
                    return evs

                def kv_ev(fi, f0, fn_, pb, v_mm, fw):
                    return ("ac", op("scalar", lambda e, f0=f0, fn_=fn_, pb=pb:
                                     e.activation(
                        out=kvf[:, f0:f0 + fn_], in_=pb[:, 0:fn_],
                        func=AF.Copy), waits=[("pe", v_mm)] + fw, inc="ac"))

                def a_ev(fi, f0, fn_, pb, v_mm, fw):
                    return ("ac", op("scalar", lambda e, f0=f0, fn_=fn_, pb=pb:
                                     e.activation(
                        out=af[:, f0:f0 + fn_], in_=pb[:, 0:fn_],
                        func=AF.Sigmoid), waits=[("pe", v_mm)] + fw, inc="ac"))

                def q_ev(fi, f0, fn_, pb, v_mm, fw):
                    return ("ac", op("scalar", lambda e, f0=f0, fn_=fn_, pb=pb:
                                     e.activation(
                        out=qb[:, f0:f0 + fn_], in_=pb[:, 0:fn_],
                        func=AF.Copy), waits=[("pe", v_mm)] + fw, inc="ac"))

                qkva(C.GC + g, kv_ev, [("dv", st["v9g"])])
                qkva(2 * C.GC + g, a_ev, [])
                v_a = cnt["ac"]
                op("vector", lambda e: e.tensor_tensor_scan(
                    out=yf[:, 0:SEQ], data0=af[:, 0:SEQ], data1=kvf[:, 0:SEQ],
                    initial=0.0, op0=ALU.mult, op1=ALU.add),
                    waits=[("ac", v_a)])
                qkva(g, q_ev, [])
                v_q = cnt["ac"]
                v9 = op("vector", lambda e: e.tensor_tensor(
                    out=of[:, 0:SEQ], in0=qb[:, 0:SEQ], in1=yf[:, 0:SEQ],
                    op=ALU.mult),
                    waits=[("ac", v_q), ("pe", st["oevg"])], inc="dv")
                st["v9g"] = v9
                o_ev = []
                for k in range(SEQT):
                    bk = p_sm[:, 0:128] if k % 2 == 0 else p_lgt[:, 0:128]
                    w0 = []
                    if k == 0:
                        w0 = [("dv", v9)]
                    elif k >= 2:
                        w0 = [("dv", o_ev[k - 2])]
                    vpe = op("tensor", lambda e, k=k, bk=bk: e.transpose(
                        out=bk, in_=of[:, k * 128:(k + 1) * 128],
                        identity=identf[:]), waits=w0, inc="pe")
                    o_ev.append(op("vector", lambda e, k=k, bk=bk: e.tensor_copy(
                        out=stg[:, k * 128:(k + 1) * 128], in_=bk),
                        waits=[("pe", vpe)] +
                              ([("gd", st["accum"])] if k == 0 else []),
                        inc="dv"))
                st["oevg"] = cnt["pe"]
                st["stg"] = ("dv", o_ev[-1])
                v10 = o_ev[-1]
                st["accum"] = op("gpsimd", lambda e, g=g, srow=srow: e.dma_start(
                    out=xcur[srow:srow + SEQ, g * 128:(g + 1) * 128].rearrange(
                        "(c p) f -> p c f", p=128),
                    in_=stg[:, 0:SEQ].rearrange("p (c f) -> p c f", f=128),
                    accum_op=ALU.add),
                    waits=[("dv", v10)], inc="gd", inc_k=16)
                st["inter"](s, g)

    def middle_phase(combine_only=False, no_inter=False, no_inter_stats=False):
        v_eo = cnt["ld"]
        wait("sync", "gd", cnt["gd"])
        v_wt = op("sync", lambda e: e.dma_start(out=wT, in_=wT_d),
                  waits=[("pe", cnt["pe"]), ("dv", cnt["dv"]),
                         ("gd", cnt["gd"])], inc="ld", inc_k=16)
        wait("tensor", "ld", v_wt)
        cm = {}
        st = {"v9g": 0, "oevg": 0, "accum": 0, "stg": ("dv", 0),
              "v8": 0, "v7": 0, "sc9": 0, "v_sm": 0, "bank": {}, "tick": 0}

        def inter(s, g):
            if no_inter:
                return
            s1 = s + 1
            if s1 >= C.NSEQ:
                return
            combine_tiles(xin, xcur,
                          range(s1 * SEQT + 2 * g, s1 * SEQT + 2 * g + 2),
                          cm, v_eo, cycle=(2, 3))
            if g >= 1 and not no_inter_stats:
                stats_tile(s1, 2 * g - 2, st, cm)
                stats_tile(s1, 2 * g - 1, st, cm)

        st["inter"] = inter
        if combine_only:
            combine_tiles(xin, xcur, range(RT), cm, v_eo, cycle=(2, 3))
            return
        combine_tiles(xin, xcur, range(SEQT), cm, v_eo, cycle=(2, 3))
        for k in range(SEQT):
            stats_tile(0, k, st, cm)
        for s in range(C.NSEQ):
            attention_seq(s, st, cm)
            if s + 1 < C.NSEQ:
                if no_inter:
                    combine_tiles(xin, xcur,
                                  range((s + 1) * SEQT, (s + 2) * SEQT),
                                  cm, v_eo, cycle=(2, 3))
                if no_inter or no_inter_stats:
                    for k in range(SEQT - 2):
                        stats_tile(s + 1, k, st, cm)
                stats_tile(s + 1, SEQT - 2, st, cm)
                stats_tile(s + 1, SEQT - 1, st, cm)

    pre_box = {}
    if "r1" in phases:
        routing_phase(xin, prefetch=lambda vg2: pre_box.update(
            m=emit_w0_prefetch([("gd", vg2), ("ld", v_wT_st)], [("gd", vg2)])))
    if "e1" in phases:
        expert_phase(pre=pre_box.pop("m", None))
    if "m" in phases:
        middle_phase()
    if "mq" in phases:
        middle_phase(combine_only=True)
    if "mn" in phases:
        middle_phase(no_inter=True)
    if "ms" in phases:
        middle_phase(no_inter_stats=True)
    if "r2" in phases:
        wait("sync", "gd", cnt["gd"])
        wait("sync", "ld", cnt["ld"])
        wait("gpsimd", "gd", cnt["gd"])
        wait("gpsimd", "ld", cnt["ld"])
        pe_r2 = cnt["pe"]
        routing_phase(xcur, prefetch=lambda vg2: pre_box.update(
            m=emit_w0_prefetch([("gd", vg2)], [("gd", vg2), ("pe", pe_r2)])))
    if "e2" in phases:
        expert_phase(pre=pre_box.pop("m", None))
    if "c2" in phases:
        wait("sync", "gd", cnt["gd"])
        combine_tiles(xcur, yout, range(RT), {}, cnt["ld"],
                      cycle=tuple(range(8)))
    if "dx" in phases:
        wait("gpsimd", "ld", cnt["ld"])
        wait("gpsimd", "gd", cnt["gd"])
        op("gpsimd", lambda e: e.dma_start(out=yout, in_=xcur),
           inc="gd", inc_k=16)
    if "dt" in phases:
        wait("gpsimd", "ld", cnt["ld"])
        wait("gpsimd", "gd", cnt["gd"])
        op("gpsimd", lambda e: e.dma_start(out=yout, in_=t_d),
           inc="gd", inc_k=16)
    if "de" in phases:
        wait("gpsimd", "ld", cnt["ld"])
        wait("gpsimd", "gd", cnt["gd"])
        op("gpsimd", lambda e: e.dma_start(
            out=yout[0:min(NT, E * CAP), :], in_=eout_d[0:min(NT, E * CAP), :]),
           inc="gd", inc_k=16)

    wait("sync", "dv", cnt["dv"])
    wait("gpsimd", "gd", cnt["gd"])
    wait("gpsimd", "ld", cnt["ld"])
    wait("sync", "ld", cnt["ld"])
    wait("sync", "gd", cnt["gd"])

    ctx.run()
    es.close()
    return nc


def _layout_inputs(C, x, wq, gamma, wr, w1, w2):
    gam_b = np.ascontiguousarray(
        np.tile(np.asarray(gamma, np.float32).reshape(1, C.D), (128, 1)))
    return dict(
        xin=np.ascontiguousarray(np.asarray(x, np.float32).reshape(C.NT, C.D)),
        wqkva=np.ascontiguousarray(np.asarray(wq, np.float32)),
        gammab=gam_b,
        wr_in=np.ascontiguousarray(np.asarray(wr, np.float32)),
        w1_in=np.ascontiguousarray(np.asarray(w1, np.float32)),
        w2_in=np.ascontiguousarray(np.asarray(w2, np.float32)),
    )


def kernel(L0_attn_x, L0_attn_W_qkva, L0_attn_gamma, L0_ffn_Wr, L0_ffn_W1,
           L0_ffn_W2):
    C = REAL
    nc = bass.Bass("TRN2", target_bir_lowering=False)
    build_program(nc, C)
    in_map = _layout_inputs(C, L0_attn_x, L0_attn_W_qkva, L0_attn_gamma,
                            L0_ffn_Wr, L0_ffn_W1, L0_ffn_W2)
    res = run_bass_kernel_spmd(nc, [in_map], core_ids=[0])
    return res.results[0]["yout"].reshape(16, 2048, 1024)


# version 14
# speedup vs baseline: 1.0660x; 1.0035x over previous
import math
import numpy as np
import concourse.bass as bass
import concourse.mybir as mybir
from concourse.bass import IndirectOffsetOnAxis
from concourse.bass_utils import run_bass_kernel_spmd

F32 = mybir.dt.float32
BF16 = mybir.dt.bfloat16
I32 = mybir.dt.int32
AF = mybir.ActivationFunctionType
ALU = mybir.AluOpType
AX = mybir.AxisListType

BIG = 1 << 20


class CFG:
    def __init__(self, ntok, d, dff, e, seq, cap):
        self.NT, self.D, self.DFF, self.E, self.SEQ, self.CAP = \
            ntok, d, dff, e, seq, cap
        self.NSEQ = ntok // seq
        self.DC = d // 128
        self.FC = dff // 128
        self.RT = ntok // 128
        self.OC = 3 * d // 128
        self.GC = d // 128
        assert ntok % 128 == 0 and d % 128 == 0 and dff % 128 == 0
        assert seq % 128 == 0 and ntok % seq == 0 and cap % 128 == 0


REAL = CFG(ntok=32768, d=1024, dff=4096, e=8, seq=2048, cap=5120)


def chunks_of(total, size):
    out, o = [], 0
    while o < total:
        out.append((o, min(size, total - o)))
        o += size
    return out


class Ctx:
    ENGINES = ("sync", "tensor", "vector", "scalar", "gpsimd")

    def __init__(self, nc):
        self.nc = nc
        self.streams = {e: [] for e in self.ENGINES}
        self.cnt = {}
        self.sems = {}
        for name in ("ld", "gd", "pe", "dv", "ac"):
            self.sems[name] = nc.alloc_semaphore(name)
            self.cnt[name] = 0

    AUTO = {"vector": "dv", "scalar": "ac"}
    SELFWAIT = {"vector"}

    def op(self, engine, build, waits=(), inc=None, inc_k=1):
        waits = list(waits)
        auto = self.AUTO.get(engine)
        if auto is not None:
            if engine in self.SELFWAIT:
                waits.append((auto, self.cnt[auto]))
            if inc is None:
                inc, inc_k = auto, 1
            assert inc == auto
        if inc is not None:
            self.cnt[inc] += inc_k
        sems = self.sems

        def fn(eng, build=build, waits=tuple(waits), inc=inc, inc_k=inc_k):
            best = {}
            for s, v in waits:
                if v > 0:
                    best[s] = max(best.get(s, 0), v)
            for s, v in best.items():
                eng.wait_ge(sems[s], v)
            ins = build(eng)
            if inc is not None and ins is not None:
                ins.then_inc(sems[inc], inc_k)
        self.streams[engine].append(fn)
        return self.cnt.get(inc) if inc is not None else None

    def wait(self, engine, name, val):
        sems = self.sems
        if val > 0:
            self.streams[engine].append(
                lambda eng, s=name, v=val: eng.wait_ge(sems[s], v))

    def run(self):
        with self.nc.Block() as block:
            @block.sync
            def _(eng):
                for fn in self.streams["sync"]:
                    fn(eng)

            @block.tensor
            def _(eng):
                for fn in self.streams["tensor"]:
                    fn(eng)

            @block.vector
            def _(eng):
                for fn in self.streams["vector"]:
                    fn(eng)

            @block.scalar
            def _(eng):
                for fn in self.streams["scalar"]:
                    fn(eng)

            @block.gpsimd
            def _(eng):
                for fn in self.streams["gpsimd"]:
                    fn(eng)


def build_program(nc, C, phases=("r1", "e1", "m", "r2", "e2", "c2")):
    NT, D, DFF, E, CAP, SEQ = C.NT, C.D, C.DFF, C.E, C.CAP, C.SEQ
    DC, FC, RT = C.DC, C.FC, C.RT

    xin = nc.dram_tensor("xin", [NT, D], F32, kind="ExternalInput").ap()
    wqkva = nc.dram_tensor("wqkva", [3 * D, D], F32, kind="ExternalInput").ap()
    gammab = nc.dram_tensor("gammab", [128, D], F32, kind="ExternalInput").ap()
    wr_in = nc.dram_tensor("wr_in", [D, E], F32, kind="ExternalInput").ap()
    w1_in = nc.dram_tensor("w1_in", [E, D, DFF], F32, kind="ExternalInput").ap()
    w2_in = nc.dram_tensor("w2_in", [E, DFF, D], F32, kind="ExternalInput").ap()
    yout = nc.dram_tensor("yout", [NT, D], F32, kind="ExternalOutput").ap()

    xcur = nc.dram_tensor("xcur", [NT, D], F32, kind="Internal").ap()
    t_d = nc.dram_tensor("t_d", [NT, D], BF16, kind="Internal").ap()
    disp = nc.dram_tensor("disp", [E * CAP + 1, D], BF16, kind="Internal").ap()
    eout_d = nc.dram_tensor("eout_d", [E * CAP, D], BF16, kind="Internal").ap()
    bases_d = nc.dram_tensor("bases_d", [1, RT * E], F32, kind="Internal").ap()
    wT_d = nc.dram_tensor("wT_d", [128, 3 * D * D // 128], BF16, kind="Internal").ap()

    ident_np = np.eye(128, dtype=np.float32)
    identb_d = nc.inline_tensor(ident_np, "identb_d").ap()
    identf_d = nc.inline_tensor(ident_np, "identf_d").ap()
    U_d = nc.inline_tensor(np.triu(np.ones((128, 128), np.float32)), "U_d").ap()
    iota8_d = nc.inline_tensor(
        np.tile(np.arange(E, dtype=np.float32), (128, 1)), "iota8_d").ap()
    ones_d = nc.inline_tensor(np.ones((128, 1), np.float32), "ones_d").ap()

    ctx = Ctx(nc)
    from contextlib import ExitStack
    es = ExitStack()

    def sb(name, shape, dtype):
        return es.enter_context(nc.sbuf_tensor(name, shape, dtype))

    def ps(name, shape, dtype):
        return es.enter_context(nc.psum_tensor(name, shape, dtype))

    identb = sb("identb", [128, 128], BF16)
    identf = sb("identf", [128, 128], F32)
    Ub = sb("Ub", [128, 128], BF16)
    iota8 = sb("iota8", [128, E], F32)
    ones8f = sb("ones8f", [128, E], F32)
    onesRT = sb("onesRT", [1, RT], F32)
    onesb = sb("onesb", [128, 1], BF16)
    gam = sb("gam", [128, D], F32)
    wrh = sb("wrh", [128, DC * E], BF16)
    wrl = sb("wrl", [128, DC * E], BF16)
    wrf = sb("wrf", [128, DC * E], F32)

    idxS = sb("idxS", [128, RT], F32)
    posS = sb("posS", [128, RT], F32)
    keptS = sb("keptS", [128, RT], F32)
    pmaxS = sb("pmaxS", [128, RT], F32)
    sfacS = pmaxS
    flatI = sb("flatI", [128, RT], F32)
    flatIi = sb("flatIi", [128, RT], I32)
    scatI = sb("scatI", [128, 2], I32)
    meanS = sb("meanS", [128, RT], F32)
    rstdS = sb("rstdS", [128, RT], F32)
    nmr = sb("nmr", [128, RT], F32)
    statsA = meanS
    statsB = rstdS
    wrhl = sb("wrhl", [128, DC * 2 * E], BF16)

    SEQT = SEQ // 128
    A1sz = max(DC * DFF, C.OC * DC * 128 + SEQ, 3 * E * RT * 2 + 2 * RT)
    big1 = sb("big1", [128, A1sz], BF16)
    A2sz = max(FC * D, DC * SEQ + 4 * 2 * SEQ)
    big2 = sb("big2", [128, A2sz], BF16)
    bb_off = ((3 * D + 4 * SEQT + 1) // 2) * 2
    A3sz = max(FC * 256, 2 * SEQ, bb_off + 2 * E * RT)
    bigC = sb("bigC", [128, A3sz], BF16)

    w1s = big1[:, 0:DC * DFF]
    w2s = big2[:, 0:FC * D]
    h_sb = bigC[:, 0:FC * 256]
    wT = big1[:, 0:C.OC * DC * 128]
    qb = big1[:, C.OC * DC * 128:C.OC * DC * 128 + SEQ]
    xn_fm = big2[:, 0:DC * SEQ]
    kvf = big2[:, DC * SEQ:DC * SEQ + 2 * SEQ].bitcast(F32)
    af = big2[:, DC * SEQ + 2 * SEQ:DC * SEQ + 4 * SEQ].bitcast(F32)
    yf = big2[:, DC * SEQ + 4 * SEQ:DC * SEQ + 6 * SEQ].bitcast(F32)
    of = big2[:, DC * SEQ + 6 * SEQ:DC * SEQ + 8 * SEQ].bitcast(F32)
    stg = bigC[:, 0:2 * SEQ].bitcast(F32)
    fmh = bigC[:, 0:D]
    fml = bigC[:, D:2 * D]
    tlo = bigC[:, 2 * D:3 * D]
    sscol = sb("sscol", [128, SEQT], F32)[:]
    smcol = sb("smcol", [128, SEQT], F32)[:]
    cnts = big1[0:1, 0:2 * E * RT].bitcast(F32)
    basesEM = big1[0:1, 2 * E * RT:4 * E * RT].bitcast(F32)
    basesTM = big1[0:1, 4 * E * RT:6 * E * RT].bitcast(F32)
    onesRT = big1[0:1, 6 * E * RT:6 * E * RT + 2 * RT].bitcast(F32)
    basesB = bigC[:, bb_off:bb_off + 2 * E * RT].bitcast(F32)

    xt = sb("xt", [128, D], F32)
    xt2 = sb("xt2", [128, D], F32)
    tf = sb("tf", [128, D], F32)
    thi = sb("thi", [128, D], BF16)
    rsc = sb("rsc", [128, 16 * E], F32)
    ohbf = sb("ohbf", [128, 2 * E], BF16)
    zz = sb("zz", [128, D], BF16)
    mark = sb("mark", [1, 1], F32)
    eo_g = sb("eo_g", [128, D], BF16)
    tt_g = sb("tt_g", [128, D], BF16)
    mucol = sb("mucol", [128, SEQT], F32)
    scal = sb("scal", [128, SEQT], F32)
    wtmp = sb("wtmp", [128, D], F32)
    wtmpb = sb("wtmpb", [128, D], BF16)
    disp_in_b = (xt[:, 0:D].bitcast(BF16), xt2[:, 0:D].bitcast(BF16))
    disp_fm_b = (tf[:, 0:D].bitcast(BF16), wtmp[:, 0:D].bitcast(BF16))
    eo_bf8 = sb("eo_bf8", [128, DC * 256], BF16)
    eo_stage = sb("eo_stage2", [128, 2 * D], BF16)
    sq_scr = tf
    dcomb = tf


    p_q0 = ps("p_q0", [128, 512], F32)
    p_q1 = ps("p_q1", [128, 512], F32)
    p_h0t = ps("p_h0t", [128, 256], F32)
    p_h1t = ps("p_h1t", [128, 256], F32)
    p_e0t = ps("p_e0t", [128, 256], F32)
    p_e1t = ps("p_e1t", [128, 256], F32)
    p_sm = ps("p_sm", [128, 512], F32)
    p_lgt = ps("p_lgt", [128, 512], F32)
    p_h0 = p_h0t[:, 0:256]
    p_h1 = p_h1t[:, 0:256]
    p_e0 = p_e0t[:, 0:256]
    p_e1 = p_e1t[:, 0:256]
    p_smb = p_sm[:, 0:512].bitcast(BF16)
    p_lgb = p_lgt[:, 0:512].bitcast(BF16)
    p_lgA = [p_lgt[:, 0:2 * E], p_lgt[:, 2 * E:4 * E]]
    p_lgB = [p_lgt[:, 4 * E:5 * E], p_lgt[:, 5 * E:6 * E]]
    p_cm = [p_lgt[:, 48:48 + E], p_lgt[:, 56:56 + E]]
    p_cs = [p_lgt[0:1, 64:64 + E], p_lgt[0:1, 72:72 + E]]

    op, wait, cnt = ctx.op, ctx.wait, ctx.cnt



    op("sync", lambda e: e.dma_start(out=identf[:], in_=identf_d), inc="ld", inc_k=16)
    op("sync", lambda e: e.dma_start(out=iota8[:], in_=iota8_d), inc="ld", inc_k=16)
    op("sync", lambda e: e.dma_start(out=gam[:], in_=gammab), inc="ld", inc_k=16)
    op("sync", lambda e: e.dma_start(
        out=wrf[:].rearrange("p (c e) -> p c e", e=E),
        in_=wr_in.rearrange("(c p) e -> p c e", p=128)),
        inc="ld", inc_k=16)
    op("gpsimd", lambda e: e.dma_start(out=identb[:], in_=identb_d), inc="gd", inc_k=16)
    op("gpsimd", lambda e: e.dma_start(out=Ub[:], in_=U_d), inc="gd", inc_k=16)
    op("gpsimd", lambda e: e.dma_start(out=onesb[:], in_=ones_d), inc="gd", inc_k=16)
    v_ld0, v_gd0 = cnt["ld"], cnt["gd"]
    op("vector", lambda e: e.tensor_copy(out=wrh[:], in_=wrf[:]),
       waits=[("ld", v_ld0), ("gd", v_gd0)])
    op("vector", lambda e: e.tensor_tensor(
        out=wrl[:], in0=wrf[:], in1=wrh[:], op=ALU.subtract))
    op("vector", lambda e: e.tensor_copy(
        out=wrhl[:].rearrange("p (c t e) -> p c t e", t=2, e=E)[:, :, 0, :],
        in_=wrh[:].rearrange("p (c e) -> p c e", e=E)))
    op("vector", lambda e: e.tensor_copy(
        out=wrhl[:].rearrange("p (c t e) -> p c t e", t=2, e=E)[:, :, 1, :],
        in_=wrl[:].rearrange("p (c e) -> p c e", e=E)))
    op("vector", lambda e: e.memset(zz[:], 0.0))
    op("vector", lambda e: e.memset(ones8f[:], 1.0))
    v_init_dv = op("vector", lambda e: e.memset(mark[:], 0.0), inc="dv")
    for r in range((E * CAP) // 128):
        op("gpsimd", lambda e, r=r: e.dma_start(
            out=disp[r * 128:(r + 1) * 128, :], in_=zz[:]),
            waits=[("dv", v_init_dv)] if r == 0 else [],
            inc="gd", inc_k=16)
    v_zero = cnt["gd"]

    wt_ev = 0
    for oc in range(C.OC):
        v = op("sync", lambda e, oc=oc: e.dma_start(
            out=wtmp[:], in_=wqkva[oc * 128:(oc + 1) * 128, :]),
            waits=[("pe", cnt["pe"])], inc="ld", inc_k=16)
        v_c = op("vector", lambda e: e.tensor_copy(out=wtmpb[:], in_=wtmp[:]),
                 waits=[("ld", v)], inc="dv")
        for dc in range(DC):
            op("tensor", lambda e, dc=dc: e.transpose(
                out=p_smb[:, dc * 128:(dc + 1) * 128],
                in_=wtmpb[:, dc * 128:(dc + 1) * 128], identity=identb[:]),
                waits=[("dv", v_c), ("dv", wt_ev)] if dc == 0 else [],
                inc="pe" if dc == DC - 1 else None)
        vpet = cnt["pe"]
        wt_ev = op("vector", lambda e, oc=oc: e.tensor_copy(
            out=wT[:, oc * DC * 128:(oc + 1) * DC * 128],
            in_=p_smb[:, 0:DC * 128]), waits=[("pe", vpet)], inc="dv")
    v_wT_dv = cnt["dv"]
    v_wT_st = op("sync", lambda e: e.dma_start(out=wT_d, in_=wT),
                 waits=[("dv", v_wT_dv)], inc="ld", inc_k=16)

    def routing_phase(xsrc, prefetch=None):
        xmark = {}
        for k in range(RT):
            x_t = xt if k % 2 == 0 else xt2
            w = [("ac", xmark[k - 2][0])] if k - 2 in xmark else []
            v = op("sync", lambda e, k=k, x_t=x_t: e.dma_start(
                out=x_t[:], in_=xsrc[k * 128:(k + 1) * 128, :]),
                waits=w, inc="ld", inc_k=16)
            va = op("scalar", lambda e, k=k, x_t=x_t: e.activation(
                out=sq_scr[:], in_=x_t[:], func=AF.Square,
                accum_out=statsB[:, k:k + 1]),
                waits=[("ld", v)], inc="ac")
            vd = op("scalar", lambda e, k=k, x_t=x_t: e.activation(
                out=sq_scr[:], in_=x_t[:], func=AF.Copy,
                accum_out=statsA[:, k:k + 1]),
                waits=[("ld", v)], inc="ac")
            xmark[k] = (vd, vd)
        v_stats_ac, v_stats_dv = cnt["ac"], cnt["dv"]
        op("vector", lambda e: e.tensor_scalar(
            out=meanS[:, 0:RT], in0=statsA[:, 0:RT], scalar1=1.0 / D,
            scalar2=None, op0=ALU.mult), waits=[("ac", v_stats_ac)])
        op("vector", lambda e: e.tensor_scalar(
            out=rstdS[:, 0:RT], in0=statsB[:, 0:RT], scalar1=1.0 / D,
            scalar2=None, op0=ALU.mult))
        op("vector", lambda e: e.tensor_tensor(
            out=posS[:, 0:RT], in0=meanS[:, 0:RT], in1=meanS[:, 0:RT],
            op=ALU.mult))
        op("vector", lambda e: e.tensor_tensor(
            out=rstdS[:, 0:RT], in0=rstdS[:, 0:RT], in1=posS[:, 0:RT],
            op=ALU.subtract))
        v1 = op("vector", lambda e: e.tensor_scalar(
            out=rstdS[:, 0:RT], in0=rstdS[:, 0:RT], scalar1=1e-5, scalar2=None,
            op0=ALU.add), inc="dv")
        va = op("scalar", lambda e: e.activation(
            out=rstdS[:, 0:RT], in_=rstdS[:, 0:RT], func=AF.Sqrt),
            waits=[("dv", v1)], inc="ac")
        op("vector", lambda e: e.reciprocal(
            out=rstdS[:, 0:RT], in_=rstdS[:, 0:RT]), waits=[("ac", va)])
        v_rstd = op("vector", lambda e: e.scalar_tensor_tensor(
            out=nmr[:, 0:RT], in0=meanS[:, 0:RT], scalar=-1.0,
            in1=rstdS[:, 0:RT], op0=ALU.mult, op1=ALU.mult), inc="dv")

        marks = {}
        for k in range(RT):
            p = k % 2
            x_t = xt if p == 0 else xt2
            pw = marks.get(k - 2, {})
            pm = marks.get(k - 1, {})
            v = op("sync", lambda e, k=k, x_t=x_t: e.dma_start(
                out=x_t[:], in_=xsrc[k * 128:(k + 1) * 128, :]),
                waits=[("gd", pw["tmp"])] if pw
                else [("ac", v_stats_ac)], inc="ld", inc_k=16)
            m = {}
            tfp = tf if p == 0 else wtmp
            m["tmp"] = op("gpsimd", lambda e, k=k, x_t=x_t, tfp=tfp:
                          e.tensor_scalar(
                out=tfp[:], in0=x_t[:], scalar1=rstdS[:, k:k + 1],
                scalar2=nmr[:, k:k + 1], op0=ALU.mult, op1=ALU.add),
                waits=[("ld", v), ("ac", pw.get("thi", 0)),
                       ("gd", pw.get("tlo", 0))] +
                      ([("dv", v_rstd)] if k < 2 else []),
                inc="gd", inc_k=1)
            m["thi"] = v_thi = op("scalar", lambda e, tfp=tfp: e.activation(
                out=thi[:], in_=tfp[:], func=AF.Copy),
                waits=[("gd", m["tmp"]), ("pe", pm.get("vpeh", 0)),
                       ("gd", max(pm.get("tstore", 0), pm.get("tlo", 0)))],
                inc="ac")
            m["tstore"] = op("gpsimd", lambda e, k=k: e.dma_start(
                out=t_d[k * 128:(k + 1) * 128, :], in_=thi[:]),
                waits=[("ac", v_thi)], inc="gd", inc_k=16)
            m["tlo"] = v_tlo = op("gpsimd", lambda e, tfp=tfp: e.tensor_tensor(
                out=tlo[:], in0=tfp[:], in1=thi[:], op=ALU.subtract),
                waits=[("ac", v_thi), ("pe", pm.get("vpel", 0))],
                inc="gd", inc_k=1)
            for dc in range(DC):
                op("tensor", lambda e, dc=dc: e.transpose(
                    out=p_smb[:, dc * 128:(dc + 1) * 128],
                    in_=thi[:, dc * 128:(dc + 1) * 128], identity=identb[:]),
                    waits=[("ac", v_thi),
                           ("dv", pm.get("fml", v_wT_dv if k == 0 else 0))]
                    if dc == 0 else [],
                    inc="pe" if dc == DC - 1 else None)
            m["vpeh"] = vpeh = cnt["pe"]
            v_fmh = op("scalar", lambda e: e.activation(
                out=fmh[:, 0:D], in_=p_smb[:, 0:D], func=AF.Copy),
                waits=[("pe", vpeh), ("pe", pm.get("lgmm", 0))], inc="ac")
            for dc in range(DC):
                op("tensor", lambda e, dc=dc: e.transpose(
                    out=p_smb[:, dc * 128:(dc + 1) * 128],
                    in_=tlo[:, dc * 128:(dc + 1) * 128], identity=identb[:]),
                    waits=[("gd", v_tlo), ("ac", v_fmh)] if dc == 0 else [],
                    inc="pe" if dc == DC - 1 else None)
            m["vpel"] = vpel = cnt["pe"]
            m["fml"] = v_fm = op("vector", lambda e: e.tensor_copy(
                out=fml[:, 0:D], in_=p_smb[:, 0:D]),
                waits=[("pe", vpel), ("pe", pm.get("lgmm", 0))], inc="dv")
            for dc in range(DC):
                op("tensor", lambda e, dc=dc, p=p: e.matmul(
                    out=p_lgA[p], lhsT=fmh[:, dc * 128:(dc + 1) * 128],
                    rhs=wrhl[:, dc * 2 * E:(dc + 1) * 2 * E],
                    start=(dc == 0), stop=(dc == DC - 1)),
                    waits=([("ac", v_fmh), ("dv", pw.get("lgread", 0))]
                           if dc == 0 else []),
                    inc="pe" if dc == DC - 1 else None)
            for dc in range(DC):
                op("tensor", lambda e, dc=dc, p=p: e.matmul(
                    out=p_lgB[p], lhsT=fml[:, dc * 128:(dc + 1) * 128],
                    rhs=wrh[:, dc * E:(dc + 1) * E],
                    start=(dc == 0), stop=(dc == DC - 1)),
                    waits=[("dv", v_fm)] if dc == 0 else [],
                    inc="pe" if dc == DC - 1 else None)
            m["lgmm"] = v_lgmm = cnt["pe"]
            rb = rsc[:, p * 8 * E:(p + 1) * 8 * E]
            lg = rb[:, 0:E]
            mx = rb[:, E:E + 1]
            s2 = rb[:, E + 1:E + 2]
            eq = rb[:, 2 * E:3 * E]
            pr = rb[:, 3 * E:4 * E]
            oh = rb[:, 4 * E:5 * E]
            ex = rb[:, 5 * E:6 * E]
            ohb = ohbf[:, p * E:(p + 1) * E]
            op("vector", lambda e, lg=lg, p=p: e.tensor_copy(
                out=lg, in_=p_lgA[p][:, 0:E]), waits=[("pe", v_lgmm)])
            op("vector", lambda e, lg=lg, p=p: e.tensor_tensor(
                out=lg, in0=lg, in1=p_lgA[p][:, E:2 * E], op=ALU.add))
            m["lgread"] = op("vector", lambda e, lg=lg, p=p: e.tensor_tensor(
                out=lg, in0=lg, in1=p_lgB[p], op=ALU.add), inc="dv")
            op("vector", lambda e, lg=lg, mx=mx: e.reduce_max(
                out=mx, in_=lg, axis=AX.X))
            op("vector", lambda e, lg=lg, mx=mx, eq=eq: e.tensor_scalar(
                out=eq, in0=lg, scalar1=mx, scalar2=None, op0=ALU.is_ge))
            op("vector", lambda e, eq=eq, pr=pr: e.tensor_tensor_scan(
                out=pr, data0=ones8f[:], data1=eq, initial=0.0,
                op0=ALU.mult, op1=ALU.add))
            op("vector", lambda e, pr=pr: e.tensor_scalar(
                out=pr, in0=pr, scalar1=1.0, scalar2=None, op0=ALU.is_le))
            op("vector", lambda e, eq=eq, pr=pr, oh=oh: e.tensor_tensor(
                out=oh, in0=eq, in1=pr, op=ALU.mult))
            op("vector", lambda e, oh=oh, ex=ex: e.tensor_tensor(
                out=ex, in0=oh, in1=iota8[:], op=ALU.mult))
            op("vector", lambda e, k=k, ex=ex: e.reduce_sum(
                out=idxS[:, k:k + 1], in_=ex, axis=AX.X))
            v2 = op("vector", lambda e, lg=lg, mx=mx, ex=ex: e.tensor_scalar(
                out=ex, in0=lg, scalar1=mx, scalar2=None, op0=ALU.subtract),
                inc="dv")
            va = op("scalar", lambda e, ex=ex: e.activation(
                out=ex, in_=ex, func=AF.Exp), waits=[("dv", v2)], inc="ac")
            op("vector", lambda e, ex=ex, s2=s2: e.reduce_sum(
                out=s2, in_=ex, axis=AX.X), waits=[("ac", va)])
            op("vector", lambda e, k=k, s2=s2: e.reciprocal(
                out=pmaxS[:, k:k + 1], in_=s2))
            v3 = op("vector", lambda e, oh=oh, ohb=ohb: e.tensor_copy(
                out=ohb, in_=oh), inc="dv")
            vcm = op("tensor", lambda e, p=p, ohb=ohb: e.matmul(
                out=p_cm[p], lhsT=Ub[:], rhs=ohb, start=True, stop=True),
                waits=[("dv", v3), ("dv", pw.get("cmread", 0)),
                       ("dv", pw.get("csread", 0))], inc="pe")
            vcs = op("tensor", lambda e, p=p, ohb=ohb: e.matmul(
                out=p_cs[p][:], lhsT=onesb[:], rhs=ohb, start=True, stop=True),
                inc="pe")
            m["csread"] = op("vector", lambda e, k=k, p=p: e.tensor_copy(
                out=cnts[0:1, k:k + (E - 1) * RT + 1:RT], in_=p_cs[p]),
                waits=[("pe", vcs)], inc="dv")
            m["cmread"] = op("vector", lambda e, ex=ex, p=p: e.tensor_copy(
                out=ex, in_=p_cm[p]), waits=[("pe", vcm)], inc="dv")
            op("vector", lambda e, ex=ex, oh=oh: e.tensor_tensor(
                out=ex, in0=ex, in1=oh, op=ALU.mult))
            m["tdone"] = op("vector", lambda e, k=k, ex=ex: e.reduce_sum(
                out=posS[:, k:k + 1], in_=ex, axis=AX.X), inc="dv")
            marks[k] = m
        v_passA = cnt["dv"]

        op("vector", lambda e: e.memset(onesRT[0:1, 0:RT], 1.0))
        for ee in range(E):
            op("vector", lambda e, ee=ee: e.tensor_tensor_scan(
                out=basesEM[0:1, ee * RT:(ee + 1) * RT],
                data0=onesRT[0:1, 0:RT],
                data1=cnts[0:1, ee * RT:(ee + 1) * RT],
                initial=0.0, op0=ALU.mult, op1=ALU.add),
                waits=[("dv", v_passA)] if ee == 0 else [])
        op("vector", lambda e: e.tensor_tensor(
            out=basesEM[0:1, :], in0=basesEM[0:1, :], in1=cnts[0:1, :],
            op=ALU.subtract))
        v_b = op("vector", lambda e: e.tensor_copy(
            out=basesTM[0:1, :].rearrange("p (k e) -> p k e", e=E),
            in_=basesEM[0:1, :].rearrange("p (e k) -> p k e", e=E)),
            inc="dv")
        vg1 = op("gpsimd", lambda e: e.dma_start(
            out=bases_d, in_=basesTM[0:1, :]),
            waits=[("dv", v_b)], inc="gd", inc_k=16)
        vg2 = op("gpsimd", lambda e: e.dma_start(
            out=basesB[:], in_=bases_d.to_broadcast([128, RT * E])),
            inc="gd", inc_k=16)
        if prefetch is not None:
            prefetch(vg2)

        oh2 = rsc[:, 0:E]
        bsl = rsc[:, E:2 * E]
        sc1 = rsc[:, 2 * E:2 * E + 1]
        sc2 = rsc[:, 2 * E + 1:2 * E + 2]
        op("vector", lambda e: e.tensor_scalar(
            out=flatI[:, 0:RT], in0=idxS[:, 0:RT], scalar1=float(CAP),
            scalar2=None, op0=ALU.mult),
            waits=[("dv", v_passA), ("gd", vg2)])
        tt_bufs = (tt_g, wtmpb)
        scat_gate = cnt["gd"]
        smark = {}
        for k in range(RT):
            p = k % 2
            op("vector", lambda e, k=k: e.tensor_scalar(
                out=oh2, in0=iota8[:], scalar1=idxS[:, k:k + 1],
                scalar2=None, op0=ALU.is_equal))
            op("vector", lambda e, k=k: e.tensor_tensor(
                out=bsl, in0=oh2, in1=basesB[:, k * E:(k + 1) * E],
                op=ALU.mult))
            op("vector", lambda e: e.reduce_sum(out=sc1, in_=bsl, axis=AX.X))
            op("vector", lambda e, k=k: e.scalar_tensor_tensor(
                out=posS[:, k:k + 1], in0=sc1, scalar=1.0,
                in1=posS[:, k:k + 1], op0=ALU.subtract, op1=ALU.add))
            op("vector", lambda e, k=k: e.tensor_scalar(
                out=keptS[:, k:k + 1], in0=posS[:, k:k + 1],
                scalar1=float(CAP), scalar2=None, op0=ALU.is_lt))
            op("vector", lambda e, k=k: e.scalar_tensor_tensor(
                out=flatI[:, k:k + 1], in0=posS[:, k:k + 1],
                scalar=float(CAP - 1), in1=flatI[:, k:k + 1],
                op0=ALU.min, op1=ALU.add))
            op("vector", lambda e, k=k: e.tensor_copy(
                out=flatIi[:, k:k + 1], in_=flatI[:, k:k + 1]))
            op("vector", lambda e, k=k: e.scalar_tensor_tensor(
                out=sc2, in0=flatI[:, k:k + 1], scalar=float(E * CAP),
                in1=keptS[:, k:k + 1], op0=ALU.subtract, op1=ALU.mult))
            op("vector", lambda e: e.tensor_scalar(
                out=sc1, in0=sc2, scalar1=float(E * CAP), scalar2=None,
                op0=ALU.add))
            v5 = op("vector", lambda e, k=k, p=p: e.tensor_copy(
                out=scatI[:, p:p + 1], in_=sc1),
                waits=[("gd", smark.get(k - 2, scat_gate))], inc="dv")
            vt = op("sync", lambda e, k=k, p=p: e.dma_start(
                out=tt_bufs[p][:], in_=t_d[k * 128:(k + 1) * 128, :]),
                waits=[("gd", smark.get(k - 2, scat_gate))] +
                      ([("dv", v_wT_dv)] if k < 2 else []),
                inc="ld", inc_k=16)
            smark[k] = op("gpsimd", lambda e, k=k, p=p: e.indirect_dma_start(
                out=disp,
                out_offset=IndirectOffsetOnAxis(scatI[:, p:p + 1], 0),
                in_=tt_bufs[p][:], in_offset=None),
                waits=[("dv", v5), ("ld", vt)], inc="gd", inc_k=16)
        op("vector", lambda e: e.tensor_scalar(
            out=sfacS[:, 0:RT], in0=pmaxS[:, 0:RT], scalar1=1.0, scalar2=None,
            op0=ALU.subtract))
        op("vector", lambda e: e.tensor_tensor(
            out=sfacS[:, 0:RT], in0=sfacS[:, 0:RT], in1=keptS[:, 0:RT],
            op=ALU.mult))
        op("vector", lambda e: e.tensor_scalar(
            out=sfacS[:, 0:RT], in0=sfacS[:, 0:RT], scalar1=1.0, scalar2=0.5,
            op0=ALU.add, op1=ALU.mult), inc="dv")

    def emit_w0_prefetch(w1_waits, w2_waits):
        FQ = (DFF // 128) // 4
        w1m, w2m = [], []
        for q in range(4):
            w1m.append(op("gpsimd", lambda e, q=q: e.dma_start(
                out=w1s[:, 0:DC * DFF].rearrange(
                    "p (c f) -> p c f", f=DFF)[:, :, q * FQ * 128:(q + 1) * FQ * 128],
                in_=w1_in[0].rearrange("(c p) f -> p c f", p=128)[
                    :, :, q * FQ * 128:(q + 1) * FQ * 128]),
                waits=w1_waits if q == 0 else [], inc="gd", inc_k=16))
        for q in range(4):
            w2m.append(op("gpsimd", lambda e, q=q: e.dma_start(
                out=w2s[:, q * (FC // 4) * D:(q + 1) * (FC // 4) * D],
                in_=w2_in[0].rearrange("(c p) f -> p c f", p=128)[
                    :, q * (FC // 4):(q + 1) * (FC // 4), :]),
                waits=w2_waits if q == 0 else [], inc="gd", inc_k=16))
        return (w1m, w2m)

    def expert_phase(pre=None):
        v_scat = cnt["gd"]
        tiles = [(ee, r0, rn) for ee in range(E) for (r0, rn) in chunks_of(CAP, 256)]
        import os
        _lim = int(os.environ.get("EXPERT_TILES", "0"))
        if _lim:
            tiles = tiles[:_lim]
        NTL = len(tiles)
        per_e = max(1, len(chunks_of(CAP, 256)))

        w1q = [[None] * 4 for _ in range(E)]
        w2q = [[None] * 4 for _ in range(E)]
        m_hlast = {}
        m_elast = {}
        FQ = FC // 4
        DQ = DC * FQ * 128

        def emit_w_loads(ee, pe_gate_h, pe_gate_e):
            for q in range(4):
                w1q[ee][q] = op("gpsimd", lambda e, ee=ee, q=q: e.dma_start(
                    out=w1s[:, 0:DC * DFF].rearrange(
                        "p (c f) -> p c f", f=DFF)[:, :, q * FQ * 128:(q + 1) * FQ * 128],
                    in_=w1_in[ee].rearrange("(c p) f -> p c f", p=128)[
                        :, :, q * FQ * 128:(q + 1) * FQ * 128]),
                    waits=[("pe", pe_gate_h)] if q == 0 else [],
                    inc="gd", inc_k=16)
            for q in range(4):
                w2q[ee][q] = op("gpsimd", lambda e, ee=ee, q=q: e.dma_start(
                    out=w2s[:, q * FQ * 8 * D // 8:0] if False else
                        w2s[:, q * FQ * D * 1:0] if False else
                        w2s[:, q * (FC // 4) * D:(q + 1) * (FC // 4) * D],
                    in_=w2_in[ee].rearrange("(c p) f -> p c f", p=128)[
                        :, q * (FC // 4):(q + 1) * (FC // 4), :]),
                    waits=[("pe", pe_gate_e)] if q == 0 else [],
                    inc="gd", inc_k=16)

        if pre is not None:
            w1q[0], w2q[0] = pre
        else:
            emit_w_loads(0, cnt["pe"], cnt["pe"])

        m_ld = {}
        m_dtrans = {}
        m_fmevac = {}
        m_eotrans = {}
        m_stgevA = {}
        m_stgev = {}
        m_store = {}
        m_relu = {}
        m_ev8 = {}

        def tinfo(T):
            ee, r0, rn = tiles[T]
            return ee, ee * CAP + r0, rn, chunks_of(rn, 128)

        def emit_disp_load(T):
            ee, base, rn, tcs = tinfo(T)
            w0 = []
            if T == 0:
                w0 = [("gd", v_scat), ("pe", cnt["pe"]),
                      ("dv", cnt["dv"]), ("ac", cnt["ac"])]
            elif T - 2 in m_dtrans:
                w0 = [("pe", m_dtrans[T - 2])]
            v = None
            for ti, (t0, tn) in enumerate(tcs):
                v = op("sync", lambda e, base=base, t0=t0, tn=tn, ti=ti, T=T:
                       e.dma_start(
                    out=disp_in_b[T % 2][0:tn, ti * D:(ti + 1) * D],
                    in_=disp[base + t0:base + t0 + tn, :]),
                    waits=w0 if ti == 0 else [], inc="ld", inc_k=16)
            m_ld[T] = v

        def emit_disp_trans(T):
            ee, base, rn, tcs = tinfo(T)
            p = T % 2
            for ti, (t0, tn) in enumerate(tcs):
                if ti == 0:
                    w0 = [("ld", m_ld[T]), ("dv", m_stgevA.get(T - 3, 0))]
                else:
                    w0 = [("dv", m_fmevac[T])]
                for dc in range(DC):
                    op("tensor", lambda e, ti=ti, dc=dc, tn=tn, p=p: e.transpose(
                        out=p_smb[0:128, dc * 128:dc * 128 + tn],
                        in_=disp_in_b[p][0:tn, ti * D + dc * 128:ti * D + (dc + 1) * 128],
                        identity=identb[:]),
                        waits=w0 if dc == 0 else [],
                        inc="pe" if dc == DC - 1 else None)
                vpet = cnt["pe"]
                m_fmevac[T] = op("vector", lambda e, ti=ti, tn=tn, rn=rn, p=p:
                                 e.tensor_copy(
                    out=disp_fm_b[p][:, 0:DC * rn].rearrange(
                        "p (dc f) -> p dc f", f=rn)[:, :, ti * 128:ti * 128 + tn],
                    in_=p_smb[:, 0:DC * 128].rearrange(
                        "p (dc f) -> p dc f", f=128)[:, :, 0:tn]),
                    waits=[("pe", vpet), ("pe", m_hlast.get(("t", T - 2), 0))],
                    inc="dv")
            m_dtrans[T] = cnt["pe"]

        def emit_h(T):
            ee, base, rn, tcs = tinfo(T)
            p = T % 2
            first_of_e = (T % per_e == 0)
            marks = []
            for ff in range(FC):
                pb = p_h0 if ff % 2 == 0 else p_h1
                w0 = []
                if ff == 0:
                    w0 = [("dv", m_fmevac[T]), ("ac", m_relu.get(T - 1, [0])[-1])]
                elif ff >= 2:
                    w0 = [("ac", marks[ff - 2])]
                if first_of_e and ff % FQ == 0:
                    w0 = w0 + [("gd", w1q[ee][ff // FQ])]
                for dc in range(DC):
                    op("tensor", lambda e, ff=ff, dc=dc, rn=rn, pb=pb, p=p: e.matmul(
                        out=pb[:, 0:rn],
                        lhsT=w1s[:, dc * DFF + ff * 128:dc * DFF + (ff + 1) * 128],
                        rhs=disp_fm_b[p][:, dc * rn:(dc + 1) * rn],
                        start=(dc == 0), stop=(dc == DC - 1)),
                        waits=w0 if dc == 0 else [],
                        inc="pe" if dc == DC - 1 else None)
                v_mm = cnt["pe"]
                marks.append(op("scalar", lambda e, ff=ff, rn=rn, pb=pb:
                                e.activation(
                    out=h_sb[:, ff * rn:(ff + 1) * rn], in_=pb[:, 0:rn],
                    func=AF.Relu), waits=[("pe", v_mm)], inc="ac"))
            m_relu[T] = marks
            m_hlast[("t", T)] = cnt["pe"]
            if (T + 1) % per_e == 0:
                m_hlast[ee] = cnt["pe"]
                if ee + 1 < E:
                    emit_w1_next(ee + 1)

        def emit_w1_next(ee):
            for q in range(4):
                w1q[ee][q] = op("gpsimd", lambda e, ee=ee, q=q: e.dma_start(
                    out=w1s[:, 0:DC * DFF].rearrange(
                        "p (c f) -> p c f", f=DFF)[:, :, q * FQ * 128:(q + 1) * FQ * 128],
                    in_=w1_in[ee].rearrange("(c p) f -> p c f", p=128)[
                        :, :, q * FQ * 128:(q + 1) * FQ * 128]),
                    waits=[("pe", m_hlast[ee - 1])] if q == 0 else [],
                    inc="gd", inc_k=16)

        def emit_w2_next(ee):
            for q in range(4):
                w2q[ee][q] = op("gpsimd", lambda e, ee=ee, q=q: e.dma_start(
                    out=w2s[:, q * (FC // 4) * D:(q + 1) * (FC // 4) * D],
                    in_=w2_in[ee].rearrange("(c p) f -> p c f", p=128)[
                        :, q * (FC // 4):(q + 1) * (FC // 4), :]),
                    waits=[("pe", m_elast[ee - 1])] if q == 0 else [],
                    inc="gd", inc_k=16)

        def emit_eout(T):
            ee, base, rn, tcs = tinfo(T)
            ev = []
            for dco in range(DC):
                pb = p_e0 if dco % 2 == 0 else p_e1
                w0 = []
                if dco == 0:
                    w0 = [("ac", m_relu[T][-1]), ("gd", w2q[ee][3]),
                          ("dv", m_ev8.get(T - 1, [0])[-1])]
                elif dco >= 2:
                    w0 = [("dv", ev[dco - 2])]
                for ff in range(FC):
                    op("tensor", lambda e, dco=dco, ff=ff, rn=rn, pb=pb: e.matmul(
                        out=pb[:, 0:rn],
                        lhsT=w2s[:, ff * D + dco * 128:ff * D + (dco + 1) * 128],
                        rhs=h_sb[:, ff * rn:(ff + 1) * rn],
                        start=(ff == 0), stop=(ff == FC - 1)),
                        waits=w0 if ff == 0 else [],
                        inc="pe" if ff == FC - 1 else None)
                v_mm = cnt["pe"]
                ev.append(op("vector", lambda e, rn=rn, pb=pb, dco=dco: e.tensor_copy(
                    out=eo_bf8[:, dco * 256:dco * 256 + rn], in_=pb[:, 0:rn]),
                    waits=[("pe", v_mm), ("pe", m_eotrans.get(T - 1, 0))],
                    inc="dv"))
            m_ev8[T] = ev
            m_elast[("t", T)] = cnt["pe"]
            if (T + 1) % per_e == 0:
                m_elast[ee] = cnt["pe"]
                if ee + 1 < E:
                    emit_w2_next(ee + 1)

        def emit_eout_trans(T):
            ee, base, rn, tcs = tinfo(T)
            ntc = len(tcs)
            half = DC // 2
            for b in range(2):
                bankv = p_smb if b == 0 else p_lgb
                dcos = list(range(b * half, (b + 1) * half))
                n_tr = len(dcos) * ntc
                i_tr = 0
                for dco in dcos:
                    for ti, (t0, tn) in enumerate(tcs):
                        w0 = []
                        if i_tr == 0:
                            w0 = [("dv", m_ev8[T][-1])]
                            if b == 0:
                                w0.append(("dv", m_fmevac.get(T + 2, 0)))
                        op("tensor", lambda e, dco=dco, ti=ti, tn=tn, bankv=bankv,
                           b=b, half=half: e.transpose(
                            out=bankv[0:tn, ((dco - b * half) * 2 + ti) * 128:
                                      ((dco - b * half) * 2 + ti) * 128 + 128],
                            in_=eo_bf8[:, dco * 256 + ti * 128:dco * 256 + ti * 128 + tn],
                            identity=identb[:]),
                            waits=w0,
                            inc="pe" if i_tr == n_tr - 1 else None)
                        i_tr += 1
                vpet = cnt["pe"]
                rm = min(rn, 128)
                bankv_cur = bankv
                m_stgev[T] = op("vector", lambda e, b=b, half=half, ntc=ntc,
                                rm=rm, bankv=bankv_cur: e.tensor_copy(
                    out=eo_stage[0:rm, 0:ntc * D].rearrange(
                        "p (ti dco j) -> p ti dco j", dco=DC, j=128)[
                        :, :, b * half:(b + 1) * half, :],
                    in_=bankv[0:rm, 0:half * ntc * 128].rearrange(
                        "p (dco ti j) -> p ti dco j", ti=ntc, j=128)),
                    waits=[("pe", vpet), ("gd", m_store.get(T - 1, 0))],
                    inc="dv")
                if b == 0:
                    m_stgevA[T] = m_stgev[T]
            m_eotrans[T] = cnt["pe"]
            v = None
            for ti, (t0, tn) in enumerate(tcs):
                v = op("gpsimd", lambda e, base=base, t0=t0, tn=tn, ti=ti:
                       e.dma_start(
                    out=eout_d[base + t0:base + t0 + tn, :],
                    in_=eo_stage[0:tn, ti * D:(ti + 1) * D]),
                    waits=[("dv", m_stgev[T])] if ti == 0 else [],
                    inc="gd", inc_k=16)
            m_store[T] = v

        emit_disp_load(0)
        emit_disp_trans(0)
        if NTL > 1:
            emit_disp_load(1)
        for T in range(NTL):
            emit_h(T)
            if T + 1 < NTL:
                emit_disp_trans(T + 1)
            if T + 2 < NTL:
                emit_disp_load(T + 2)
            if T - 1 >= 0:
                emit_eout_trans(T - 1)
            emit_eout(T)
        emit_eout_trans(NTL - 1)

    cb_eo = (eo_g, thi,
             bigC[:, 4096:4096 + D], bigC[:, 4096 + D:4096 + 2 * D],
             zz, bigC[:, 2 * D:3 * D],
             bigC[:, 0:D], bigC[:, D:2 * D])
    cb_tw = (tt_g, wtmpb,
             bigC[:, 4096 + 2 * D:4096 + 3 * D],
             bigC[:, 4096 + 3 * D:4096 + 4 * D],
             big2[:, 0:D], big2[:, D:2 * D],
             big2[:, 2 * D:3 * D], big2[:, 3 * D:4 * D])
    cb_x = (xt, xt2,
            big1[:, 26624:26624 + 2048].bitcast(F32),
            big1[:, 26624 + 2048:26624 + 4096].bitcast(F32),
            big2[:, 24576:24576 + 2048].bitcast(F32),
            big2[:, 24576 + 2048:24576 + 4096].bitcast(F32),
            big1[:, 16384:16384 + 2048].bitcast(F32),
            big1[:, 16384 + 2048:16384 + 4096].bitcast(F32))
    cb_o = (tf, wtmp,
            eo_bf8[:, 0:2048].bitcast(F32),
            eo_stage[:, 0:2048].bitcast(F32),
            big2[:, 24576 + 4096:24576 + 6144].bitcast(F32),
            big2[:, 24576 + 6144:24576 + 8192].bitcast(F32),
            big1[:, 20480:20480 + 2048].bitcast(F32),
            big1[:, 20480 + 2048:20480 + 4096].bitcast(F32))

    def combine_tiles(xsrc, dst, tiles, cm, v_eo, cycle=(0, 1, 2, 3),
                      v_eo_gd=0):
        for k in tiles:
            p = cycle[cm.get("_i", 0) % len(cycle)]
            cm["_i"] = cm.get("_i", 0) + 1
            pw = cm.get(("p", p))
            vg = op("gpsimd", lambda e, k=k, p=p: e.indirect_dma_start(
                out=cb_eo[p][:, 0:D], out_offset=None,
                in_=eout_d,
                in_offset=IndirectOffsetOnAxis(flatIi[:, k:k + 1], 0)),
                waits=[("ld", v_eo), ("gd", v_eo_gd),
                       ("dv", pw[0] if pw else 0)],
                inc="gd", inc_k=16)
            vx = op("sync", lambda e, k=k, p=p: e.dma_start(
                out=cb_x[p][:], in_=xsrc[k * 128:(k + 1) * 128, :]),
                waits=([("dv", pw[1])] if pw
                       else [("ld", v_eo), ("gd", v_eo_gd)]),
                inc="ld", inc_k=16)
            vtw = op("gpsimd", lambda e, k=k, p=p: e.tensor_scalar(
                out=cb_tw[p][:], in0=cb_x[p][:], scalar1=rstdS[:, k:k + 1],
                scalar2=nmr[:, k:k + 1], op0=ALU.mult, op1=ALU.add),
                waits=[("ld", vx)] + ([("dv", pw[1])] if pw else []),
                inc="gd", inc_k=1)
            op("vector", lambda e, p=p: e.tensor_tensor(
                out=cb_o[p][:], in0=cb_eo[p][:, 0:D], in1=cb_tw[p][:],
                op=ALU.subtract),
                waits=[("gd", vg), ("gd", vtw), ("gd", pw[2] if pw else 0)])
            op("vector", lambda e, k=k, p=p: e.scalar_tensor_tensor(
                out=cb_o[p][:], in0=cb_o[p][:], scalar=keptS[:, k:k + 1],
                in1=cb_tw[p][:], op0=ALU.mult, op1=ALU.add))
            v6 = op("vector", lambda e, k=k, p=p: e.scalar_tensor_tensor(
                out=cb_o[p][:], in0=cb_o[p][:], scalar=sfacS[:, k:k + 1],
                in1=cb_x[p][:], op0=ALU.mult, op1=ALU.add), inc="dv")
            vs = op("gpsimd", lambda e, k=k, p=p, dst=dst: e.dma_start(
                out=dst[k * 128:(k + 1) * 128, :], in_=cb_o[p][:]),
                waits=[("dv", v6)], inc="gd", inc_k=16)
            cm[k] = cm[("p", p)] = (v6, v6, vs)

    def stats_tile(s, k, st, cm):
        srow = s * SEQ
        amark = st.setdefault(("am", s), {})
        x_t = xt if k % 2 == 0 else xt2
        w0 = [("gd", cm[s * SEQT + k][2])]
        w0 += ([("dv", st["v8"])] if k < 2 else
               [("ac", amark[k - 2][0]), ("dv", amark[k - 2][1])])
        v = op("sync", lambda e, k=k, srow=srow, x_t=x_t: e.dma_start(
            out=x_t[:], in_=xcur[srow + k * 128:srow + (k + 1) * 128, :]),
            waits=w0, inc="ld", inc_k=16)
        va = op("scalar", lambda e, k=k, x_t=x_t: e.activation(
            out=sq_scr[:], in_=x_t[:], func=AF.Square,
            accum_out=sscol[:, k:k + 1]),
            waits=[("ld", v)] + ([("dv", st["v7"])] if k == 0 else []),
            inc="ac")
        vd = op("vector", lambda e, k=k, x_t=x_t: e.reduce_sum(
            out=smcol[:, k:k + 1], in_=x_t[:], axis=AX.X),
            waits=[("ld", v)], inc="dv")
        st["v_sm"] = 0
        amark[k] = (va, vd)

    def attention_seq(s, st, cm):
        frs = chunks_of(SEQ, 512)
        srow = s * SEQ
        if True:
            v_ss = cnt["ac"]
            op("vector", lambda e: e.tensor_scalar(
                out=mucol[:, 0:SEQT], in0=smcol[:, 0:SEQT], scalar1=1.0 / D,
                scalar2=None, op0=ALU.mult),
                waits=[("ac", v_ss), ("gd", st["v_sm"])])
            op("vector", lambda e: e.tensor_scalar(
                out=scal[:, 0:SEQT], in0=sscol[:, 0:SEQT], scalar1=1.0 / D,
                scalar2=None, op0=ALU.mult))
            op("vector", lambda e: e.tensor_tensor(
                out=smcol[:, 0:SEQT], in0=mucol[:, 0:SEQT],
                in1=mucol[:, 0:SEQT], op=ALU.mult))
            v1a = op("vector", lambda e: e.tensor_tensor(
                out=scal[:, 0:SEQT], in0=scal[:, 0:SEQT], in1=smcol[:, 0:SEQT],
                op=ALU.subtract), inc="dv")
            v1b = op("vector", lambda e: e.tensor_scalar(
                out=smcol[:, 0:SEQT], in0=scal[:, 0:SEQT], scalar1=1e-5,
                scalar2=None, op0=ALU.add), inc="dv")
            va = op("scalar", lambda e: e.activation(
                out=smcol[:, 0:SEQT], in_=smcol[:, 0:SEQT], func=AF.Sqrt),
                waits=[("dv", v1b)], inc="ac")
            op("vector", lambda e: e.reciprocal(
                out=smcol[:, 0:SEQT], in_=smcol[:, 0:SEQT]), waits=[("ac", va)])
            op("vector", lambda e: e.tensor_tensor(
                out=scal[:, 0:SEQT], in0=scal[:, 0:SEQT], in1=smcol[:, 0:SEQT],
                op=ALU.mult))
            op("vector", lambda e: e.tensor_tensor(
                out=scal[:, 0:SEQT], in0=scal[:, 0:SEQT], in1=smcol[:, 0:SEQT],
                op=ALU.mult))
            v1c = op("vector", lambda e: e.tensor_scalar(
                out=scal[:, 0:SEQT], in0=scal[:, 0:SEQT], scalar1=float(D),
                scalar2=None, op0=ALU.mult), inc="dv")
            va = op("scalar", lambda e: e.activation(
                out=scal[:, 0:SEQT], in_=scal[:, 0:SEQT], func=AF.Sqrt),
                waits=[("dv", v1c)], inc="ac")
            op("vector", lambda e: e.tensor_scalar(
                out=scal[:, 0:SEQT], in0=scal[:, 0:SEQT], scalar1=1e-12,
                scalar2=None, op0=ALU.max), waits=[("ac", va)])
            op("vector", lambda e: e.reciprocal(
                out=scal[:, 0:SEQT], in_=scal[:, 0:SEQT]))
            op("vector", lambda e: e.tensor_scalar(
                out=scal[:, 0:SEQT], in0=scal[:, 0:SEQT],
                scalar1=float(math.sqrt(D)), scalar2=None, op0=ALU.mult))
            v7 = op("vector", lambda e: e.tensor_tensor(
                out=scal[:, 0:SEQT], in0=scal[:, 0:SEQT], in1=smcol[:, 0:SEQT],
                op=ALU.mult), inc="dv")
            st["v7"] = v7
            xn_ev = [st["stg"]]
            xmark2 = {}
            for k in range(SEQT):
                x_t = xt if k % 2 == 0 else xt2
                w0 = ([("dv", v7)] if k == 0 else
                      ([("dv", xmark2[k - 2])] if k - 2 in xmark2 else []))
                v = op("sync", lambda e, k=k, srow=srow, x_t=x_t: e.dma_start(
                    out=x_t[:], in_=xcur[srow + k * 128:srow + (k + 1) * 128, :]),
                    waits=w0, inc="ld", inc_k=16)
                op("vector", lambda e, k=k, x_t=x_t: e.tensor_scalar(
                    out=tf[:], in0=x_t[:], scalar1=mucol[:, k:k + 1],
                    scalar2=scal[:, k:k + 1], op0=ALU.subtract, op1=ALU.mult),
                    waits=[("ld", v), ("dv", v7)])
                op("vector", lambda e: e.tensor_tensor(
                    out=tf[:], in0=tf[:], in1=gam[:], op=ALU.mult))
                v8 = op("vector", lambda e: e.tensor_copy(out=thi[:], in_=tf[:]),
                        inc="dv")
                xmark2[k] = v8
                for dc in range(DC):
                    op("tensor", lambda e, dc=dc: e.transpose(
                        out=p_smb[:, dc * 128:(dc + 1) * 128],
                        in_=thi[:, dc * 128:(dc + 1) * 128], identity=identb[:]),
                        waits=[("dv", v8), xn_ev[0]] if dc == 0 else [],
                        inc="pe" if dc == DC - 1 else None)
                vpet = cnt["pe"]
                xn_ev[0] = ("dv", op("vector", lambda e, k=k: e.tensor_copy(
                    out=xn_fm[:, 0:DC * SEQ].rearrange(
                        "p (dc s) -> p dc s", s=SEQ)[:, :, k * 128:(k + 1) * 128],
                    in_=p_smb[:, 0:DC * 128].rearrange(
                        "p (dc f) -> p dc f", f=128)),
                    waits=[("pe", vpet)], inc="dv"))
            st["v8"] = xmark2[SEQT - 1]
            st["xnpe"] = cnt["pe"]
            v_xn = xn_ev[0]
            for g in range(C.GC):
                def qkva(oc, dest, first_w):
                    evs = []
                    for fi, (f0, fn_) in enumerate(frs):
                        tick = st["tick"]
                        st["tick"] = tick + 1
                        pb = p_q0 if tick % 2 == 0 else p_q1
                        w0 = [st["bank"][tick - 2]] if tick - 2 in st["bank"] \
                            else []
                        if fi == 0 and g == 0:
                            w0 = w0 + [v_xn]
                        for dc in range(DC):
                            op("tensor", lambda e, oc=oc, dc=dc, f0=f0, fn_=fn_,
                               pb=pb: e.matmul(
                                out=pb[:, 0:fn_],
                                lhsT=wT[:, (oc * DC + dc) * 128:(oc * DC + dc + 1) * 128],
                                rhs=xn_fm[:, dc * SEQ + f0:dc * SEQ + f0 + fn_],
                                start=(dc == 0), stop=(dc == DC - 1)),
                                waits=w0 if dc == 0 else [],
                                inc="pe" if dc == DC - 1 else None)
                        v_mm = cnt["pe"]
                        ev = dest(fi, f0, fn_, pb, v_mm,
                                  first_w if fi == 0 else [])
                        st["bank"][tick] = ev
                        evs.append(ev)
                    return evs

                def kv_ev(fi, f0, fn_, pb, v_mm, fw):
                    return ("ac", op("scalar", lambda e, f0=f0, fn_=fn_, pb=pb:
                                     e.activation(
                        out=kvf[:, f0:f0 + fn_], in_=pb[:, 0:fn_],
                        func=AF.Copy), waits=[("pe", v_mm)] + fw, inc="ac"))

                def a_ev(fi, f0, fn_, pb, v_mm, fw):
                    return ("ac", op("scalar", lambda e, f0=f0, fn_=fn_, pb=pb:
                                     e.activation(
                        out=af[:, f0:f0 + fn_], in_=pb[:, 0:fn_],
                        func=AF.Sigmoid), waits=[("pe", v_mm)] + fw, inc="ac"))

                def q_ev(fi, f0, fn_, pb, v_mm, fw):
                    return ("ac", op("scalar", lambda e, f0=f0, fn_=fn_, pb=pb:
                                     e.activation(
                        out=qb[:, f0:f0 + fn_], in_=pb[:, 0:fn_],
                        func=AF.Copy), waits=[("pe", v_mm)] + fw, inc="ac"))

                qkva(C.GC + g, kv_ev, [("dv", st["v9g"])])
                qkva(2 * C.GC + g, a_ev, [])
                v_a = cnt["ac"]
                op("vector", lambda e: e.tensor_tensor_scan(
                    out=yf[:, 0:SEQ], data0=af[:, 0:SEQ], data1=kvf[:, 0:SEQ],
                    initial=0.0, op0=ALU.mult, op1=ALU.add),
                    waits=[("ac", v_a)])
                qkva(g, q_ev, [])
                v_q = cnt["ac"]
                v9 = op("vector", lambda e: e.tensor_tensor(
                    out=of[:, 0:SEQ], in0=qb[:, 0:SEQ], in1=yf[:, 0:SEQ],
                    op=ALU.mult),
                    waits=[("ac", v_q), ("pe", st["oevg"])], inc="dv")
                st["v9g"] = v9
                o_ev = []
                for k in range(SEQT):
                    bk = p_sm[:, 0:128] if k % 2 == 0 else p_lgt[:, 0:128]
                    w0 = []
                    if k == 0:
                        w0 = [("dv", v9)]
                    elif k >= 2:
                        w0 = [("dv", o_ev[k - 2])]
                    vpe = op("tensor", lambda e, k=k, bk=bk: e.transpose(
                        out=bk, in_=of[:, k * 128:(k + 1) * 128],
                        identity=identf[:]), waits=w0, inc="pe")
                    o_ev.append(op("vector", lambda e, k=k, bk=bk: e.tensor_copy(
                        out=stg[:, k * 128:(k + 1) * 128], in_=bk),
                        waits=[("pe", vpe)] +
                              ([("gd", st["accum"])] if k == 0 else []),
                        inc="dv"))
                st["oevg"] = cnt["pe"]
                st["stg"] = ("dv", o_ev[-1])
                v10 = o_ev[-1]
                st["accum"] = op("gpsimd", lambda e, g=g, srow=srow: e.dma_start(
                    out=xcur[srow:srow + SEQ, g * 128:(g + 1) * 128].rearrange(
                        "(c p) f -> p c f", p=128),
                    in_=stg[:, 0:SEQ].rearrange("p (c f) -> p c f", f=128),
                    accum_op=ALU.add),
                    waits=[("dv", v10)], inc="gd", inc_k=16)
                st["inter"](s, g)

    def middle_phase(combine_only=False, no_inter=False, no_inter_stats=False):
        v_eo = cnt["ld"]
        v_eo_gd = cnt["gd"]
        wait("sync", "gd", cnt["gd"])
        v_wt = op("sync", lambda e: e.dma_start(out=wT, in_=wT_d),
                  waits=[("pe", cnt["pe"]), ("dv", cnt["dv"]),
                         ("gd", cnt["gd"])], inc="ld", inc_k=16)
        wait("tensor", "ld", v_wt)
        cm = {}
        st = {"v9g": 0, "oevg": 0, "accum": 0, "stg": ("dv", 0),
              "v8": 0, "v7": 0, "sc9": 0, "v_sm": 0, "bank": {}, "tick": 0}

        def inter(s, g):
            if no_inter:
                return
            s1 = s + 1
            if s1 >= C.NSEQ:
                return
            combine_tiles(xin, xcur,
                          range(s1 * SEQT + 2 * g, s1 * SEQT + 2 * g + 2),
                          cm, v_eo, cycle=(2, 3), v_eo_gd=v_eo_gd)
            if g >= 1 and not no_inter_stats:
                stats_tile(s1, 2 * g - 2, st, cm)
                stats_tile(s1, 2 * g - 1, st, cm)

        st["inter"] = inter
        if combine_only:
            combine_tiles(xin, xcur, range(RT), cm, v_eo,
                          cycle=(2, 3), v_eo_gd=v_eo_gd)
            return
        combine_tiles(xin, xcur, range(SEQT), cm, v_eo, cycle=(2, 3),
                      v_eo_gd=v_eo_gd)
        for k in range(SEQT):
            stats_tile(0, k, st, cm)
        for s in range(C.NSEQ):
            attention_seq(s, st, cm)
            if s + 1 < C.NSEQ:
                if no_inter:
                    combine_tiles(xin, xcur,
                                  range((s + 1) * SEQT, (s + 2) * SEQT),
                                  cm, v_eo, cycle=(2, 3), v_eo_gd=v_eo_gd)
                if no_inter or no_inter_stats:
                    for k in range(SEQT - 2):
                        stats_tile(s + 1, k, st, cm)
                stats_tile(s + 1, SEQT - 2, st, cm)
                stats_tile(s + 1, SEQT - 1, st, cm)

    pre_box = {}
    if "r1" in phases:
        routing_phase(xin, prefetch=lambda vg2: pre_box.update(
            m=emit_w0_prefetch([("gd", vg2), ("ld", v_wT_st)], [("gd", vg2)])))
    if "e1" in phases:
        expert_phase(pre=pre_box.pop("m", None))
    if "m" in phases:
        middle_phase()
    if "mq" in phases:
        middle_phase(combine_only=True)
    if "mn" in phases:
        middle_phase(no_inter=True)
    if "ms" in phases:
        middle_phase(no_inter_stats=True)
    if "r2" in phases:
        wait("sync", "gd", cnt["gd"])
        wait("sync", "ld", cnt["ld"])
        wait("gpsimd", "gd", cnt["gd"])
        wait("gpsimd", "ld", cnt["ld"])
        pe_r2 = cnt["pe"]
        routing_phase(xcur, prefetch=lambda vg2: pre_box.update(
            m=emit_w0_prefetch([("gd", vg2)], [("gd", vg2), ("pe", pe_r2)])))
    if "e2" in phases:
        expert_phase(pre=pre_box.pop("m", None))
    if "c2" in phases:
        wait("sync", "gd", cnt["gd"])
        combine_tiles(xcur, yout, range(RT), {}, cnt["ld"],
                      cycle=tuple(range(8)), v_eo_gd=cnt["gd"])
    if "dx" in phases:
        wait("gpsimd", "ld", cnt["ld"])
        wait("gpsimd", "gd", cnt["gd"])
        op("gpsimd", lambda e: e.dma_start(out=yout, in_=xcur),
           inc="gd", inc_k=16)
    if "dt" in phases:
        wait("gpsimd", "ld", cnt["ld"])
        wait("gpsimd", "gd", cnt["gd"])
        op("gpsimd", lambda e: e.dma_start(out=yout, in_=t_d),
           inc="gd", inc_k=16)
    if "de" in phases:
        wait("gpsimd", "ld", cnt["ld"])
        wait("gpsimd", "gd", cnt["gd"])
        op("gpsimd", lambda e: e.dma_start(
            out=yout[0:min(NT, E * CAP), :], in_=eout_d[0:min(NT, E * CAP), :]),
           inc="gd", inc_k=16)

    wait("sync", "dv", cnt["dv"])
    wait("gpsimd", "gd", cnt["gd"])
    wait("gpsimd", "ld", cnt["ld"])
    wait("sync", "ld", cnt["ld"])
    wait("sync", "gd", cnt["gd"])

    ctx.run()
    es.close()
    return nc


def _layout_inputs(C, x, wq, gamma, wr, w1, w2):
    gam_b = np.ascontiguousarray(
        np.tile(np.asarray(gamma, np.float32).reshape(1, C.D), (128, 1)))
    return dict(
        xin=np.ascontiguousarray(np.asarray(x, np.float32).reshape(C.NT, C.D)),
        wqkva=np.ascontiguousarray(np.asarray(wq, np.float32)),
        gammab=gam_b,
        wr_in=np.ascontiguousarray(np.asarray(wr, np.float32)),
        w1_in=np.ascontiguousarray(np.asarray(w1, np.float32)),
        w2_in=np.ascontiguousarray(np.asarray(w2, np.float32)),
    )


def kernel(L0_attn_x, L0_attn_W_qkva, L0_attn_gamma, L0_ffn_Wr, L0_ffn_W1,
           L0_ffn_W2):
    C = REAL
    nc = bass.Bass("TRN2", target_bir_lowering=False)
    build_program(nc, C)
    in_map = _layout_inputs(C, L0_attn_x, L0_attn_W_qkva, L0_attn_gamma,
                            L0_ffn_Wr, L0_ffn_W1, L0_ffn_W2)
    res = run_bass_kernel_spmd(nc, [in_map], core_ids=[0])
    return res.results[0]["yout"].reshape(16, 2048, 1024)
